# version 01
# baseline (speedup 1.0000x reference)
import numpy as np
from contextlib import ExitStack
import concourse.bass as bass
import concourse.mybir as mybir
from concourse.bass_utils import run_bass_kernel_spmd

F32 = mybir.dt.float32
BF16 = mybir.dt.bfloat16
I32 = mybir.dt.int32
AF = mybir.ActivationFunctionType
ALU = mybir.AluOpType
AX = mybir.AxisListType

NC = 8
D = 2048
TPC = 2048
NTOK = 16384
NH = 16
TB = 8192
QS = 0.08838834764831845

ENGS = ("tensor", "vector", "scalar", "sync", "gpsimd")


class PB:

    def __init__(self):
        self.items = {e: [] for e in ENGS}
        self.total = 0
        self.waited = {e: 0 for e in ENGS}
        self.cur = {e: None for e in ENGS}
        self.dmas = {e: 0 for e in ENGS}
        self.pdma = {e: False for e in ENGS}

    def op(self, eng, fn, dma=False):
        it = {"fn": fn, "waits": [], "incs": []}
        if self.waited[eng] < self.total:
            it["waits"].append(("P", self.total))
            self.waited[eng] = self.total
        self.items[eng].append(it)
        if dma:
            self.dmas[eng] += 1
            it["incs"].append(("D" + eng, 16))
            self.pdma[eng] = True
        else:
            self.cur[eng] = it
        return it

    def t(self, fn): self.op("tensor", fn)
    def v(self, fn): self.op("vector", fn)
    def s(self, fn): self.op("scalar", fn)
    def g(self, fn): self.op("gpsimd", fn)
    def dl(self, fn): self.op("sync", fn, dma=True)
    def dg(self, fn): self.op("gpsimd", fn, dma=True)

    def bar(self):
        n = 0
        for e in ENGS:
            if self.cur[e] is not None:
                self.cur[e]["incs"].append(("P", 1))
                n += 1
                self.cur[e] = None
            if self.pdma[e]:
                it = {"fn": None, "waits": [("D" + e, 16 * self.dmas[e])],
                      "incs": [("P", 1)]}
                self.items[e].append(it)
                n += 1
                self.pdma[e] = False
        self.total += n

    def emit(self, nc):
        self.bar()
        with ExitStack() as st:
            sems = {"P": st.enter_context(nc.semaphore("Pph"))}
            for e in ENGS:
                sems["D" + e] = st.enter_context(nc.semaphore("Dq_" + e))
            block = st.enter_context(nc.Block())
            for e in ENGS:
                items = self.items[e]
                if not items:
                    continue

                def body(eng, items=items):
                    for it in items:
                        for (s_, v_) in it["waits"]:
                            eng.wait_ge(sems[s_], v_)
                        if it["fn"] is None:
                            for (s_, n_) in it["incs"]:
                                eng.sem_inc(sems[s_], n_)
                        else:
                            inst = it["fn"](eng)
                            for (s_, n_) in it["incs"]:
                                inst = inst.then_inc(sems[s_], n_)
                getattr(block, e)(body)


def _np_masks():
    i = np.arange(128)
    mUi = (i[:, None] <= i[None, :]).astype(np.float32)
    mLs = (i[:, None] > i[None, :]).astype(np.float32)
    mNeg = np.where(i[:, None] > i[None, :], -200.0, 0.0).astype(np.float32)
    return mUi, mLs, mNeg


def emit_ln(pb, xin, groups, psA, psB, pr1, pr2, sq, mrow, rrow, scr, xout,
            ones_col, ones_row, ebias, d_total=2048.0):
    nd = len(xin)
    for (off, w) in groups:
        for r0 in range(0, nd, 4):
            for d in range(r0, min(r0 + 4, nd)):
                pb.s(lambda e, d=d, r0=r0, off=off, w=w: e.activation(
                    out=sq[:, (d - r0) * 512:(d - r0) * 512 + w],
                    in_=xin[d][:, off:off + w], func=AF.Square))
            pb.bar()
            for d in range(r0, min(r0 + 4, nd)):
                pb.t(lambda e, d=d, off=off, w=w: e.matmul(
                    pr1[:1, :w], ones_col[:, 0:1], xin[d][:, off:off + w],
                    start=(d == 0), stop=(d == nd - 1)))
            for d in range(r0, min(r0 + 4, nd)):
                pb.t(lambda e, d=d, r0=r0, w=w: e.matmul(
                    pr2[:1, :w], ones_col[:, 0:1], sq[:, (d - r0) * 512:(d - r0) * 512 + w],
                    start=(d == 0), stop=(d == nd - 1)))
            pb.bar()
        pb.v(lambda e, w=w: e.tensor_scalar(
            out=mrow[:1, :w], in0=pr1[:1, :w], scalar1=1.0 / d_total,
            scalar2=None, op0=ALU.mult))
        pb.bar()
        pb.v(lambda e, w=w: e.tensor_mul(scr[:1, :w], mrow[:1, :w], mrow[:1, :w]))
        pb.bar()
        pb.v(lambda e, w=w: e.scalar_tensor_tensor(
            out=rrow[:1, :w], in0=pr2[:1, :w], scalar=1.0 / d_total,
            in1=scr[:1, :w], op0=ALU.mult, op1=ALU.subtract))
        pb.bar()
        pb.s(lambda e, w=w: e.activation(out=rrow[:1, :w], in_=rrow[:1, :w],
                                         func=AF.Ln, bias=ebias))
        pb.bar()
        pb.s(lambda e, w=w: e.activation(out=rrow[:1, :w], in_=rrow[:1, :w],
                                         func=AF.Exp, scale=-0.5))
        pb.bar()
        pb.t(lambda e, w=w: e.matmul(psA[:, :w], ones_row[:1, :], mrow[:1, :w],
                                     start=True, stop=True))
        pb.t(lambda e, w=w: e.matmul(psB[:, :w], ones_row[:1, :], rrow[:1, :w],
                                     start=True, stop=True))
        pb.bar()
        for d in range(nd):
            pb.v(lambda e, d=d, off=off, w=w: e.tensor_sub(
                sq[:, 0:w], xin[d][:, off:off + w], psA[:, :w]))
            pb.bar()
            pb.v(lambda e, d=d, off=off, w=w: e.tensor_mul(
                xout[d][:, off:off + w], sq[:, 0:w], psB[:, :w]))
            pb.bar()


def emit_silu(pb, out_ap, in_ap, scratch, w):
    pb.s(lambda e: e.activation(out=scratch[:, :w], in_=in_ap, func=AF.Exp, scale=-1.0))
    pb.bar()
    pb.v(lambda e: e.tensor_scalar(out=scratch[:, :w], in0=scratch[:, :w],
                                   scalar1=1.0, scalar2=None, op0=ALU.add))
    pb.bar()
    pb.v(lambda e: e.reciprocal(out=scratch[:, :w], in_=scratch[:, :w]))
    pb.bar()
    pb.v(lambda e: e.tensor_mul(out_ap, in_ap, scratch[:, :w]))
    pb.bar()


def build_A():
    nc = bass.Bass("TRN2", target_bir_lowering=False)
    xh = nc.dram_tensor("xh", [D, TPC + 3], F32, kind="ExternalInput").ap()
    wqkv = nc.dram_tensor("wqkv", [D, 6144], F32, kind="ExternalInput").ap()
    waa = nc.dram_tensor("waa", [D, 16], F32, kind="ExternalInput").ap()
    wbb = nc.dram_tensor("wbb", [D, 16], F32, kind="ExternalInput").ap()
    wgate = nc.dram_tensor("wgate", [D, 2048], F32, kind="ExternalInput").ap()
    convw = nc.dram_tensor("convw", [6144, 4], F32, kind="ExternalInput").ap()
    alog = nc.dram_tensor("alog", [16], F32, kind="ExternalInput").ap()
    dtb = nc.dram_tensor("dtb", [16], F32, kind="ExternalInput").ap()
    qf = nc.dram_tensor("qf", [NH, 128, TPC], F32, kind="ExternalOutput").ap()
    kf = nc.dram_tensor("kf", [NH, 128, TPC], F32, kind="ExternalOutput").ap()
    vf = nc.dram_tensor("vf", [NH, 128, TPC], F32, kind="ExternalOutput").ap()
    gg = nc.dram_tensor("gg", [16, TPC], F32, kind="ExternalOutput").ap()
    bb = nc.dram_tensor("bb", [16, TPC], F32, kind="ExternalOutput").ap()
    gatef = nc.dram_tensor("gatef", [D, TPC], F32, kind="ExternalOutput").ap()

    W = TPC + 3
    groups = [(0, 512), (512, 512), (1024, 512), (1536, 512), (2048, 3)]
    pb = PB()
    with ExitStack() as st:
        e_ = st.enter_context
        xt = [e_(nc.sbuf_tensor("xsb%d" % d, [128, W], F32)) for d in range(16)]
        sq = e_(nc.sbuf_tensor("sq", [128, 4 * 512], F32))
        scr = e_(nc.sbuf_tensor("scr", [1, 512], F32))
        mrow = e_(nc.sbuf_tensor("mrow", [1, 512], F32))
        rrow = e_(nc.sbuf_tensor("rrow", [1, 512], F32))
        wsb = e_(nc.sbuf_tensor("wsb", [128, 2048], F32))
        qkvg = [e_(nc.sbuf_tensor("qkvg%d" % i, [128, 512], F32)) for i in range(2)]
        conv = e_(nc.sbuf_tensor("conv", [128, 512], F32))
        esb = e_(nc.sbuf_tensor("esb", [128, 512], F32))
        sln = e_(nc.sbuf_tensor("sln", [128, 512], F32))
        outt = e_(nc.sbuf_tensor("outt", [128, 512], F32))
        cw = e_(nc.sbuf_tensor("cw", [128, 48 * 4], F32))
        av = e_(nc.sbuf_tensor("av", [16, 4], F32))
        ones_col = e_(nc.sbuf_tensor("onesc", [128, 1], F32))
        ones_row = e_(nc.sbuf_tensor("onesr", [1, 128], F32))
        cb = e_(nc.sbuf_tensor("cb", [128, 4], F32))
        abg = e_(nc.sbuf_tensor("abg", [16, 512], F32))
        ab2 = e_(nc.sbuf_tensor("ab2", [16, 512], F32))
        psA = e_(nc.psum_tensor("psA", [128, 512], F32))
        psB = e_(nc.psum_tensor("psB", [128, 512], F32))
        pr1 = e_(nc.psum_tensor("pr1", [1, 512], F32))
        pr2 = e_(nc.psum_tensor("pr2", [1, 512], F32))

        for d in range(16):
            pb.dl(lambda e, d=d: e.dma_start(out=xt[d][:, :], in_=xh[d * 128:(d + 1) * 128, :]))
        for o in range(48):
            pb.dl(lambda e, o=o: e.dma_start(out=cw[:, o * 4:o * 4 + 4],
                                             in_=convw[o * 128:(o + 1) * 128, :]))
        pb.dl(lambda e: e.dma_start(out=av[:, 0], in_=alog[:]))
        pb.dl(lambda e: e.dma_start(out=av[:, 1], in_=dtb[:]))
        pb.v(lambda e: e.memset(ones_col[:, :], 1.0))
        pb.v(lambda e: e.memset(ones_row[:1, :], 1.0))
        pb.v(lambda e: e.memset(cb[:, 0:1], 1e-5))
        pb.v(lambda e: e.memset(cb[:, 1:2], 1e-6))
        pb.v(lambda e: e.memset(cb[:, 2:3], 1.0))
        pb.v(lambda e: e.memset(cb[:, 3:4], -16.0))
        pb.bar()
        pb.s(lambda e: e.activation(out=av[:16, 2:3], in_=av[:16, 0:1], func=AF.Exp))
        pb.bar()
        pb.v(lambda e: e.tensor_scalar(out=av[:16, 2:3], in0=av[:16, 2:3],
                                       scalar1=-1.0, scalar2=None, op0=ALU.mult))
        pb.bar()
        emit_ln(pb, xt, groups, psA, psB, pr1, pr2, sq, mrow, rrow, scr, xt,
                ones_col, ones_row, cb[:1, 0:1])

        for o in range(48):
            for dt in range(16):
                pb.dl(lambda e, dt=dt, o=o: e.dma_start(
                    out=wsb[:, dt * 128:(dt + 1) * 128],
                    in_=wqkv[dt * 128:(dt + 1) * 128, o * 128:(o + 1) * 128]))
            pb.bar()
            for gi, (off, w) in enumerate(groups):
                cur = qkvg[gi % 2]
                prev = qkvg[(gi + 1) % 2]
                for dt in range(16):
                    pb.t(lambda e, dt=dt, off=off, w=w: e.matmul(
                        psA[:, :w], wsb[:, dt * 128:(dt + 1) * 128],
                        xt[dt][:, off:off + w], start=(dt == 0), stop=(dt == 15)))
                pb.bar()
                pb.v(lambda e, cur=cur, w=w: e.tensor_copy(out=cur[:, :w], in_=psA[:, :w]))
                pb.bar()
                pb.v(lambda e, cur=cur, w=w, o=o: e.tensor_scalar(
                    out=conv[:, :w], in0=cur[:, :w], scalar1=cw[:, o * 4 + 3:o * 4 + 4],
                    scalar2=None, op0=ALU.mult))
                pb.bar()
                for j in range(3):
                    if w > 3 - j:
                        pb.v(lambda e, j=j, cur=cur, w=w, o=o: e.scalar_tensor_tensor(
                            out=conv[:, 3 - j:w], in0=cur[:, 0:w - (3 - j)],
                            scalar=cw[:, o * 4 + j:o * 4 + j + 1],
                            in1=conv[:, 3 - j:w], op0=ALU.mult, op1=ALU.add))
                        pb.bar()
                    if gi > 0:
                        pw = groups[gi - 1][1]
                        pb.v(lambda e, j=j, prev=prev, o=o, pw=pw: e.scalar_tensor_tensor(
                            out=conv[:, 0:3 - j], in0=prev[:, pw - (3 - j):pw],
                            scalar=cw[:, o * 4 + j:o * 4 + j + 1],
                            in1=conv[:, 0:3 - j], op0=ALU.mult, op1=ALU.add))
                        pb.bar()
                emit_silu(pb, sln[:, :w], conv[:, :w], esb, w)
                lo = 3 if gi == 0 else 0
                hb0 = off + lo - 3
                sw = w - lo
                if sw <= 0:
                    continue
                if o < 16:
                    h, dst, scale = o, qf, QS
                elif o < 32:
                    h, dst, scale = o - 16, kf, 1.0
                else:
                    h, dst, scale = o - 32, vf, None
                if scale is None:
                    pb.dg(lambda e, h=h, hb0=hb0, sw=sw, lo=lo, w=w, dst=dst: e.dma_start(
                        out=dst[h, :, hb0:hb0 + sw], in_=sln[:, lo:w]))
                else:
                    pb.s(lambda e, w=w: e.activation(out=esb[:, :w], in_=sln[:, :w],
                                                     func=AF.Square))
                    pb.bar()
                    pb.t(lambda e, w=w: e.matmul(pr1[:1, :w], ones_col[:, 0:1],
                                                 esb[:, :w], start=True, stop=True))
                    pb.bar()
                    pb.s(lambda e, w=w: e.activation(out=mrow[:1, :w], in_=pr1[:1, :w],
                                                     func=AF.Ln, bias=cb[:1, 1:2]))
                    pb.bar()
                    pb.s(lambda e, w=w: e.activation(out=mrow[:1, :w], in_=mrow[:1, :w],
                                                     func=AF.Exp, scale=-0.5))
                    pb.bar()
                    pb.t(lambda e, w=w: e.matmul(psB[:, :w], ones_row[:1, :],
                                                 mrow[:1, :w], start=True, stop=True))
                    pb.bar()
                    pb.v(lambda e, w=w, scale=scale: e.scalar_tensor_tensor(
                        out=outt[:, :w], in0=sln[:, :w], scalar=scale,
                        in1=psB[:, :w], op0=ALU.mult, op1=ALU.mult))
                    pb.bar()
                    pb.dg(lambda e, h=h, hb0=hb0, sw=sw, lo=lo, w=w, dst=dst: e.dma_start(
                        out=dst[h, :, hb0:hb0 + sw], in_=outt[:, lo:w]))
            pb.bar()

        for o in range(16):
            for dt in range(16):
                pb.dl(lambda e, dt=dt, o=o: e.dma_start(
                    out=wsb[:, dt * 128:(dt + 1) * 128],
                    in_=wgate[dt * 128:(dt + 1) * 128, o * 128:(o + 1) * 128]))
            pb.bar()
            for gi, (off, w) in enumerate(groups):
                lo = 3 if gi == 0 else 0
                sw = w - lo
                for dt in range(16):
                    pb.t(lambda e, dt=dt, off=off, w=w: e.matmul(
                        psA[:, :w], wsb[:, dt * 128:(dt + 1) * 128],
                        xt[dt][:, off:off + w], start=(dt == 0), stop=(dt == 15)))
                pb.bar()
                pb.v(lambda e, w=w: e.tensor_copy(out=outt[:, :w], in_=psA[:, :w]))
                pb.bar()
                if sw > 0:
                    pb.dg(lambda e, o=o, off=off, lo=lo, sw=sw, w=w: e.dma_start(
                        out=gatef[o * 128:(o + 1) * 128, off + lo - 3:off + lo - 3 + sw],
                        in_=outt[:, lo:w]))
            pb.bar()

        for which in range(2):
            wap = waa if which == 0 else wbb
            for dt in range(16):
                pb.dl(lambda e, dt=dt, wap=wap: e.dma_start(
                    out=wsb[:, dt * 16:(dt + 1) * 16],
                    in_=wap[dt * 128:(dt + 1) * 128, :]))
            pb.bar()
            for gi, (off, w) in enumerate(groups):
                lo = 3 if gi == 0 else 0
                sw = w - lo
                for dt in range(16):
                    pb.t(lambda e, dt=dt, off=off, w=w: e.matmul(
                        psA[:16, :w], wsb[:, dt * 16:(dt + 1) * 16],
                        xt[dt][:, off:off + w], start=(dt == 0), stop=(dt == 15)))
                pb.bar()
                if which == 0:
                    pb.s(lambda e, w=w: e.activation(out=abg[:16, :w], in_=psA[:16, :w],
                                                     func=AF.Exp, bias=av[:16, 1:2]))
                    pb.bar()
                    pb.s(lambda e, w=w: e.activation(out=abg[:16, :w], in_=abg[:16, :w],
                                                     func=AF.Ln, bias=cb[:16, 2:3]))
                    pb.bar()
                    pb.v(lambda e, w=w: e.tensor_scalar(
                        out=ab2[:16, :w], in0=abg[:16, :w], scalar1=av[:16, 2:3],
                        scalar2=None, op0=ALU.mult))
                    pb.bar()
                    if sw > 0:
                        pb.dg(lambda e, off=off, lo=lo, sw=sw, w=w: e.dma_start(
                            out=gg[:, off + lo - 3:off + lo - 3 + sw], in_=ab2[:16, lo:w]))
                else:
                    pb.s(lambda e, w=w: e.activation(out=abg[:16, :w], in_=psA[:16, :w],
                                                     func=AF.Exp, scale=-1.0))
                    pb.bar()
                    pb.v(lambda e, w=w: e.tensor_scalar(out=abg[:16, :w], in0=abg[:16, :w],
                                                        scalar1=1.0, scalar2=None, op0=ALU.add))
                    pb.bar()
                    pb.v(lambda e, w=w: e.reciprocal(out=abg[:16, :w], in_=abg[:16, :w]))
                    pb.bar()
                    pb.v(lambda e, w=w: e.tensor_scalar(out=ab2[:16, :w], in0=abg[:16, :w],
                                                        scalar1=2.0, scalar2=None, op0=ALU.mult))
                    pb.bar()
                    if sw > 0:
                        pb.dg(lambda e, off=off, lo=lo, sw=sw, w=w: e.dma_start(
                            out=bb[:, off + lo - 3:off + lo - 3 + sw], in_=ab2[:16, lo:w]))
            pb.bar()
        pb.emit(nc)
    return nc


def build_B():
    nc = bass.Bass("TRN2", target_bir_lowering=False)
    qb = nc.dram_tensor("qb", [4, 128, TB], F32, kind="ExternalInput").ap()
    kb = nc.dram_tensor("kb", [4, 128, TB], F32, kind="ExternalInput").ap()
    vb = nc.dram_tensor("vb", [4, 128, TB], F32, kind="ExternalInput").ap()
    gb = nc.dram_tensor("gb", [4, 128, 64], F32, kind="ExternalInput").ap()
    bbeta = nc.dram_tensor("bbeta", [4, 128, 64], F32, kind="ExternalInput").ap()
    mui = nc.dram_tensor("mui", [128, 128], F32, kind="ExternalInput").ap()
    mls = nc.dram_tensor("mls", [128, 128], F32, kind="ExternalInput").ap()
    idn = nc.dram_tensor("idn", [128, 128], F32, kind="ExternalInput").ap()
    of = nc.dram_tensor("of", [4, 128, TB], F32, kind="ExternalOutput").ap()

    pb = PB()
    with ExitStack() as st:
        e_ = st.enter_context
        sb = lambda n, p, f: e_(nc.sbuf_tensor(n, [p, f], F32))
        cb = sb("cb", 128, 4)
        MUi = sb("MUi", 128, 128)
        MLs = sb("MLs", 128, 128)
        IDN = sb("IDN", 128, 128)
        ones_col = sb("onesc", 128, 1)
        ones_row = sb("onesr", 1, 128)
        ggs = sb("ggs", 128, 64)
        bbs = sb("bbs", 128, 64)
        cum = sb("cum", 128, 64)
        totr = sb("totr", 1, 64)
        tb_ = sb("tb_", 128, 64)
        gtb = sb("gtb", 128, 64)
        ksc = sb("ksc", 128, 64)
        gam = sb("gam", 128, 64)
        bga = sb("bga", 128, 64)
        Ssb = sb("Ssb", 128, 128)
        Qf = sb("Qf", 128, 128)
        Kf = sb("Kf", 128, 128)
        Vfm = sb("Vfm", 128, 128)
        Kt = sb("Kt", 128, 128)
        Vt = sb("Vt", 128, 128)
        crow = sb("crow", 1, 128)
        edarg = sb("edarg", 128, 128)
        Ed = sb("Ed", 128, 128)
        EdT = sb("EdT", 128, 128)
        ATm = sb("ATm", 128, 128)
        Dm = sb("Dm", 128, 128)
        Dt = sb("Dt", 128, 128)
        Pc = sb("Pc", 128, 128)
        Pts = [sb("Pt%d" % j, 128, 128) for j in range(6)]
        usb = sb("usb", 128, 128)
        t1 = sb("t1", 128, 128)
        osb = sb("osb", 128, 128)
        Kh = sb("Kh", 128, 128)
        pA = e_(nc.psum_tensor("pA", [128, 128], F32))
        pB = e_(nc.psum_tensor("pB", [128, 128], F32))
        pC = e_(nc.psum_tensor("pC", [128, 128], F32))
        pR = e_(nc.psum_tensor("pR", [1, 128], F32))

        pb.dl(lambda e: e.dma_start(out=MUi[:, :], in_=mui[:, :]))
        pb.dl(lambda e: e.dma_start(out=MLs[:, :], in_=mls[:, :]))
        pb.dl(lambda e: e.dma_start(out=IDN[:, :], in_=idn[:, :]))
        pb.v(lambda e: e.memset(ones_col[:, :], 1.0))
        pb.v(lambda e: e.memset(ones_row[:1, :], 1.0))
        pb.v(lambda e: e.memset(cb[:, 0:1], 1e-5))
        pb.v(lambda e: e.memset(cb[:, 1:2], 1e-6))
        pb.v(lambda e: e.memset(cb[:, 2:3], 1.0))
        pb.v(lambda e: e.memset(cb[:, 3:4], -16.0))
        pb.bar()

        for inst in range(4):
            pb.dl(lambda e, inst=inst: e.dma_start(out=ggs[:, :], in_=gb[inst, :, :]))
            pb.dl(lambda e, inst=inst: e.dma_start(out=bbs[:, :], in_=bbeta[inst, :, :]))
            pb.v(lambda e: e.memset(Ssb[:, :], 0.0))
            pb.bar()
            pb.t(lambda e: e.matmul(pA[:, :64], MUi[:, :], ggs[:, :], start=True, stop=True))
            pb.t(lambda e: e.matmul(pR[:1, :64], ones_col[:, 0:1], ggs[:, :], start=True, stop=True))
            pb.bar()
            pb.v(lambda e: e.tensor_copy(out=cum[:, :], in_=pA[:, :64]))
            pb.v(lambda e: e.tensor_copy(out=totr[:1, :64], in_=pR[:1, :64]))
            pb.bar()
            pb.t(lambda e: e.matmul(pB[:, :64], ones_row[:1, :], totr[:1, :64], start=True, stop=True))
            pb.bar()
            pb.v(lambda e: e.tensor_copy(out=tb_[:, :], in_=pB[:, :64]))
            pb.bar()
            pb.s(lambda e: e.activation(out=gtb[:, :], in_=tb_[:, :], func=AF.Exp))
            pb.s(lambda e: e.activation(out=gam[:, :], in_=cum[:, :], func=AF.Exp))
            pb.v(lambda e: e.tensor_sub(ksc[:, :], tb_[:, :], cum[:, :]))
            pb.bar()
            pb.s(lambda e: e.activation(out=ksc[:, :], in_=ksc[:, :], func=AF.Exp))
            pb.v(lambda e: e.tensor_mul(bga[:, :], bbs[:, :], gam[:, :]))
            pb.bar()

            for ci in range(64):
                pb.dl(lambda e, inst=inst, ci=ci: e.dma_start(
                    out=Qf[:, :], in_=qb[inst, :, ci * 128:(ci + 1) * 128]))
                pb.dl(lambda e, inst=inst, ci=ci: e.dma_start(
                    out=Kf[:, :], in_=kb[inst, :, ci * 128:(ci + 1) * 128]))
                pb.dl(lambda e, inst=inst, ci=ci: e.dma_start(
                    out=Vfm[:, :], in_=vb[inst, :, ci * 128:(ci + 1) * 128]))
                pb.bar()
                pb.t(lambda e: e.transpose(pA[:, :], Kf[:, :], IDN[:, :]))
                pb.t(lambda e: e.transpose(pB[:, :], Vfm[:, :], IDN[:, :]))
                pb.t(lambda e, ci=ci: e.transpose(pR[:1, :], cum[:, ci:ci + 1], IDN[:, :]))
                pb.bar()
                pb.v(lambda e: e.tensor_copy(out=Kt[:, :], in_=pA[:, :]))
                pb.v(lambda e: e.tensor_copy(out=Vt[:, :], in_=pB[:, :]))
                pb.v(lambda e: e.tensor_copy(out=crow[:1, :], in_=pR[:1, :]))
                pb.bar()
                pb.t(lambda e: e.matmul(pC[:, :], ones_row[:1, :], crow[:1, :], start=True, stop=True))
                pb.bar()
                pb.v(lambda e, ci=ci: e.tensor_scalar(
                    out=edarg[:, :], in0=pC[:, :], scalar1=-1.0,
                    scalar2=cum[:, ci:ci + 1], op0=ALU.mult, op1=ALU.add))
                pb.bar()
                pb.v(lambda e: e.tensor_scalar_min(out=edarg[:, :], in0=edarg[:, :], scalar1=0.0))
                pb.bar()
                pb.s(lambda e: e.activation(out=Ed[:, :], in_=edarg[:, :], func=AF.Exp))
                pb.v(lambda e, ci=ci: e.tensor_scalar(
                    out=EdT[:, :], in0=pC[:, :], scalar1=cum[:, ci:ci + 1],
                    scalar2=None, op0=ALU.subtract))
                pb.bar()
                pb.v(lambda e: e.tensor_scalar_min(out=EdT[:, :], in0=EdT[:, :], scalar1=0.0))
                pb.bar()
                pb.s(lambda e: e.activation(out=EdT[:, :], in_=EdT[:, :], func=AF.Exp))
                pb.v(lambda e: e.tensor_mul(Ed[:, :], Ed[:, :], MLs[:, :]))
                pb.bar()
                pb.v(lambda e: e.tensor_mul(EdT[:, :], EdT[:, :], MUi[:, :]))
                pb.t(lambda e: e.matmul(pA[:, :], Kf[:, :], Qf[:, :], start=True, stop=True))
                pb.t(lambda e: e.matmul(pB[:, :], Kf[:, :], Kf[:, :], start=True, stop=True))
                pb.bar()
                pb.v(lambda e: e.tensor_mul(ATm[:, :], EdT[:, :], pA[:, :]))
                pb.v(lambda e, ci=ci: e.scalar_tensor_tensor(
                    out=Dm[:, :], in0=pB[:, :], scalar=bbs[:, ci:ci + 1],
                    in1=Ed[:, :], op0=ALU.mult, op1=ALU.mult))
                pb.bar()
                pb.t(lambda e: e.transpose(pC[:, :], Dm[:, :], IDN[:, :]))
                pb.bar()
                pb.v(lambda e: e.tensor_copy(out=Dt[:, :], in_=pC[:, :]))
                pb.bar()
                src_t, src = Dt, Dm
                for j in range(6):
                    pb.t(lambda e, st_=src_t, s_=src: e.matmul(pC[:, :], st_[:, :], s_[:, :], start=True, stop=True))
                    pb.bar()
                    pb.v(lambda e: e.tensor_copy(out=Pc[:, :], in_=pC[:, :]))
                    pb.bar()
                    pb.t(lambda e: e.transpose(pC[:, :], Pc[:, :], IDN[:, :]))
                    pb.bar()
                    pb.v(lambda e, j=j: e.tensor_copy(out=Pts[j][:, :], in_=pC[:, :]))
                    pb.bar()
                    src_t, src = Pts[j], Pc
                pb.t(lambda e: e.matmul(pA[:, :], Kf[:, :], Ssb[:, :], start=True, stop=True))
                pb.bar()
                pb.v(lambda e, ci=ci: e.tensor_scalar(
                    out=t1[:, :], in0=pA[:, :], scalar1=bga[:, ci:ci + 1],
                    scalar2=None, op0=ALU.mult))
                pb.bar()
                pb.v(lambda e, ci=ci: e.scalar_tensor_tensor(
                    out=usb[:, :], in0=Vt[:, :], scalar=bbs[:, ci:ci + 1],
                    in1=t1[:, :], op0=ALU.mult, op1=ALU.subtract))
                pb.bar()
                for j in [5, 4, 3, 2, 1, 0]:
                    pb.t(lambda e, j=j: e.matmul(pA[:, :], Pts[j][:, :], usb[:, :], start=True, stop=True))
                    pb.bar()
                    pb.v(lambda e: e.tensor_add(usb[:, :], usb[:, :], pA[:, :]))
                    pb.bar()
                pb.t(lambda e: e.matmul(pA[:, :], Dt[:, :], usb[:, :], start=True, stop=True))
                pb.bar()
                pb.v(lambda e: e.tensor_sub(usb[:, :], usb[:, :], pA[:, :]))
                pb.bar()
                pb.t(lambda e: e.matmul(pA[:, :], ATm[:, :], usb[:, :], start=True, stop=True))
                pb.t(lambda e: e.matmul(pB[:, :], Qf[:, :], Ssb[:, :], start=True, stop=True))
                pb.bar()
                pb.v(lambda e, ci=ci: e.tensor_scalar(
                    out=t1[:, :], in0=pB[:, :], scalar1=gam[:, ci:ci + 1],
                    scalar2=None, op0=ALU.mult))
                pb.bar()
                pb.v(lambda e: e.tensor_add(osb[:, :], t1[:, :], pA[:, :]))
                pb.v(lambda e, ci=ci: e.tensor_scalar(
                    out=Kh[:, :], in0=Kt[:, :], scalar1=ksc[:, ci:ci + 1],
                    scalar2=None, op0=ALU.mult))
                pb.bar()
                pb.t(lambda e: e.matmul(pB[:, :], Kh[:, :], usb[:, :], start=True, stop=True))
                pb.t(lambda e: e.transpose(pC[:, :], osb[:, :], IDN[:, :]))
                pb.bar()
                pb.v(lambda e, ci=ci: e.scalar_tensor_tensor(
                    out=Ssb[:, :], in0=Ssb[:, :], scalar=gtb[:, ci:ci + 1],
                    in1=pB[:, :], op0=ALU.mult, op1=ALU.add))
                pb.v(lambda e: e.tensor_copy(out=t1[:, :], in_=pC[:, :]))
                pb.bar()
                pb.dg(lambda e, inst=inst, ci=ci: e.dma_start(
                    out=of[inst, :, ci * 128:(ci + 1) * 128], in_=t1[:, :]))
            pb.bar()
        pb.emit(nc)
    return nc


def build_C1():
    nc = bass.Bass("TRN2", target_bir_lowering=False)
    xsh = nc.dram_tensor("xsh", [D, TPC], F32, kind="ExternalInput").ap()
    osh = nc.dram_tensor("osh", [D, TPC], F32, kind="ExternalInput").ap()
    gsh = nc.dram_tensor("gsh", [D, TPC], F32, kind="ExternalInput").ap()
    onw = nc.dram_tensor("onw", [128], F32, kind="ExternalInput").ap()
    wo = nc.dram_tensor("wo", [D, D], F32, kind="ExternalInput").ap()
    wra = nc.dram_tensor("wra", [D, 8], F32, kind="ExternalInput").ap()
    bra = nc.dram_tensor("bra", [8], F32, kind="ExternalInput").ap()
    ffa = nc.dram_tensor("ffa", [8, D, 64], F32, kind="ExternalInput").ap()
    ffb = nc.dram_tensor("ffb", [8, 64, D], F32, kind="ExternalInput").ap()
    idn = nc.dram_tensor("idn", [128, 128], F32, kind="ExternalInput").ap()
    x2 = nc.dram_tensor("x2", [D, TPC], F32, kind="ExternalOutput").ap()

    pb = PB()
    with ExitStack() as st:
        e_ = st.enter_context
        xg = [e_(nc.sbuf_tensor("xsb%d" % d, [128, 512], F32)) for d in range(16)]
        on = [e_(nc.sbuf_tensor("on%d" % h, [128, 512], F32)) for h in range(16)]
        xln = [e_(nc.sbuf_tensor("xl%d" % d, [128, 512], F32)) for d in range(16)]
        hprime = [e_(nc.sbuf_tensor("hp%d" % ee, [64, 512], F32)) for ee in range(8)]
        ot = e_(nc.sbuf_tensor("ot", [128, 512], F32))
        gt = e_(nc.sbuf_tensor("gt", [128, 512], F32))
        sq = e_(nc.sbuf_tensor("sq", [128, 4 * 512], F32))
        scr = e_(nc.sbuf_tensor("scr", [1, 512], F32))
        mrow = e_(nc.sbuf_tensor("mrow", [1, 512], F32))
        rrow = e_(nc.sbuf_tensor("rrow", [1, 512], F32))
        wsb = e_(nc.sbuf_tensor("wsb", [128, 2048], F32))
        IDN = e_(nc.sbuf_tensor("IDN", [128, 128], F32))
        onwc = e_(nc.sbuf_tensor("onwc", [128, 1], F32))
        brc = e_(nc.sbuf_tensor("brc", [8, 1], F32))
        wrs = e_(nc.sbuf_tensor("wrs", [128, 16 * 8], F32))
        ffas = e_(nc.sbuf_tensor("ffas", [128, 16 * 64], F32))
        ffbt = e_(nc.sbuf_tensor("ffbt", [64, 8 * 128], F32))
        lg = e_(nc.sbuf_tensor("lg", [8, 512], F32))
        lgt = e_(nc.sbuf_tensor("lgt", [128, 8], F32))
        rmx = e_(nc.sbuf_tensor("rmx", [128, 1], F32))
        exr = e_(nc.sbuf_tensor("exr", [128, 8], F32))
        ssr = e_(nc.sbuf_tensor("ssr", [128, 1], F32))
        coefT = e_(nc.sbuf_tensor("coefT", [128, 8], F32))
        coef = e_(nc.sbuf_tensor("coef", [8, 512], F32))
        crow = e_(nc.sbuf_tensor("crw", [1, 512], F32))
        bcsb = e_(nc.sbuf_tensor("bcsb", [64, 512], F32))
        ones_col = e_(nc.sbuf_tensor("onesc", [128, 1], F32))
        ones_row = e_(nc.sbuf_tensor("onesr", [1, 128], F32))
        cb = e_(nc.sbuf_tensor("cb", [128, 4], F32))
        psA = e_(nc.psum_tensor("psA", [128, 512], F32))
        psB = e_(nc.psum_tensor("psB", [128, 512], F32))
        psC = e_(nc.psum_tensor("psC", [128, 512], F32))
        pr1 = e_(nc.psum_tensor("pr1", [1, 512], F32))
        pr2 = e_(nc.psum_tensor("pr2", [1, 512], F32))
        pt8 = e_(nc.psum_tensor("pt8", [128, 128], F32))

        pb.dl(lambda e: e.dma_start(out=IDN[:, :], in_=idn[:, :]))
        pb.dl(lambda e: e.dma_start(out=onwc[:, 0], in_=onw[:]))
        pb.dl(lambda e: e.dma_start(out=brc[:8, 0], in_=bra[:]))
        for dt in range(16):
            pb.dl(lambda e, dt=dt: e.dma_start(out=wrs[:, dt * 8:(dt + 1) * 8],
                                               in_=wra[dt * 128:(dt + 1) * 128, :]))
        pb.v(lambda e: e.memset(ones_col[:, :], 1.0))
        pb.v(lambda e: e.memset(ones_row[:1, :], 1.0))
        pb.v(lambda e: e.memset(cb[:, 0:1], 1e-5))
        pb.v(lambda e: e.memset(cb[:, 1:2], 1e-6))
        pb.v(lambda e: e.memset(cb[:, 2:3], 1.0))
        pb.v(lambda e: e.memset(cb[:, 3:4], -16.0))
        pb.bar()

        for gi in range(4):
            g0 = gi * 512
            for d in range(16):
                pb.dl(lambda e, d=d, g0=g0: e.dma_start(
                    out=xg[d][:, :], in_=xsh[d * 128:(d + 1) * 128, g0:g0 + 512]))
            pb.bar()
            for h in range(16):
                pb.dl(lambda e, h=h, g0=g0: e.dma_start(
                    out=ot[:, :], in_=osh[h * 128:(h + 1) * 128, g0:g0 + 512]))
                pb.dl(lambda e, h=h, g0=g0: e.dma_start(
                    out=gt[:, :], in_=gsh[h * 128:(h + 1) * 128, g0:g0 + 512]))
                pb.bar()
                emit_silu(pb, gt[:, :], gt[:, :], sq, 512)
                pb.v(lambda e, h=h: e.tensor_mul(on[h][:, :], ot[:, :], gt[:, :]))
                pb.bar()
                pb.s(lambda e, h=h: e.activation(out=sq[:, :512], in_=on[h][:, :], func=AF.Square))
                pb.bar()
                pb.t(lambda e: e.matmul(pr1[:1, :], ones_col[:, 0:1], sq[:, :512],
                                        start=True, stop=True))
                pb.bar()
                pb.s(lambda e: e.activation(out=mrow[:1, :], in_=pr1[:1, :], func=AF.Ln,
                                            scale=1.0 / 128.0, bias=cb[:1, 0:1]))
                pb.bar()
                pb.s(lambda e: e.activation(out=mrow[:1, :], in_=mrow[:1, :], func=AF.Exp, scale=-0.5))
                pb.bar()
                pb.t(lambda e: e.matmul(psA[:, :], ones_row[:1, :], mrow[:1, :],
                                        start=True, stop=True))
                pb.bar()
                pb.v(lambda e, h=h: e.scalar_tensor_tensor(
                    out=on[h][:, :], in0=on[h][:, :], scalar=onwc[:, 0:1],
                    in1=psA[:, :], op0=ALU.mult, op1=ALU.mult))
                pb.bar()
            for dm in range(16):
                for dt in range(16):
                    pb.dl(lambda e, dm=dm, dt=dt: e.dma_start(
                        out=wsb[:, dt * 128:(dt + 1) * 128],
                        in_=wo[dt * 128:(dt + 1) * 128, dm * 128:(dm + 1) * 128]))
                pb.bar()
                for dt in range(16):
                    pb.t(lambda e, dm=dm, dt=dt: e.matmul(
                        psB[:, :], wsb[:, dt * 128:(dt + 1) * 128], on[dt][:, :],
                        start=(dt == 0), stop=(dt == 15)))
                pb.bar()
                pb.v(lambda e, dm=dm: e.tensor_add(xg[dm][:, :], xg[dm][:, :], psB[:, :]))
                pb.bar()
            emit_ln(pb, xg, [(0, 512)], psA, psB, pr1, pr2, sq, mrow, rrow, scr,
                    xln, ones_col, ones_row, cb[:1, 0:1])
            for dt in range(16):
                pb.t(lambda e, dt=dt: e.matmul(
                    psA[:8, :], wrs[:, dt * 8:(dt + 1) * 8], xln[dt][:, :],
                    start=(dt == 0), stop=(dt == 15)))
            pb.bar()
            pb.v(lambda e: e.tensor_scalar(out=lg[:8, :], in0=psA[:8, :],
                                           scalar1=brc[:8, 0:1], scalar2=None, op0=ALU.add))
            pb.bar()
            for sub in range(4):
                pb.t(lambda e, sub=sub: e.transpose(pt8[:, :8], lg[:8, sub * 128:(sub + 1) * 128],
                                                    IDN[:8, :8]))
                pb.bar()
                pb.v(lambda e: e.tensor_copy(out=lgt[:, :8], in_=pt8[:, :8]))
                pb.bar()
                pb.v(lambda e: e.reduce_max(out=rmx[:, 0:1], in_=lgt[:, :8], axis=AX.X))
                pb.bar()
                pb.v(lambda e: e.tensor_scalar(out=rmx[:, 0:1], in0=rmx[:, 0:1],
                                               scalar1=-1.0, scalar2=None, op0=ALU.mult))
                pb.bar()
                pb.s(lambda e: e.activation(out=exr[:, :8], in_=lgt[:, :8], func=AF.Exp,
                                            bias=rmx[:, 0:1]))
                pb.bar()
                pb.v(lambda e: e.reduce_sum(out=ssr[:, 0:1], in_=exr[:, :8], axis=AX.X))
                pb.bar()
                pb.v(lambda e: e.reciprocal(out=ssr[:, 0:1], in_=ssr[:, 0:1]))
                pb.bar()
                pb.v(lambda e: e.tensor_scalar(out=ssr[:, 0:1], in0=ssr[:, 0:1],
                                               scalar1=1.0 / 64.0, scalar2=None, op0=ALU.mult))
                pb.bar()
                pb.v(lambda e: e.tensor_scalar(out=coefT[:, :8], in0=exr[:, :8],
                                               scalar1=1.0, scalar2=ssr[:, 0:1],
                                               op0=ALU.is_ge, op1=ALU.mult))
                pb.bar()
                pb.t(lambda e: e.transpose(pt8[:8, :], coefT[:, :8], IDN[:, :]))
                pb.bar()
                pb.v(lambda e, sub=sub: e.tensor_copy(out=coef[:8, sub * 128:(sub + 1) * 128],
                                                      in_=pt8[:8, :]))
                pb.bar()
            for ee in range(8):
                pb.dl(lambda e, ee=ee: e.dma_start(out=crow[:1, :], in_=coef[ee:ee + 1, :]))
                for dt in range(16):
                    pb.dl(lambda e, ee=ee, dt=dt: e.dma_start(
                        out=ffas[:, dt * 64:(dt + 1) * 64],
                        in_=ffa[ee, dt * 128:(dt + 1) * 128, :]))
                pb.bar()
                for dt in range(16):
                    pb.t(lambda e, dt=dt: e.matmul(
                        psA[:64, :], ffas[:, dt * 64:(dt + 1) * 64],
                        xln[dt][:, :], start=(dt == 0), stop=(dt == 15)))
                pb.t(lambda e: e.matmul(psB[:64, :], ones_row[:1, :64], crow[:1, :],
                                        start=True, stop=True))
                pb.bar()
                pb.v(lambda e: e.tensor_copy(out=bcsb[:64, :], in_=psB[:64, :]))
                pb.bar()
                pb.v(lambda e, ee=ee: e.tensor_mul(hprime[ee][:64, :], psA[:64, :], bcsb[:64, :]))
                pb.bar()
            for dm in range(16):
                for ee in range(8):
                    pb.dl(lambda e, ee=ee, dm=dm: e.dma_start(
                        out=ffbt[:64, ee * 128:(ee + 1) * 128],
                        in_=ffb[ee, :, dm * 128:(dm + 1) * 128]))
                pb.bar()
                for ee in range(8):
                    pb.t(lambda e, ee=ee: e.matmul(
                        psC[:, :], ffbt[:64, ee * 128:(ee + 1) * 128],
                        hprime[ee][:64, :], start=(ee == 0), stop=(ee == 7)))
                pb.bar()
                pb.v(lambda e, dm=dm: e.tensor_add(xg[dm][:, :], xg[dm][:, :], psC[:, :]))
                pb.bar()
                pb.dg(lambda e, dm=dm, g0=g0: e.dma_start(
                    out=x2[dm * 128:(dm + 1) * 128, g0:g0 + 512], in_=xg[dm][:, :]))
            pb.bar()
        pb.emit(nc)
    return nc


def build_C2():
    nc = bass.Bass("TRN2", target_bir_lowering=False)
    x2 = nc.dram_tensor("x2", [D, TPC], F32, kind="ExternalInput").ap()
    wq = nc.dram_tensor("wq", [NH, D, 128], F32, kind="ExternalInput").ap()
    wk = nc.dram_tensor("wk", [NH, D, 128], F32, kind="ExternalInput").ap()
    wv = nc.dram_tensor("wv", [NH, D, 128], F32, kind="ExternalInput").ap()
    q1 = nc.dram_tensor("q1", [NH, 128, TPC], BF16, kind="ExternalOutput").ap()
    k1 = nc.dram_tensor("k1", [NH, 128, TPC], BF16, kind="ExternalOutput").ap()
    v1 = nc.dram_tensor("v1", [NH, 128, TPC], BF16, kind="ExternalOutput").ap()

    pb = PB()
    with ExitStack() as st:
        e_ = st.enter_context
        xg = [e_(nc.sbuf_tensor("xsb%d" % d, [128, 512], F32)) for d in range(16)]
        xlb = [e_(nc.sbuf_tensor("xl%d" % d, [128, TPC], BF16)) for d in range(16)]
        sq = e_(nc.sbuf_tensor("sq", [128, 4 * 512], F32))
        scr = e_(nc.sbuf_tensor("scr", [1, 512], F32))
        mrow = e_(nc.sbuf_tensor("mrow", [1, 512], F32))
        rrow = e_(nc.sbuf_tensor("rrow", [1, 512], F32))
        wsb = e_(nc.sbuf_tensor("wsb", [128, 2048], BF16))
        outb = e_(nc.sbuf_tensor("outb", [128, 512], BF16))
        ones_col = e_(nc.sbuf_tensor("onesc", [128, 1], F32))
        ones_row = e_(nc.sbuf_tensor("onesr", [1, 128], F32))
        cb = e_(nc.sbuf_tensor("cb", [128, 4], F32))
        psA = e_(nc.psum_tensor("psA", [128, 512], F32))
        psB = e_(nc.psum_tensor("psB", [128, 512], F32))
        pr1 = e_(nc.psum_tensor("pr1", [1, 512], F32))
        pr2 = e_(nc.psum_tensor("pr2", [1, 512], F32))

        pb.v(lambda e: e.memset(ones_col[:, :], 1.0))
        pb.v(lambda e: e.memset(ones_row[:1, :], 1.0))
        pb.v(lambda e: e.memset(cb[:, 0:1], 1e-5))
        pb.v(lambda e: e.memset(cb[:, 1:2], 1e-6))
        pb.v(lambda e: e.memset(cb[:, 2:3], 1.0))
        pb.v(lambda e: e.memset(cb[:, 3:4], -16.0))
        pb.bar()
        for gi in range(4):
            g0 = gi * 512
            for d in range(16):
                pb.dl(lambda e, d=d, g0=g0: e.dma_start(
                    out=xg[d][:, :], in_=x2[d * 128:(d + 1) * 128, g0:g0 + 512]))
            pb.bar()
            emit_ln(pb, xg, [(0, 512)], psA, psB, pr1, pr2, sq, mrow, rrow, scr,
                    [xx[:, g0:g0 + 512] for xx in xlb], ones_col, ones_row, cb[:1, 0:1])
        for h in range(NH):
            for (wap, dst, scale) in ((wq, q1, QS), (wk, k1, None), (wv, v1, None)):
                for dt in range(16):
                    pb.dg(lambda e, h=h, dt=dt, wap=wap: e.dma_start(
                        out=wsb[:, dt * 128:(dt + 1) * 128],
                        in_=wap[h, dt * 128:(dt + 1) * 128, :]))
                pb.bar()
                for gi in range(4):
                    g0 = gi * 512
                    for dt in range(16):
                        pb.t(lambda e, dt=dt, g0=g0: e.matmul(
                            psA[:, :], wsb[:, dt * 128:(dt + 1) * 128],
                            xlb[dt][:, g0:g0 + 512], start=(dt == 0), stop=(dt == 15)))
                    pb.bar()
                    if scale is None:
                        pb.v(lambda e: e.tensor_copy(out=outb[:, :], in_=psA[:, :]))
                    else:
                        pb.v(lambda e, scale=scale: e.tensor_scalar(
                            out=outb[:, :], in0=psA[:, :], scalar1=scale,
                            scalar2=None, op0=ALU.mult))
                    pb.bar()
                    pb.dg(lambda e, h=h, g0=g0, dst=dst: e.dma_start(
                        out=dst[h, :, g0:g0 + 512], in_=outb[:, :]))
                pb.bar()
        pb.emit(nc)
    return nc


def build_E():
    nc = bass.Bass("TRN2", target_bir_lowering=False)
    q1 = nc.dram_tensor("q1", [4, 128, TB], BF16, kind="ExternalInput").ap()
    k1 = nc.dram_tensor("k1", [4, 128, TB], BF16, kind="ExternalInput").ap()
    v1 = nc.dram_tensor("v1", [4, 128, TB], BF16, kind="ExternalInput").ap()
    mneg = nc.dram_tensor("mneg", [128, 128], F32, kind="ExternalInput").ap()
    idn = nc.dram_tensor("idn", [128, 128], F32, kind="ExternalInput").ap()
    zf = nc.dram_tensor("zf", [4, 128, TB], BF16, kind="ExternalOutput").ap()

    pb = PB()
    with ExitStack() as st:
        e_ = st.enter_context
        qs = e_(nc.sbuf_tensor("qs", [128, TB], BF16))
        ks = e_(nc.sbuf_tensor("ks", [128, TB], BF16))
        vs = e_(nc.sbuf_tensor("vs", [128, TB], BF16))
        vt = e_(nc.sbuf_tensor("vt", [128, TB], BF16))
        zfm = e_(nc.sbuf_tensor("zfm", [128, TB], BF16))
        PT = [e_(nc.sbuf_tensor("PT%d" % i, [128, 512], BF16)) for i in range(2)]
        MNb = e_(nc.sbuf_tensor("MNb", [128, 128], BF16))
        IDN = e_(nc.sbuf_tensor("IDN", [128, 128], F32))
        IDNb = e_(nc.sbuf_tensor("IDNb", [128, 128], BF16))
        onecb = e_(nc.sbuf_tensor("onecb", [128, 1], BF16))
        cb = e_(nc.sbuf_tensor("cb", [128, 4], F32))
        lrow = e_(nc.sbuf_tensor("lrow", [1, 128], F32))
        lcol = e_(nc.sbuf_tensor("lcol", [128, 1], F32))
        zsc = e_(nc.sbuf_tensor("zsc", [128, 128], F32))
        pS = [e_(nc.psum_tensor("pS%d" % i, [128, 512], F32)) for i in range(2)]
        pZ = e_(nc.psum_tensor("pZ", [128, 128], F32))
        pL = e_(nc.psum_tensor("pL", [1, 128], F32))
        pT = e_(nc.psum_tensor("pT", [128, 128], F32))
        pTb = e_(nc.psum_tensor("pTb", [128, 128], BF16))

        pb.dl(lambda e: e.dma_start(out=IDN[:, :], in_=idn[:, :]))
        pb.dg(lambda e: e.dma_start(out=MNb[:, :], in_=mneg[:, :]))
        pb.dg(lambda e: e.dma_start(out=IDNb[:, :], in_=idn[:, :]))
        pb.v(lambda e: e.memset(onecb[:, :], 1.0))
        pb.v(lambda e: e.memset(cb[:, 3:4], -16.0))
        pb.bar()
        for inst in range(4):
            pb.dl(lambda e, inst=inst: e.dma_start(out=qs[:, :], in_=q1[inst, :, :]))
            pb.dl(lambda e, inst=inst: e.dma_start(out=ks[:, :], in_=k1[inst, :, :]))
            pb.dl(lambda e, inst=inst: e.dma_start(out=vs[:, :], in_=v1[inst, :, :]))
            pb.bar()
            for j in range(64):
                pb.t(lambda e, j=j: e.transpose(pTb[:, :], vs[:, j * 128:(j + 1) * 128], IDNb[:, :]))
                pb.bar()
                pb.s(lambda e, j=j: e.copy(out=vt[:, j * 128:(j + 1) * 128], in_=pTb[:, :]))
                pb.bar()
            for qb in range(64):
                nch = qb // 4 + 1
                nbtot = qb + 1
                done = 0
                for c in range(nch):
                    jlo = 4 * c
                    jhi = min(4 * c + 3, qb)
                    nb = jhi - jlo + 1
                    par = c % 2
                    for j in range(jlo, jhi + 1):
                        pb.t(lambda e, j=j, jlo=jlo, par=par, qb=qb: e.matmul(
                            pS[par][:, (j - jlo) * 128:(j - jlo + 1) * 128],
                            ks[:, j * 128:(j + 1) * 128], qs[:, qb * 128:(qb + 1) * 128],
                            start=True, stop=(j != qb), skip_group_check=True))
                    if jhi == qb:
                        pb.t(lambda e, par=par, nb=nb: e.matmul(
                            pS[par][:, (nb - 1) * 128:nb * 128], IDNb[:, :], MNb[:, :],
                            start=False, stop=True, skip_group_check=True))
                    pb.bar()
                    pb.s(lambda e, par=par, nb=nb: e.activation(
                        out=PT[par][:, :nb * 128], in_=pS[par][:, :nb * 128],
                        func=AF.Exp, bias=cb[:, 3:4]))
                    pb.bar()
                    for bi in range(nb):
                        pb.t(lambda e, par=par, bi=bi, done=done, nbtot=nbtot: e.matmul(
                            pL[:1, :], onecb[:, 0:1], PT[par][:, bi * 128:(bi + 1) * 128],
                            start=(done + bi == 0), stop=(done + bi == nbtot - 1),
                            skip_group_check=True))
                    for bi in range(nb):
                        pb.t(lambda e, par=par, bi=bi, jlo=jlo, done=done, nbtot=nbtot: e.matmul(
                            pZ[:, :], PT[par][:, bi * 128:(bi + 1) * 128],
                            vt[:, (jlo + bi) * 128:(jlo + bi + 1) * 128],
                            start=(done + bi == 0), stop=(done + bi == nbtot - 1),
                            skip_group_check=True))
                    pb.bar()
                    done += nb
                pb.v(lambda e: e.tensor_copy(out=lrow[:1, :], in_=pL[:1, :]))
                pb.bar()
                pb.t(lambda e: e.transpose(pT[:, :1], lrow[:1, :], IDN[:1, :1]))
                pb.bar()
                pb.v(lambda e: e.tensor_copy(out=lcol[:, 0:1], in_=pT[:, 0:1]))
                pb.bar()
                pb.v(lambda e: e.reciprocal(out=lcol[:, 0:1], in_=lcol[:, 0:1]))
                pb.bar()
                pb.v(lambda e: e.tensor_scalar(out=zsc[:, :], in0=pZ[:, :],
                                               scalar1=lcol[:, 0:1], scalar2=None, op0=ALU.mult))
                pb.bar()
                pb.t(lambda e: e.transpose(pT[:, :], zsc[:, :], IDN[:, :]))
                pb.bar()
                pb.v(lambda e, qb=qb: e.tensor_copy(out=zfm[:, qb * 128:(qb + 1) * 128], in_=pT[:, :]))
                pb.bar()
            pb.dg(lambda e, inst=inst: e.dma_start(out=zf[inst, :, :], in_=zfm[:, :]))
            pb.bar()
        pb.emit(nc)
    return nc


def build_F():
    nc = bass.Bass("TRN2", target_bir_lowering=False)
    x2 = nc.dram_tensor("x2", [D, TPC], F32, kind="ExternalInput").ap()
    zsh = nc.dram_tensor("zsh", [NH, 128, TPC], BF16, kind="ExternalInput").ap()
    wot = nc.dram_tensor("wot", [NH, 128, D], F32, kind="ExternalInput").ap()
    x3 = nc.dram_tensor("x3", [D, TPC], F32, kind="ExternalOutput").ap()

    pb = PB()
    with ExitStack() as st:
        e_ = st.enter_context
        zs = [e_(nc.sbuf_tensor("z%d" % h, [128, TPC], BF16)) for h in range(NH)]
        ws = e_(nc.sbuf_tensor("ws", [128, NH * 128], BF16))
        xtile = e_(nc.sbuf_tensor("xtile", [128, 512], F32))
        psA = e_(nc.psum_tensor("psA", [128, 512], F32))

        for h in range(NH):
            pb.dl(lambda e, h=h: e.dma_start(out=zs[h][:, :], in_=zsh[h, :, :]))
        pb.bar()
        for dm in range(16):
            for h in range(NH):
                pb.dg(lambda e, h=h, dm=dm: e.dma_start(
                    out=ws[:, h * 128:(h + 1) * 128],
                    in_=wot[h, :, dm * 128:(dm + 1) * 128]))
            pb.bar()
            for gi in range(4):
                g0 = gi * 512
                pb.dl(lambda e, dm=dm, g0=g0: e.dma_start(
                    out=xtile[:, :], in_=x2[dm * 128:(dm + 1) * 128, g0:g0 + 512]))
                for h in range(NH):
                    pb.t(lambda e, h=h, g0=g0: e.matmul(
                        psA[:, :], ws[:, h * 128:(h + 1) * 128], zs[h][:, g0:g0 + 512],
                        start=(h == 0), stop=(h == NH - 1)))
                pb.bar()
                pb.v(lambda e: e.tensor_add(xtile[:, :], xtile[:, :], psA[:, :]))
                pb.bar()
                pb.dg(lambda e, dm=dm, g0=g0: e.dma_start(
                    out=x3[dm * 128:(dm + 1) * 128, g0:g0 + 512], in_=xtile[:, :]))
                pb.bar()
        pb.emit(nc)
    return nc


def build_G():
    nc = bass.Bass("TRN2", target_bir_lowering=False)
    x3a = nc.dram_tensor("x3a", [D, 2048], F32, kind="ExternalInput").ap()
    tidx = nc.dram_tensor("tidx", [4, 2048], I32, kind="ExternalInput").ap()
    tw = nc.dram_tensor("tw", [4, 2048], F32, kind="ExternalInput").ap()
    eids = nc.dram_tensor("eids", [4, 2], F32, kind="ExternalInput").ap()
    gwt = nc.dram_tensor("gwt", [2, D, 1408], F32, kind="ExternalInput").ap()
    uwt = nc.dram_tensor("uwt", [2, D, 1408], F32, kind="ExternalInput").ap()
    dwt = nc.dram_tensor("dwt", [2, 1408, D], F32, kind="ExternalInput").ap()
    part = nc.dram_tensor("part", [D, 2048], F32, kind="ExternalOutput").ap()

    NI = 11
    pb = PB()
    with ExitStack() as st:
        e_ = st.enter_context
        xg = [e_(nc.sbuf_tensor("xsb%d" % d, [128, 512], F32)) for d in range(16)]
        xlb = [e_(nc.sbuf_tensor("xl%d" % d, [128, 1024], BF16)) for d in range(16)]
        hp = [[e_(nc.sbuf_tensor("hp%d_%d" % (ee, o), [128, 1024], BF16))
               for o in range(NI)] for ee in range(2)]
        sq = e_(nc.sbuf_tensor("sq", [128, 4 * 512], F32))
        scr = e_(nc.sbuf_tensor("scr", [1, 512], F32))
        mrow = e_(nc.sbuf_tensor("mrow", [1, 512], F32))
        rrow = e_(nc.sbuf_tensor("rrow", [1, 512], F32))
        gws = e_(nc.sbuf_tensor("gws", [128, 2048], BF16))
        uws = e_(nc.sbuf_tensor("uws", [128, 2048], BF16))
        dws = e_(nc.sbuf_tensor("dws", [128, 2 * NI * 128], BF16))
        idxf = e_(nc.sbuf_tensor("idxf", [4, 2048], F32))
        idxi = e_(nc.sbuf_tensor("idxi", [4, 2048], I32))
        tws = e_(nc.sbuf_tensor("tws", [4, 2048], F32))
        eid = e_(nc.sbuf_tensor("eid", [4, 2], F32))
        wmk = e_(nc.sbuf_tensor("wmk", [4, 2048], F32))
        mrowe = [e_(nc.sbuf_tensor("mr%d" % ee, [1, 2048], F32)) for ee in range(2)]
        gact = e_(nc.sbuf_tensor("gact", [128, 512], F32))
        uact = e_(nc.sbuf_tensor("uact", [128, 512], F32))
        bcm = e_(nc.sbuf_tensor("bcm", [128, 512], F32))
        ones_col = e_(nc.sbuf_tensor("onesc", [128, 1], F32))
        ones_row = e_(nc.sbuf_tensor("onesr", [1, 128], F32))
        cb = e_(nc.sbuf_tensor("cb", [128, 4], F32))
        psA = e_(nc.psum_tensor("psA", [128, 512], F32))
        psB = e_(nc.psum_tensor("psB", [128, 512], F32))
        psC = e_(nc.psum_tensor("psC", [128, 512], F32))
        pr1 = e_(nc.psum_tensor("pr1", [1, 512], F32))
        pr2 = e_(nc.psum_tensor("pr2", [1, 512], F32))

        pb.dl(lambda e: e.dma_start(out=idxi[:, :], in_=tidx[:, :]))
        pb.dl(lambda e: e.dma_start(out=tws[:, :], in_=tw[:, :]))
        pb.dl(lambda e: e.dma_start(out=eid[:, :], in_=eids[:, :]))
        pb.v(lambda e: e.memset(ones_col[:, :], 1.0))
        pb.v(lambda e: e.memset(ones_row[:1, :], 1.0))
        pb.v(lambda e: e.memset(cb[:, 0:1], 1e-5))
        pb.v(lambda e: e.memset(cb[:, 1:2], 1e-6))
        pb.v(lambda e: e.memset(cb[:, 2:3], 1.0))
        pb.v(lambda e: e.memset(cb[:, 3:4], -16.0))
        pb.bar()
        pb.v(lambda e: e.tensor_copy(out=idxf[:, :], in_=idxi[:, :]))
        pb.bar()
        for ee in range(2):
            pb.v(lambda e, ee=ee: e.tensor_scalar(out=wmk[:, :], in0=idxf[:, :],
                                                  scalar1=eid[:4, ee:ee + 1], scalar2=None,
                                                  op0=ALU.is_equal))
            pb.bar()
            pb.v(lambda e: e.tensor_mul(wmk[:, :], wmk[:, :], tws[:, :]))
            pb.bar()
            for gi in range(4):
                pb.t(lambda e, gi=gi: e.matmul(pr1[:1, :], ones_col[:4, 0:1],
                                               wmk[:4, gi * 512:(gi + 1) * 512],
                                               start=True, stop=True))
                pb.bar()
                pb.v(lambda e, ee=ee, gi=gi: e.tensor_copy(
                    out=mrowe[ee][:1, gi * 512:(gi + 1) * 512], in_=pr1[:1, :]))
                pb.bar()
        for hf in range(2):
            h0 = hf * 1024
            for gi2 in range(2):
                g0 = h0 + gi2 * 512
                for d in range(16):
                    pb.dl(lambda e, d=d, g0=g0: e.dma_start(
                        out=xg[d][:, :], in_=x3a[d * 128:(d + 1) * 128, g0:g0 + 512]))
                pb.bar()
                emit_ln(pb, xg, [(0, 512)], psA, psB, pr1, pr2, sq, mrow, rrow,
                        scr, [xx[:, gi2 * 512:gi2 * 512 + 512] for xx in xlb],
                        ones_col, ones_row, cb[:1, 0:1])
            for ee in range(2):
                for o in range(NI):
                    for dt in range(16):
                        pb.dg(lambda e, ee=ee, o=o, dt=dt: e.dma_start(
                            out=gws[:, dt * 128:(dt + 1) * 128],
                            in_=gwt[ee, dt * 128:(dt + 1) * 128, o * 128:(o + 1) * 128]))
                        pb.dg(lambda e, ee=ee, o=o, dt=dt: e.dma_start(
                            out=uws[:, dt * 128:(dt + 1) * 128],
                            in_=uwt[ee, dt * 128:(dt + 1) * 128, o * 128:(o + 1) * 128]))
                    pb.bar()
                    for gi2 in range(2):
                        gl = gi2 * 512
                        for dt in range(16):
                            pb.t(lambda e, dt=dt, gl=gl: e.matmul(
                                psA[:, :], gws[:, dt * 128:(dt + 1) * 128],
                                xlb[dt][:, gl:gl + 512], start=(dt == 0), stop=(dt == 15)))
                        for dt in range(16):
                            pb.t(lambda e, dt=dt, gl=gl: e.matmul(
                                psB[:, :], uws[:, dt * 128:(dt + 1) * 128],
                                xlb[dt][:, gl:gl + 512], start=(dt == 0), stop=(dt == 15)))
                        pb.t(lambda e, ee=ee, h0=h0, gl=gl: e.matmul(
                            psC[:, :], ones_row[:1, :], mrowe[ee][:1, h0 + gl:h0 + gl + 512],
                            start=True, stop=True))
                        pb.bar()
                        pb.s(lambda e: e.activation(out=gact[:, :], in_=psA[:, :],
                                                    func=AF.Exp, scale=-1.0))
                        pb.bar()
                        pb.v(lambda e: e.tensor_scalar(out=gact[:, :], in0=gact[:, :],
                                                       scalar1=1.0, scalar2=None, op0=ALU.add))
                        pb.bar()
                        pb.v(lambda e: e.reciprocal(out=gact[:, :], in_=gact[:, :]))
                        pb.bar()
                        pb.v(lambda e: e.tensor_mul(gact[:, :], gact[:, :], psA[:, :]))
                        pb.bar()
                        pb.v(lambda e: e.tensor_mul(uact[:, :], gact[:, :], psB[:, :]))
                        pb.bar()
                        pb.v(lambda e: e.tensor_copy(out=bcm[:, :], in_=psC[:, :]))
                        pb.bar()
                        pb.v(lambda e, ee=ee, o=o, gl=gl: e.tensor_mul(
                            hp[ee][o][:, gl:gl + 512], uact[:, :], bcm[:, :]))
                        pb.bar()
            for dm in range(16):
                for ee in range(2):
                    for o in range(NI):
                        pb.dg(lambda e, ee=ee, o=o, dm=dm: e.dma_start(
                            out=dws[:, (ee * NI + o) * 128:(ee * NI + o + 1) * 128],
                            in_=dwt[ee, o * 128:(o + 1) * 128, dm * 128:(dm + 1) * 128]))
                pb.bar()
                for gi2 in range(2):
                    gl = gi2 * 512
                    for i2 in range(2 * NI):
                        ee, o = i2 // NI, i2 % NI
                        pb.t(lambda e, ee=ee, o=o, gl=gl, i2=i2: e.matmul(
                            psA[:, :], dws[:, (ee * NI + o) * 128:(ee * NI + o + 1) * 128],
                            hp[ee][o][:, gl:gl + 512], start=(i2 == 0), stop=(i2 == 2 * NI - 1)))
                    pb.bar()
                    pb.v(lambda e: e.tensor_copy(out=gact[:, :], in_=psA[:, :]))
                    pb.bar()
                    pb.dg(lambda e, dm=dm, h0=h0, gl=gl: e.dma_start(
                        out=part[dm * 128:(dm + 1) * 128, h0 + gl:h0 + gl + 512],
                        in_=gact[:, :]))
                pb.bar()
        pb.emit(nc)
    return nc


def build_H():
    nc = bass.Bass("TRN2", target_bir_lowering=False)
    parts = nc.dram_tensor("parts", [8, D, 256], F32, kind="ExternalInput").ap()
    x3s = nc.dram_tensor("x3s", [D, 256], F32, kind="ExternalInput").ap()
    outh = nc.dram_tensor("outh", [D, 256], F32, kind="ExternalOutput").ap()
    pb = PB()
    with ExitStack() as st:
        e_ = st.enter_context
        acc = e_(nc.sbuf_tensor("acc", [128, 256], F32))
        pt = e_(nc.sbuf_tensor("pt", [128, 256], F32))
        for d in range(16):
            pb.dl(lambda e, d=d: e.dma_start(out=acc[:, :], in_=x3s[d * 128:(d + 1) * 128, :]))
            pb.bar()
            for i in range(8):
                pb.dl(lambda e, i=i, d=d: e.dma_start(out=pt[:, :],
                                                      in_=parts[i, d * 128:(d + 1) * 128, :]))
                pb.bar()
                pb.v(lambda e: e.tensor_add(acc[:, :], acc[:, :], pt[:, :]))
                pb.bar()
            pb.dg(lambda e, d=d: e.dma_start(out=outh[d * 128:(d + 1) * 128, :], in_=acc[:, :]))
            pb.bar()
        pb.emit(nc)
    return nc


def kernel(L0_attn_hidden_states, L0_attn_W_qkv, L0_attn_W_ab, L0_attn_conv_w,
           L0_attn_A_log, L0_attn_dt_bias, L0_attn_o_norm_w, L0_attn_W_o,
           L0_ffn_A, L0_ffn_B, L0_ffn_Wr, L0_ffn_br,
           L1_attn_W_Q, L1_attn_W_K, L1_attn_W_V, L1_attn_W_O,
           L1_ffn_topk_idx, L1_ffn_topk_weight, L1_ffn_gate_weight,
           L1_ffn_up_weight, L1_ffn_down_weight):
    cores = list(range(NC))
    f32 = np.float32
    asc = np.ascontiguousarray
    mUi, mLs, mNeg = _np_masks()
    idn = np.eye(128, dtype=f32)

    x = np.asarray(L0_attn_hidden_states, f32)
    x_fm = asc(x.reshape(NTOK, D).T)

    ncA = build_A()
    imA = []
    wab = np.asarray(L0_attn_W_ab, f32)
    for c in cores:
        xh = np.zeros((D, TPC + 3), f32)
        xh[:, 3:] = x_fm[:, c * TPC:(c + 1) * TPC]
        if c % 4 != 0:
            xh[:, :3] = x_fm[:, c * TPC - 3:c * TPC]
        imA.append({"xh": asc(xh), "wqkv": np.asarray(L0_attn_W_qkv, f32),
                    "waa": asc(wab[:, 0:16]), "wbb": asc(wab[:, 16:32]),
                    "wgate": asc(wab[:, 32:]),
                    "convw": np.asarray(L0_attn_conv_w, f32),
                    "alog": np.asarray(L0_attn_A_log, f32),
                    "dtb": np.asarray(L0_attn_dt_bias, f32)})
    rA = run_bass_kernel_spmd(ncA, imA, core_ids=cores).results
    qf_g = np.concatenate([r["qf"] for r in rA], axis=2)
    kf_g = np.concatenate([r["kf"] for r in rA], axis=2)
    vf_g = np.concatenate([r["vf"] for r in rA], axis=2)
    gg_g = np.concatenate([r["gg"] for r in rA], axis=1)
    bb_g = np.concatenate([r["bb"] for r in rA], axis=1)
    gate_g = np.concatenate([r["gatef"] for r in rA], axis=1)

    insts = [(i // NH, i % NH) for i in range(32)]

    ncB = build_B()
    imB = []
    for c in cores:
        my = insts[4 * c:4 * c + 4]
        imB.append({
            "qb": asc(np.stack([qf_g[h, :, b * TB:(b + 1) * TB] for (b, h) in my])),
            "kb": asc(np.stack([kf_g[h, :, b * TB:(b + 1) * TB] for (b, h) in my])),
            "vb": asc(np.stack([vf_g[h, :, b * TB:(b + 1) * TB] for (b, h) in my])),
            "gb": asc(np.stack([gg_g[h, b * TB:(b + 1) * TB].reshape(64, 128).T for (b, h) in my])),
            "bbeta": asc(np.stack([bb_g[h, b * TB:(b + 1) * TB].reshape(64, 128).T for (b, h) in my])),
            "mui": mUi, "mls": mLs, "idn": idn})
    rB = run_bass_kernel_spmd(ncB, imB, core_ids=cores).results
    o_g = np.zeros((2, NH, 128, TB), f32)
    for c in cores:
        for k2, (b, h) in enumerate(insts[4 * c:4 * c + 4]):
            o_g[b, h] = rB[c]["of"][k2]

    ncC1 = build_C1()
    imC1 = []
    for c in cores:
        b = c // 4
        tl = (c * TPC) % TB
        o_b = o_g[b].reshape(D, TB)
        imC1.append({"xsh": asc(x_fm[:, c * TPC:(c + 1) * TPC]),
                     "osh": asc(o_b[:, tl:tl + TPC]),
                     "gsh": asc(gate_g[:, c * TPC:(c + 1) * TPC]),
                     "onw": np.asarray(L0_attn_o_norm_w, f32),
                     "wo": np.asarray(L0_attn_W_o, f32),
                     "wra": asc(np.asarray(L0_ffn_Wr, f32).T),
                     "bra": np.asarray(L0_ffn_br, f32),
                     "ffa": asc(np.asarray(L0_ffn_A, f32).transpose(0, 2, 1)),
                     "ffb": asc(np.asarray(L0_ffn_B, f32).transpose(0, 2, 1)),
                     "idn": idn})
    rC1 = run_bass_kernel_spmd(ncC1, imC1, core_ids=cores).results
    x2_sh = [r["x2"] for r in rC1]

    ncC2 = build_C2()
    wqh = asc(np.asarray(L1_attn_W_Q, f32).transpose(0, 2, 1))
    wkh = asc(np.asarray(L1_attn_W_K, f32).transpose(0, 2, 1))
    wvh = asc(np.asarray(L1_attn_W_V, f32).transpose(0, 2, 1))
    imC2 = [{"x2": asc(x2_sh[c]), "wq": wqh, "wk": wkh, "wv": wvh} for c in cores]
    rC2 = run_bass_kernel_spmd(ncC2, imC2, core_ids=cores).results
    q1_g = np.concatenate([r["q1"] for r in rC2], axis=2)
    k1_g = np.concatenate([r["k1"] for r in rC2], axis=2)
    v1_g = np.concatenate([r["v1"] for r in rC2], axis=2)

    ncE = build_E()
    imE = []
    for c in cores:
        my = insts[4 * c:4 * c + 4]
        imE.append({"q1": asc(np.stack([q1_g[h, :, b * TB:(b + 1) * TB] for (b, h) in my])),
                    "k1": asc(np.stack([k1_g[h, :, b * TB:(b + 1) * TB] for (b, h) in my])),
                    "v1": asc(np.stack([v1_g[h, :, b * TB:(b + 1) * TB] for (b, h) in my])),
                    "mneg": mNeg, "idn": idn})
    rE = run_bass_kernel_spmd(ncE, imE, core_ids=cores).results
    z_g = np.zeros((2, NH, 128, TB), rE[0]["zf"].dtype)
    for c in cores:
        for k2, (b, h) in enumerate(insts[4 * c:4 * c + 4]):
            z_g[b, h] = rE[c]["zf"][k2]

    ncF = build_F()
    woth = asc(np.asarray(L1_attn_W_O, f32).transpose(0, 2, 1))
    imF = []
    for c in cores:
        b = c // 4
        tl = (c * TPC) % TB
        imF.append({"x2": asc(x2_sh[c]), "zsh": asc(z_g[b][:, :, tl:tl + TPC]),
                    "wot": woth})
    rF = run_bass_kernel_spmd(ncF, imF, core_ids=cores).results
    x3_g = np.concatenate([r["x3"] for r in rF], axis=1)

    ncG = build_G()
    x3a = asc(x3_g[:, :2048])
    tidx = asc(np.asarray(L1_ffn_topk_idx).reshape(2048, 4).T.astype(np.int32))
    twr = asc(np.asarray(L1_ffn_topk_weight, f32).reshape(2048, 4).T)
    gwt = np.asarray(L1_ffn_gate_weight, f32).transpose(0, 2, 1)
    uwt = np.asarray(L1_ffn_up_weight, f32).transpose(0, 2, 1)
    dwt = np.asarray(L1_ffn_down_weight, f32).transpose(0, 2, 1)
    imG = []
    for c in cores:
        eids = np.zeros((4, 2), f32)
        eids[:, 0] = 2 * c
        eids[:, 1] = 2 * c + 1
        imG.append({"x3a": x3a, "tidx": tidx, "tw": twr, "eids": eids,
                    "gwt": asc(gwt[2 * c:2 * c + 2]),
                    "uwt": asc(uwt[2 * c:2 * c + 2]),
                    "dwt": asc(dwt[2 * c:2 * c + 2])})
    rG = run_bass_kernel_spmd(ncG, imG, core_ids=cores).results
    parts_g = np.stack([r["part"] for r in rG])

    ncH = build_H()
    imH = []
    for c in cores:
        imH.append({"parts": asc(parts_g[:, :, c * 256:(c + 1) * 256]),
                    "x3s": asc(x3a[:, c * 256:(c + 1) * 256])})
    rH = run_bass_kernel_spmd(ncH, imH, core_ids=cores).results
    outh = np.concatenate([r["outh"] for r in rH], axis=1)

    final = np.array(x3_g)
    final[:, :2048] = outh
    return asc(final.T).reshape(2, TB, D).astype(np.float32)


# version 02
# speedup vs baseline: 1.2157x; 1.2157x over previous
import numpy as np
from contextlib import ExitStack
import concourse.bass as bass
import concourse.mybir as mybir
from concourse.bass_utils import run_bass_kernel_spmd

F32 = mybir.dt.float32
BF16 = mybir.dt.bfloat16
I32 = mybir.dt.int32
AF = mybir.ActivationFunctionType
ALU = mybir.AluOpType
AX = mybir.AxisListType

NC = 8
B_CHUNKS = 64
D = 2048
TPC = 2048
NTOK = 16384
NH = 16
TB = 8192
QS = 0.08838834764831845

ENGS = ("tensor", "vector", "scalar", "sync", "gpsimd")


class PB:

    def __init__(self):
        self.items = {e: [] for e in ENGS}
        self.total = 0
        self.waited = {e: 0 for e in ENGS}
        self.cur = {e: None for e in ENGS}
        self.dmas = {e: 0 for e in ENGS}
        self.pdma = {e: False for e in ENGS}

    def op(self, eng, fn, dma=False):
        it = {"fn": fn, "waits": [], "incs": []}
        if self.waited[eng] < self.total:
            it["waits"].append(("P", self.total))
            self.waited[eng] = self.total
        self.items[eng].append(it)
        if dma:
            self.dmas[eng] += 1
            it["incs"].append(("D" + eng, 16))
            self.pdma[eng] = True
        else:
            self.cur[eng] = it
        return it

    def t(self, fn): self.op("tensor", fn)
    def v(self, fn): self.op("vector", fn)
    def s(self, fn): self.op("scalar", fn)
    def g(self, fn): self.op("gpsimd", fn)
    def dl(self, fn): self.op("sync", fn, dma=True)
    def dg(self, fn): self.op("gpsimd", fn, dma=True)

    def bar(self):
        n = 0
        for e in ENGS:
            if self.cur[e] is not None:
                self.cur[e]["incs"].append(("P", 1))
                n += 1
                self.cur[e] = None
            if self.pdma[e]:
                it = {"fn": None, "waits": [("D" + e, 16 * self.dmas[e])],
                      "incs": [("P", 1)]}
                self.items[e].append(it)
                n += 1
                self.pdma[e] = False
        self.total += n

    def emit(self, nc):
        self.bar()
        with ExitStack() as st:
            sems = {"P": st.enter_context(nc.semaphore("Pph"))}
            for e in ENGS:
                sems["D" + e] = st.enter_context(nc.semaphore("Dq_" + e))
            block = st.enter_context(nc.Block())
            for e in ENGS:
                items = self.items[e]
                if not items:
                    continue

                def body(eng, items=items):
                    for it in items:
                        for (s_, v_) in it["waits"]:
                            eng.wait_ge(sems[s_], v_)
                        if it["fn"] is None:
                            for (s_, n_) in it["incs"]:
                                eng.sem_inc(sems[s_], n_)
                        else:
                            inst = it["fn"](eng)
                            for (s_, n_) in it["incs"]:
                                inst = inst.then_inc(sems[s_], n_)
                getattr(block, e)(body)


def _np_masks():
    i = np.arange(128)
    mUi = (i[:, None] <= i[None, :]).astype(np.float32)
    mLs = (i[:, None] > i[None, :]).astype(np.float32)
    mNeg = np.where(i[:, None] > i[None, :], -200.0, 0.0).astype(np.float32)
    return mUi, mLs, mNeg


def emit_ln(pb, xin, groups, psA, psB, pr1, pr2, sq, mrow, rrow, scr, xout,
            ones_col, ones_row, ebias, d_total=2048.0):
    nd = len(xin)
    for (off, w) in groups:
        for r0 in range(0, nd, 4):
            for d in range(r0, min(r0 + 4, nd)):
                pb.s(lambda e, d=d, r0=r0, off=off, w=w: e.activation(
                    out=sq[:, (d - r0) * 512:(d - r0) * 512 + w],
                    in_=xin[d][:, off:off + w], func=AF.Square))
            pb.bar()
            for d in range(r0, min(r0 + 4, nd)):
                pb.t(lambda e, d=d, off=off, w=w: e.matmul(
                    pr1[:1, :w], ones_col[:, 0:1], xin[d][:, off:off + w],
                    start=(d == 0), stop=(d == nd - 1)))
            for d in range(r0, min(r0 + 4, nd)):
                pb.t(lambda e, d=d, r0=r0, w=w: e.matmul(
                    pr2[:1, :w], ones_col[:, 0:1], sq[:, (d - r0) * 512:(d - r0) * 512 + w],
                    start=(d == 0), stop=(d == nd - 1)))
            pb.bar()
        pb.v(lambda e, w=w: e.tensor_scalar(
            out=mrow[:1, :w], in0=pr1[:1, :w], scalar1=1.0 / d_total,
            scalar2=None, op0=ALU.mult))
        pb.bar()
        pb.v(lambda e, w=w: e.tensor_mul(scr[:1, :w], mrow[:1, :w], mrow[:1, :w]))
        pb.bar()
        pb.v(lambda e, w=w: e.scalar_tensor_tensor(
            out=rrow[:1, :w], in0=pr2[:1, :w], scalar=1.0 / d_total,
            in1=scr[:1, :w], op0=ALU.mult, op1=ALU.subtract))
        pb.bar()
        pb.s(lambda e, w=w: e.activation(out=rrow[:1, :w], in_=rrow[:1, :w],
                                         func=AF.Ln, bias=ebias))
        pb.bar()
        pb.s(lambda e, w=w: e.activation(out=rrow[:1, :w], in_=rrow[:1, :w],
                                         func=AF.Exp, scale=-0.5))
        pb.bar()
        pb.t(lambda e, w=w: e.matmul(psA[:, :w], ones_row[:1, :], mrow[:1, :w],
                                     start=True, stop=True))
        pb.t(lambda e, w=w: e.matmul(psB[:, :w], ones_row[:1, :], rrow[:1, :w],
                                     start=True, stop=True))
        pb.bar()
        for d in range(nd):
            pb.v(lambda e, d=d, off=off, w=w: e.tensor_sub(
                sq[:, 0:w], xin[d][:, off:off + w], psA[:, :w]))
            pb.bar()
            pb.v(lambda e, d=d, off=off, w=w: e.tensor_mul(
                xout[d][:, off:off + w], sq[:, 0:w], psB[:, :w]))
            pb.bar()


def emit_silu(pb, out_ap, in_ap, scratch, w):
    pb.s(lambda e: e.activation(out=scratch[:, :w], in_=in_ap, func=AF.Exp, scale=-1.0))
    pb.bar()
    pb.v(lambda e: e.tensor_scalar(out=scratch[:, :w], in0=scratch[:, :w],
                                   scalar1=1.0, scalar2=None, op0=ALU.add))
    pb.bar()
    pb.v(lambda e: e.reciprocal(out=scratch[:, :w], in_=scratch[:, :w]))
    pb.bar()
    pb.v(lambda e: e.tensor_mul(out_ap, in_ap, scratch[:, :w]))
    pb.bar()


def build_A():
    nc = bass.Bass("TRN2", target_bir_lowering=False)
    xh = nc.dram_tensor("xh", [D, TPC + 3], F32, kind="ExternalInput").ap()
    wqkv = nc.dram_tensor("wqkv", [D, 6144], F32, kind="ExternalInput").ap()
    waa = nc.dram_tensor("waa", [D, 16], F32, kind="ExternalInput").ap()
    wbb = nc.dram_tensor("wbb", [D, 16], F32, kind="ExternalInput").ap()
    wgate = nc.dram_tensor("wgate", [D, 2048], F32, kind="ExternalInput").ap()
    convw = nc.dram_tensor("convw", [6144, 4], F32, kind="ExternalInput").ap()
    alog = nc.dram_tensor("alog", [16], F32, kind="ExternalInput").ap()
    dtb = nc.dram_tensor("dtb", [16], F32, kind="ExternalInput").ap()
    qf = nc.dram_tensor("qf", [NH, 128, TPC], F32, kind="ExternalOutput").ap()
    kf = nc.dram_tensor("kf", [NH, 128, TPC], F32, kind="ExternalOutput").ap()
    vf = nc.dram_tensor("vf", [NH, 128, TPC], F32, kind="ExternalOutput").ap()
    gg = nc.dram_tensor("gg", [16, TPC], F32, kind="ExternalOutput").ap()
    bb = nc.dram_tensor("bb", [16, TPC], F32, kind="ExternalOutput").ap()
    gatef = nc.dram_tensor("gatef", [D, TPC], F32, kind="ExternalOutput").ap()

    W = TPC + 3
    groups = [(0, 512), (512, 512), (1024, 512), (1536, 512), (2048, 3)]
    pb = PB()
    with ExitStack() as st:
        e_ = st.enter_context
        xt = [e_(nc.sbuf_tensor("xsb%d" % d, [128, W], F32)) for d in range(16)]
        sq = e_(nc.sbuf_tensor("sq", [128, 4 * 512], F32))
        scr = e_(nc.sbuf_tensor("scr", [1, 512], F32))
        mrow = e_(nc.sbuf_tensor("mrow", [1, 512], F32))
        rrow = e_(nc.sbuf_tensor("rrow", [1, 512], F32))
        wsb = e_(nc.sbuf_tensor("wsb", [128, 2048], F32))
        qkvg = [e_(nc.sbuf_tensor("qkvg%d" % i, [128, 512], F32)) for i in range(2)]
        conv = e_(nc.sbuf_tensor("conv", [128, 512], F32))
        esb = e_(nc.sbuf_tensor("esb", [128, 512], F32))
        sln = e_(nc.sbuf_tensor("sln", [128, 512], F32))
        outt = e_(nc.sbuf_tensor("outt", [128, 512], F32))
        cw = e_(nc.sbuf_tensor("cw", [128, 48 * 4], F32))
        av = e_(nc.sbuf_tensor("av", [16, 4], F32))
        ones_col = e_(nc.sbuf_tensor("onesc", [128, 1], F32))
        ones_row = e_(nc.sbuf_tensor("onesr", [1, 128], F32))
        cb = e_(nc.sbuf_tensor("cb", [128, 4], F32))
        abg = e_(nc.sbuf_tensor("abg", [16, 512], F32))
        ab2 = e_(nc.sbuf_tensor("ab2", [16, 512], F32))
        psA = e_(nc.psum_tensor("psA", [128, 512], F32))
        psB = e_(nc.psum_tensor("psB", [128, 512], F32))
        pr1 = e_(nc.psum_tensor("pr1", [1, 512], F32))
        pr2 = e_(nc.psum_tensor("pr2", [1, 512], F32))

        for d in range(16):
            pb.dl(lambda e, d=d: e.dma_start(out=xt[d][:, :], in_=xh[d * 128:(d + 1) * 128, :]))
        for o in range(48):
            pb.dl(lambda e, o=o: e.dma_start(out=cw[:, o * 4:o * 4 + 4],
                                             in_=convw[o * 128:(o + 1) * 128, :]))
        pb.dl(lambda e: e.dma_start(out=av[:, 0], in_=alog[:]))
        pb.dl(lambda e: e.dma_start(out=av[:, 1], in_=dtb[:]))
        pb.v(lambda e: e.memset(ones_col[:, :], 1.0))
        pb.v(lambda e: e.memset(ones_row[:1, :], 1.0))
        pb.v(lambda e: e.memset(cb[:, 0:1], 1e-5))
        pb.v(lambda e: e.memset(cb[:, 1:2], 1e-6))
        pb.v(lambda e: e.memset(cb[:, 2:3], 1.0))
        pb.v(lambda e: e.memset(cb[:, 3:4], -16.0))
        pb.bar()
        pb.s(lambda e: e.activation(out=av[:16, 2:3], in_=av[:16, 0:1], func=AF.Exp))
        pb.bar()
        pb.v(lambda e: e.tensor_scalar(out=av[:16, 2:3], in0=av[:16, 2:3],
                                       scalar1=-1.0, scalar2=None, op0=ALU.mult))
        pb.bar()
        emit_ln(pb, xt, groups, psA, psB, pr1, pr2, sq, mrow, rrow, scr, xt,
                ones_col, ones_row, cb[:1, 0:1])

        for o in range(48):
            for dt in range(16):
                pb.dl(lambda e, dt=dt, o=o: e.dma_start(
                    out=wsb[:, dt * 128:(dt + 1) * 128],
                    in_=wqkv[dt * 128:(dt + 1) * 128, o * 128:(o + 1) * 128]))
            pb.bar()
            for gi, (off, w) in enumerate(groups):
                cur = qkvg[gi % 2]
                prev = qkvg[(gi + 1) % 2]
                for dt in range(16):
                    pb.t(lambda e, dt=dt, off=off, w=w: e.matmul(
                        psA[:, :w], wsb[:, dt * 128:(dt + 1) * 128],
                        xt[dt][:, off:off + w], start=(dt == 0), stop=(dt == 15)))
                pb.bar()
                pb.v(lambda e, cur=cur, w=w: e.tensor_copy(out=cur[:, :w], in_=psA[:, :w]))
                pb.bar()
                pb.v(lambda e, cur=cur, w=w, o=o: e.tensor_scalar(
                    out=conv[:, :w], in0=cur[:, :w], scalar1=cw[:, o * 4 + 3:o * 4 + 4],
                    scalar2=None, op0=ALU.mult))
                pb.bar()
                for j in range(3):
                    if w > 3 - j:
                        pb.v(lambda e, j=j, cur=cur, w=w, o=o: e.scalar_tensor_tensor(
                            out=conv[:, 3 - j:w], in0=cur[:, 0:w - (3 - j)],
                            scalar=cw[:, o * 4 + j:o * 4 + j + 1],
                            in1=conv[:, 3 - j:w], op0=ALU.mult, op1=ALU.add))
                        pb.bar()
                    if gi > 0:
                        pw = groups[gi - 1][1]
                        pb.v(lambda e, j=j, prev=prev, o=o, pw=pw: e.scalar_tensor_tensor(
                            out=conv[:, 0:3 - j], in0=prev[:, pw - (3 - j):pw],
                            scalar=cw[:, o * 4 + j:o * 4 + j + 1],
                            in1=conv[:, 0:3 - j], op0=ALU.mult, op1=ALU.add))
                        pb.bar()
                emit_silu(pb, sln[:, :w], conv[:, :w], esb, w)
                lo = 3 if gi == 0 else 0
                hb0 = off + lo - 3
                sw = w - lo
                if sw <= 0:
                    continue
                if o < 16:
                    h, dst, scale = o, qf, QS
                elif o < 32:
                    h, dst, scale = o - 16, kf, 1.0
                else:
                    h, dst, scale = o - 32, vf, None
                if scale is None:
                    pb.dg(lambda e, h=h, hb0=hb0, sw=sw, lo=lo, w=w, dst=dst: e.dma_start(
                        out=dst[h, :, hb0:hb0 + sw], in_=sln[:, lo:w]))
                else:
                    pb.s(lambda e, w=w: e.activation(out=esb[:, :w], in_=sln[:, :w],
                                                     func=AF.Square))
                    pb.bar()
                    pb.t(lambda e, w=w: e.matmul(pr1[:1, :w], ones_col[:, 0:1],
                                                 esb[:, :w], start=True, stop=True))
                    pb.bar()
                    pb.s(lambda e, w=w: e.activation(out=mrow[:1, :w], in_=pr1[:1, :w],
                                                     func=AF.Ln, bias=cb[:1, 1:2]))
                    pb.bar()
                    pb.s(lambda e, w=w: e.activation(out=mrow[:1, :w], in_=mrow[:1, :w],
                                                     func=AF.Exp, scale=-0.5))
                    pb.bar()
                    pb.t(lambda e, w=w: e.matmul(psB[:, :w], ones_row[:1, :],
                                                 mrow[:1, :w], start=True, stop=True))
                    pb.bar()
                    pb.v(lambda e, w=w, scale=scale: e.scalar_tensor_tensor(
                        out=outt[:, :w], in0=sln[:, :w], scalar=scale,
                        in1=psB[:, :w], op0=ALU.mult, op1=ALU.mult))
                    pb.bar()
                    pb.dg(lambda e, h=h, hb0=hb0, sw=sw, lo=lo, w=w, dst=dst: e.dma_start(
                        out=dst[h, :, hb0:hb0 + sw], in_=outt[:, lo:w]))
            pb.bar()

        for o in range(16):
            for dt in range(16):
                pb.dl(lambda e, dt=dt, o=o: e.dma_start(
                    out=wsb[:, dt * 128:(dt + 1) * 128],
                    in_=wgate[dt * 128:(dt + 1) * 128, o * 128:(o + 1) * 128]))
            pb.bar()
            for gi, (off, w) in enumerate(groups):
                lo = 3 if gi == 0 else 0
                sw = w - lo
                for dt in range(16):
                    pb.t(lambda e, dt=dt, off=off, w=w: e.matmul(
                        psA[:, :w], wsb[:, dt * 128:(dt + 1) * 128],
                        xt[dt][:, off:off + w], start=(dt == 0), stop=(dt == 15)))
                pb.bar()
                pb.v(lambda e, w=w: e.tensor_copy(out=outt[:, :w], in_=psA[:, :w]))
                pb.bar()
                if sw > 0:
                    pb.dg(lambda e, o=o, off=off, lo=lo, sw=sw, w=w: e.dma_start(
                        out=gatef[o * 128:(o + 1) * 128, off + lo - 3:off + lo - 3 + sw],
                        in_=outt[:, lo:w]))
            pb.bar()

        for which in range(2):
            wap = waa if which == 0 else wbb
            for dt in range(16):
                pb.dl(lambda e, dt=dt, wap=wap: e.dma_start(
                    out=wsb[:, dt * 16:(dt + 1) * 16],
                    in_=wap[dt * 128:(dt + 1) * 128, :]))
            pb.bar()
            for gi, (off, w) in enumerate(groups):
                lo = 3 if gi == 0 else 0
                sw = w - lo
                for dt in range(16):
                    pb.t(lambda e, dt=dt, off=off, w=w: e.matmul(
                        psA[:16, :w], wsb[:, dt * 16:(dt + 1) * 16],
                        xt[dt][:, off:off + w], start=(dt == 0), stop=(dt == 15)))
                pb.bar()
                if which == 0:
                    pb.s(lambda e, w=w: e.activation(out=abg[:16, :w], in_=psA[:16, :w],
                                                     func=AF.Exp, bias=av[:16, 1:2]))
                    pb.bar()
                    pb.s(lambda e, w=w: e.activation(out=abg[:16, :w], in_=abg[:16, :w],
                                                     func=AF.Ln, bias=cb[:16, 2:3]))
                    pb.bar()
                    pb.v(lambda e, w=w: e.tensor_scalar(
                        out=ab2[:16, :w], in0=abg[:16, :w], scalar1=av[:16, 2:3],
                        scalar2=None, op0=ALU.mult))
                    pb.bar()
                    if sw > 0:
                        pb.dg(lambda e, off=off, lo=lo, sw=sw, w=w: e.dma_start(
                            out=gg[:, off + lo - 3:off + lo - 3 + sw], in_=ab2[:16, lo:w]))
                else:
                    pb.s(lambda e, w=w: e.activation(out=abg[:16, :w], in_=psA[:16, :w],
                                                     func=AF.Exp, scale=-1.0))
                    pb.bar()
                    pb.v(lambda e, w=w: e.tensor_scalar(out=abg[:16, :w], in0=abg[:16, :w],
                                                        scalar1=1.0, scalar2=None, op0=ALU.add))
                    pb.bar()
                    pb.v(lambda e, w=w: e.reciprocal(out=abg[:16, :w], in_=abg[:16, :w]))
                    pb.bar()
                    pb.v(lambda e, w=w: e.tensor_scalar(out=ab2[:16, :w], in0=abg[:16, :w],
                                                        scalar1=2.0, scalar2=None, op0=ALU.mult))
                    pb.bar()
                    if sw > 0:
                        pb.dg(lambda e, off=off, lo=lo, sw=sw, w=w: e.dma_start(
                            out=bb[:, off + lo - 3:off + lo - 3 + sw], in_=ab2[:16, lo:w]))
            pb.bar()
        pb.emit(nc)
    return nc


def build_B():
    nc = bass.Bass("TRN2", target_bir_lowering=False)
    qb = nc.dram_tensor("qb", [4, 128, TB], F32, kind="ExternalInput").ap()
    kb = nc.dram_tensor("kb", [4, 128, TB], F32, kind="ExternalInput").ap()
    vb = nc.dram_tensor("vb", [4, 128, TB], F32, kind="ExternalInput").ap()
    gb = nc.dram_tensor("gb", [4, 128, 64], F32, kind="ExternalInput").ap()
    bbeta = nc.dram_tensor("bbeta", [4, 128, 64], F32, kind="ExternalInput").ap()
    mui = nc.dram_tensor("mui", [128, 128], F32, kind="ExternalInput").ap()
    mls = nc.dram_tensor("mls", [128, 128], F32, kind="ExternalInput").ap()
    idn = nc.dram_tensor("idn", [128, 128], F32, kind="ExternalInput").ap()
    of = nc.dram_tensor("of", [4, 128, TB], F32, kind="ExternalOutput").ap()

    NI4 = 4
    NCH = B_CHUNKS
    pb = PB()
    with ExitStack() as st:
        e_ = st.enter_context
        sb = lambda n, f: e_(nc.sbuf_tensor(n, [128, f], F32))
        sb4 = lambda n: [e_(nc.sbuf_tensor(n + str(i), [128, 128], F32)) for i in range(NI4)]
        MUi = sb("MUi", 128)
        MLs = sb("MLs", 128)
        IDN = sb("IDN", 128)
        ones_col = sb("onesc", 1)
        ones_row = e_(nc.sbuf_tensor("onesr", [1, 128], F32))
        ggs = sb("ggs", 4 * 64)
        bbs = sb("bbs", 4 * 64)
        cum = sb("cum", 4 * 64)
        totr = e_(nc.sbuf_tensor("totr", [1, 4 * 64], F32))
        tb_ = sb("tb_", 4 * 64)
        gtb = sb("gtb", 4 * 64)
        ksc = sb("ksc", 4 * 64)
        gam = sb("gam", 4 * 64)
        bga = sb("bga", 4 * 64)
        Ssb = sb4("Ssb")
        Qf = [[sb4("Qf%d" % p)[i] for i in range(1)][0] for p in range(0)]
        QfB = [sb("QfA", 4 * 128), sb("QfBp", 4 * 128)]
        KfB = [sb("KfA", 4 * 128), sb("KfBp", 4 * 128)]
        VfB = [sb("VfA", 4 * 128), sb("VfBp", 4 * 128)]
        Kt = sb("Kt", 4 * 128)
        Vt = sb("Vt", 4 * 128)
        crow = e_(nc.sbuf_tensor("crow", [1, 4 * 128], F32))
        edarg = sb("edarg", 4 * 128)
        edtarg = sb("edtarg", 4 * 128)
        Ed = sb("Ed", 4 * 128)
        EdT = sb("EdT", 4 * 128)
        ATm = sb("ATm", 4 * 128)
        Dm = sb("Dm", 4 * 128)
        Dt = sb("Dt", 4 * 128)
        Pc = sb("Pc", 4 * 128)
        Pts = [sb("Pt%d" % j, 4 * 128) for j in range(6)]
        usb = sb("usb", 4 * 128)
        t1 = sb("t1", 4 * 128)
        qs0g = sb("qs0g", 4 * 128)
        osb = sb("osb", 4 * 128)
        Kh = sb("Kh", 4 * 128)
        cb = e_(nc.sbuf_tensor("cbB", [128, 1], F32))
        pA = e_(nc.psum_tensor("pA", [128, 512], F32))
        pB = e_(nc.psum_tensor("pB", [128, 512], F32))
        pC = e_(nc.psum_tensor("pC", [128, 512], F32))
        pD = e_(nc.psum_tensor("pD", [128, 512], F32))
        pE = e_(nc.psum_tensor("pE", [128, 512], F32))
        pR = e_(nc.psum_tensor("pR", [1, 512], F32))

        BL = lambda ap, i: ap[:, i * 128:(i + 1) * 128]
        B64 = lambda ap, i: ap[:, i * 64:(i + 1) * 64]
        CC = lambda ap, i, ci: ap[:, i * 64 + ci:i * 64 + ci + 1]

        pb.dl(lambda e: e.dma_start(out=MUi[:, :], in_=mui[:, :]))
        pb.dl(lambda e: e.dma_start(out=MLs[:, :], in_=mls[:, :]))
        pb.dl(lambda e: e.dma_start(out=IDN[:, :], in_=idn[:, :]))
        for i in range(NI4):
            pb.dl(lambda e, i=i: e.dma_start(out=B64(ggs, i), in_=gb[i, :, :]))
            pb.dl(lambda e, i=i: e.dma_start(out=B64(bbs, i), in_=bbeta[i, :, :]))
        pb.v(lambda e: e.memset(ones_col[:, :], 1.0))
        pb.v(lambda e: e.memset(ones_row[:1, :], 1.0))
        for i in range(NI4):
            pb.v(lambda e, i=i: e.memset(Ssb[i][:, :], 0.0))
        pb.bar()
        pb.t(lambda e: e.matmul(pA[:, :256], MUi[:, :], ggs[:, :], start=True, stop=True))
        pb.t(lambda e: e.matmul(pR[:1, :256], ones_col[:, 0:1], ggs[:, :], start=True, stop=True))
        pb.bar()
        pb.v(lambda e: e.tensor_copy(out=cum[:, :], in_=pA[:, :256]))
        pb.v(lambda e: e.tensor_copy(out=totr[:1, :], in_=pR[:1, :256]))
        pb.bar()
        pb.t(lambda e: e.matmul(pB[:, :256], ones_row[:1, :], totr[:1, :], start=True, stop=True))
        pb.bar()
        pb.v(lambda e: e.tensor_copy(out=tb_[:, :], in_=pB[:, :256]))
        pb.bar()
        pb.s(lambda e: e.activation(out=gtb[:, :], in_=tb_[:, :], func=AF.Exp))
        pb.v(lambda e: e.tensor_sub(ksc[:, :], tb_[:, :], cum[:, :]))
        pb.bar()
        pb.s(lambda e: e.activation(out=gam[:, :], in_=cum[:, :], func=AF.Exp))
        pb.s(lambda e: e.activation(out=ksc[:, :], in_=ksc[:, :], func=AF.Exp))
        pb.bar()
        pb.v(lambda e: e.tensor_mul(bga[:, :], bbs[:, :], gam[:, :]))
        pb.bar()
        for i in range(NI4):
            pb.dl(lambda e, i=i: e.dma_start(out=BL(QfB[0], i), in_=qb[i, :, 0:128]))
            pb.dl(lambda e, i=i: e.dma_start(out=BL(KfB[0], i), in_=kb[i, :, 0:128]))
            pb.dl(lambda e, i=i: e.dma_start(out=BL(VfB[0], i), in_=vb[i, :, 0:128]))
        pb.bar()

        for ci in range(NCH):
            par = ci % 2
            Qc, Kc, Vc = QfB[par], KfB[par], VfB[par]
            if ci < NCH - 1:
                for i in range(NI4):
                    pb.dl(lambda e, i=i, ci=ci, par=par: e.dma_start(
                        out=BL(QfB[1 - par], i), in_=qb[i, :, (ci + 1) * 128:(ci + 2) * 128]))
                    pb.dl(lambda e, i=i, ci=ci, par=par: e.dma_start(
                        out=BL(KfB[1 - par], i), in_=kb[i, :, (ci + 1) * 128:(ci + 2) * 128]))
                    pb.dl(lambda e, i=i, ci=ci, par=par: e.dma_start(
                        out=BL(VfB[1 - par], i), in_=vb[i, :, (ci + 1) * 128:(ci + 2) * 128]))
            for i in range(NI4):
                pb.t(lambda e, i=i, Kc=Kc: e.transpose(BL(pA, i), BL(Kc, i), IDN[:, :]))
                pb.t(lambda e, i=i, Vc=Vc: e.transpose(BL(pB, i), BL(Vc, i), IDN[:, :]))
                pb.t(lambda e, i=i, ci=ci: e.transpose(pR[:1, i * 128:i * 128 + 128],
                                                       CC(cum, i, ci), IDN[:, :]))
                pb.t(lambda e, i=i, Kc=Kc, Qc=Qc: e.matmul(BL(pD, i), BL(Kc, i), BL(Qc, i), start=True, stop=True))
                pb.t(lambda e, i=i, Kc=Kc: e.matmul(BL(pE, i), BL(Kc, i), BL(Kc, i), start=True, stop=True))
            pb.bar()
            pb.v(lambda e: e.tensor_copy(out=Kt[:, :], in_=pA[:, :]))
            pb.v(lambda e: e.tensor_copy(out=Vt[:, :], in_=pB[:, :]))
            pb.v(lambda e: e.tensor_copy(out=crow[:1, :], in_=pR[:1, :]))
            pb.bar()
            pb.t(lambda e: e.matmul(pA[:, :], ones_row[:1, :], crow[:1, :], start=True, stop=True))
            for i in range(NI4):
                pb.t(lambda e, i=i, Kc=Kc: e.matmul(BL(pB, i), BL(Kc, i), Ssb[i][:, :], start=True, stop=True))
                pb.t(lambda e, i=i, Qc=Qc: e.matmul(BL(pC, i), BL(Qc, i), Ssb[i][:, :], start=True, stop=True))
            pb.bar()
            for i in range(NI4):
                pb.v(lambda e, i=i, ci=ci: e.tensor_scalar(
                    out=BL(edarg, i), in0=BL(pA, i), scalar1=-1.0,
                    scalar2=CC(cum, i, ci), op0=ALU.mult, op1=ALU.add))
                pb.v(lambda e, i=i, ci=ci: e.tensor_scalar(
                    out=BL(edtarg, i), in0=BL(pA, i), scalar1=CC(cum, i, ci),
                    scalar2=None, op0=ALU.subtract))
                pb.v(lambda e, i=i, ci=ci: e.tensor_scalar(
                    out=BL(t1, i), in0=BL(pB, i), scalar1=CC(bga, i, ci),
                    scalar2=None, op0=ALU.mult))
                pb.v(lambda e, i=i, ci=ci: e.tensor_scalar(
                    out=BL(qs0g, i), in0=BL(pC, i), scalar1=CC(gam, i, ci),
                    scalar2=None, op0=ALU.mult))
            pb.bar()
            pb.v(lambda e: e.tensor_scalar_min(out=edarg[:, :], in0=edarg[:, :], scalar1=0.0))
            pb.v(lambda e: e.tensor_scalar_min(out=edtarg[:, :], in0=edtarg[:, :], scalar1=0.0))
            pb.bar()
            pb.s(lambda e: e.activation(out=Ed[:, :], in_=edarg[:, :], func=AF.Exp))
            pb.s(lambda e: e.activation(out=EdT[:, :], in_=edtarg[:, :], func=AF.Exp))
            for i in range(NI4):
                pb.v(lambda e, i=i, ci=ci: e.scalar_tensor_tensor(
                    out=BL(usb, i), in0=BL(Vt, i), scalar=CC(bbs, i, ci),
                    in1=BL(t1, i), op0=ALU.mult, op1=ALU.subtract))
                pb.v(lambda e, i=i, ci=ci: e.tensor_scalar(
                    out=BL(Kh, i), in0=BL(Kt, i), scalar1=CC(ksc, i, ci),
                    scalar2=None, op0=ALU.mult))
            pb.bar()
            for i in range(NI4):
                pb.v(lambda e, i=i: e.tensor_mul(BL(Ed, i), BL(Ed, i), MLs[:, :]))
                pb.v(lambda e, i=i: e.tensor_mul(BL(EdT, i), BL(EdT, i), MUi[:, :]))
            pb.bar()
            for i in range(NI4):
                pb.v(lambda e, i=i: e.tensor_mul(BL(ATm, i), BL(EdT, i), BL(pD, i)))
                pb.v(lambda e, i=i, ci=ci: e.scalar_tensor_tensor(
                    out=BL(Dm, i), in0=BL(pE, i), scalar=CC(bbs, i, ci),
                    in1=BL(Ed, i), op0=ALU.mult, op1=ALU.mult))
            pb.bar()
            for i in range(NI4):
                pb.t(lambda e, i=i: e.transpose(BL(pA, i), BL(Dm, i), IDN[:, :]))
            pb.bar()
            pb.v(lambda e: e.tensor_copy(out=Dt[:, :], in_=pA[:, :]))
            pb.bar()
            srcT, src = Dt, Dm
            for j in range(6):
                for i in range(NI4):
                    pb.t(lambda e, i=i, sT=srcT, s_=src: e.matmul(BL(pA, i), BL(sT, i), BL(s_, i), start=True, stop=True))
                    pb.t(lambda e, i=i, sT=srcT, s_=src: e.matmul(BL(pB, i), BL(s_, i), BL(sT, i), start=True, stop=True))
                pb.bar()
                pb.v(lambda e: e.tensor_copy(out=Pc[:, :], in_=pA[:, :]))
                pb.v(lambda e, j=j: e.tensor_copy(out=Pts[j][:, :], in_=pB[:, :]))
                pb.bar()
                srcT, src = Pts[j], Pc
            for j in [5, 4, 3, 2, 1, 0]:
                for i in range(NI4):
                    pb.t(lambda e, i=i, j=j: e.matmul(BL(pA, i), BL(Pts[j], i), BL(usb, i), start=True, stop=True))
                pb.bar()
                pb.v(lambda e: e.tensor_add(usb[:, :], usb[:, :], pA[:, :]))
                pb.bar()
            for i in range(NI4):
                pb.t(lambda e, i=i: e.matmul(BL(pA, i), BL(Dt, i), BL(usb, i), start=True, stop=True))
            pb.bar()
            pb.v(lambda e: e.tensor_sub(usb[:, :], usb[:, :], pA[:, :]))
            pb.bar()
            for i in range(NI4):
                pb.t(lambda e, i=i: e.matmul(BL(pA, i), BL(ATm, i), BL(usb, i), start=True, stop=True))
                pb.t(lambda e, i=i: e.matmul(BL(pB, i), BL(Kh, i), BL(usb, i), start=True, stop=True))
            pb.bar()
            pb.v(lambda e: e.tensor_add(osb[:, :], qs0g[:, :], pA[:, :]))
            for i in range(NI4):
                pb.v(lambda e, i=i, ci=ci: e.scalar_tensor_tensor(
                    out=Ssb[i][:, :], in0=Ssb[i][:, :], scalar=CC(gtb, i, ci),
                    in1=BL(pB, i), op0=ALU.mult, op1=ALU.add))
            pb.bar()
            for i in range(NI4):
                pb.t(lambda e, i=i: e.transpose(BL(pC, i), BL(osb, i), IDN[:, :]))
            pb.bar()
            pb.v(lambda e: e.tensor_copy(out=t1[:, :], in_=pC[:, :]))
            pb.bar()
            for i in range(NI4):
                pb.dg(lambda e, i=i, ci=ci: e.dma_start(
                    out=of[i, :, ci * 128:(ci + 1) * 128], in_=BL(t1, i)))
        pb.bar()
        pb.emit(nc)
    return nc


def build_C1():
    nc = bass.Bass("TRN2", target_bir_lowering=False)
    xsh = nc.dram_tensor("xsh", [D, TPC], F32, kind="ExternalInput").ap()
    osh = nc.dram_tensor("osh", [D, TPC], F32, kind="ExternalInput").ap()
    gsh = nc.dram_tensor("gsh", [D, TPC], F32, kind="ExternalInput").ap()
    onw = nc.dram_tensor("onw", [128], F32, kind="ExternalInput").ap()
    wo = nc.dram_tensor("wo", [D, D], F32, kind="ExternalInput").ap()
    wra = nc.dram_tensor("wra", [D, 8], F32, kind="ExternalInput").ap()
    bra = nc.dram_tensor("bra", [8], F32, kind="ExternalInput").ap()
    ffa = nc.dram_tensor("ffa", [8, D, 64], F32, kind="ExternalInput").ap()
    ffb = nc.dram_tensor("ffb", [8, 64, D], F32, kind="ExternalInput").ap()
    idn = nc.dram_tensor("idn", [128, 128], F32, kind="ExternalInput").ap()
    x2 = nc.dram_tensor("x2", [D, TPC], F32, kind="ExternalOutput").ap()

    pb = PB()
    with ExitStack() as st:
        e_ = st.enter_context
        xg = [e_(nc.sbuf_tensor("xsb%d" % d, [128, 512], F32)) for d in range(16)]
        on = [e_(nc.sbuf_tensor("on%d" % h, [128, 512], F32)) for h in range(16)]
        xln = [e_(nc.sbuf_tensor("xl%d" % d, [128, 512], F32)) for d in range(16)]
        hprime = [e_(nc.sbuf_tensor("hp%d" % ee, [64, 512], F32)) for ee in range(8)]
        ot = e_(nc.sbuf_tensor("ot", [128, 512], F32))
        gt = e_(nc.sbuf_tensor("gt", [128, 512], F32))
        sq = e_(nc.sbuf_tensor("sq", [128, 4 * 512], F32))
        scr = e_(nc.sbuf_tensor("scr", [1, 512], F32))
        mrow = e_(nc.sbuf_tensor("mrow", [1, 512], F32))
        rrow = e_(nc.sbuf_tensor("rrow", [1, 512], F32))
        wsb = e_(nc.sbuf_tensor("wsb", [128, 2048], F32))
        IDN = e_(nc.sbuf_tensor("IDN", [128, 128], F32))
        onwc = e_(nc.sbuf_tensor("onwc", [128, 1], F32))
        brc = e_(nc.sbuf_tensor("brc", [8, 1], F32))
        wrs = e_(nc.sbuf_tensor("wrs", [128, 16 * 8], F32))
        ffas = e_(nc.sbuf_tensor("ffas", [128, 16 * 64], F32))
        ffbt = e_(nc.sbuf_tensor("ffbt", [64, 8 * 128], F32))
        lg = e_(nc.sbuf_tensor("lg", [8, 512], F32))
        lgt = e_(nc.sbuf_tensor("lgt", [128, 8], F32))
        rmx = e_(nc.sbuf_tensor("rmx", [128, 1], F32))
        exr = e_(nc.sbuf_tensor("exr", [128, 8], F32))
        ssr = e_(nc.sbuf_tensor("ssr", [128, 1], F32))
        coefT = e_(nc.sbuf_tensor("coefT", [128, 8], F32))
        coef = e_(nc.sbuf_tensor("coef", [8, 512], F32))
        crow = e_(nc.sbuf_tensor("crw", [1, 512], F32))
        bcsb = e_(nc.sbuf_tensor("bcsb", [64, 512], F32))
        ones_col = e_(nc.sbuf_tensor("onesc", [128, 1], F32))
        ones_row = e_(nc.sbuf_tensor("onesr", [1, 128], F32))
        cb = e_(nc.sbuf_tensor("cb", [128, 4], F32))
        psA = e_(nc.psum_tensor("psA", [128, 512], F32))
        psB = e_(nc.psum_tensor("psB", [128, 512], F32))
        psC = e_(nc.psum_tensor("psC", [128, 512], F32))
        pr1 = e_(nc.psum_tensor("pr1", [1, 512], F32))
        pr2 = e_(nc.psum_tensor("pr2", [1, 512], F32))
        pt8 = e_(nc.psum_tensor("pt8", [128, 128], F32))

        pb.dl(lambda e: e.dma_start(out=IDN[:, :], in_=idn[:, :]))
        pb.dl(lambda e: e.dma_start(out=onwc[:, 0], in_=onw[:]))
        pb.dl(lambda e: e.dma_start(out=brc[:8, 0], in_=bra[:]))
        for dt in range(16):
            pb.dl(lambda e, dt=dt: e.dma_start(out=wrs[:, dt * 8:(dt + 1) * 8],
                                               in_=wra[dt * 128:(dt + 1) * 128, :]))
        pb.v(lambda e: e.memset(ones_col[:, :], 1.0))
        pb.v(lambda e: e.memset(ones_row[:1, :], 1.0))
        pb.v(lambda e: e.memset(cb[:, 0:1], 1e-5))
        pb.v(lambda e: e.memset(cb[:, 1:2], 1e-6))
        pb.v(lambda e: e.memset(cb[:, 2:3], 1.0))
        pb.v(lambda e: e.memset(cb[:, 3:4], -16.0))
        pb.bar()

        for gi in range(4):
            g0 = gi * 512
            for d in range(16):
                pb.dl(lambda e, d=d, g0=g0: e.dma_start(
                    out=xg[d][:, :], in_=xsh[d * 128:(d + 1) * 128, g0:g0 + 512]))
            pb.bar()
            for h in range(16):
                pb.dl(lambda e, h=h, g0=g0: e.dma_start(
                    out=ot[:, :], in_=osh[h * 128:(h + 1) * 128, g0:g0 + 512]))
                pb.dl(lambda e, h=h, g0=g0: e.dma_start(
                    out=gt[:, :], in_=gsh[h * 128:(h + 1) * 128, g0:g0 + 512]))
                pb.bar()
                emit_silu(pb, gt[:, :], gt[:, :], sq, 512)
                pb.v(lambda e, h=h: e.tensor_mul(on[h][:, :], ot[:, :], gt[:, :]))
                pb.bar()
                pb.s(lambda e, h=h: e.activation(out=sq[:, :512], in_=on[h][:, :], func=AF.Square))
                pb.bar()
                pb.t(lambda e: e.matmul(pr1[:1, :], ones_col[:, 0:1], sq[:, :512],
                                        start=True, stop=True))
                pb.bar()
                pb.s(lambda e: e.activation(out=mrow[:1, :], in_=pr1[:1, :], func=AF.Ln,
                                            scale=1.0 / 128.0, bias=cb[:1, 0:1]))
                pb.bar()
                pb.s(lambda e: e.activation(out=mrow[:1, :], in_=mrow[:1, :], func=AF.Exp, scale=-0.5))
                pb.bar()
                pb.t(lambda e: e.matmul(psA[:, :], ones_row[:1, :], mrow[:1, :],
                                        start=True, stop=True))
                pb.bar()
                pb.v(lambda e, h=h: e.scalar_tensor_tensor(
                    out=on[h][:, :], in0=on[h][:, :], scalar=onwc[:, 0:1],
                    in1=psA[:, :], op0=ALU.mult, op1=ALU.mult))
                pb.bar()
            for dm in range(16):
                for dt in range(16):
                    pb.dl(lambda e, dm=dm, dt=dt: e.dma_start(
                        out=wsb[:, dt * 128:(dt + 1) * 128],
                        in_=wo[dt * 128:(dt + 1) * 128, dm * 128:(dm + 1) * 128]))
                pb.bar()
                for dt in range(16):
                    pb.t(lambda e, dm=dm, dt=dt: e.matmul(
                        psB[:, :], wsb[:, dt * 128:(dt + 1) * 128], on[dt][:, :],
                        start=(dt == 0), stop=(dt == 15)))
                pb.bar()
                pb.v(lambda e, dm=dm: e.tensor_add(xg[dm][:, :], xg[dm][:, :], psB[:, :]))
                pb.bar()
            emit_ln(pb, xg, [(0, 512)], psA, psB, pr1, pr2, sq, mrow, rrow, scr,
                    xln, ones_col, ones_row, cb[:1, 0:1])
            for dt in range(16):
                pb.t(lambda e, dt=dt: e.matmul(
                    psA[:8, :], wrs[:, dt * 8:(dt + 1) * 8], xln[dt][:, :],
                    start=(dt == 0), stop=(dt == 15)))
            pb.bar()
            pb.v(lambda e: e.tensor_scalar(out=lg[:8, :], in0=psA[:8, :],
                                           scalar1=brc[:8, 0:1], scalar2=None, op0=ALU.add))
            pb.bar()
            for sub in range(4):
                pb.t(lambda e, sub=sub: e.transpose(pt8[:, :8], lg[:8, sub * 128:(sub + 1) * 128],
                                                    IDN[:8, :8]))
                pb.bar()
                pb.v(lambda e: e.tensor_copy(out=lgt[:, :8], in_=pt8[:, :8]))
                pb.bar()
                pb.v(lambda e: e.reduce_max(out=rmx[:, 0:1], in_=lgt[:, :8], axis=AX.X))
                pb.bar()
                pb.v(lambda e: e.tensor_scalar(out=rmx[:, 0:1], in0=rmx[:, 0:1],
                                               scalar1=-1.0, scalar2=None, op0=ALU.mult))
                pb.bar()
                pb.s(lambda e: e.activation(out=exr[:, :8], in_=lgt[:, :8], func=AF.Exp,
                                            bias=rmx[:, 0:1]))
                pb.bar()
                pb.v(lambda e: e.reduce_sum(out=ssr[:, 0:1], in_=exr[:, :8], axis=AX.X))
                pb.bar()
                pb.v(lambda e: e.reciprocal(out=ssr[:, 0:1], in_=ssr[:, 0:1]))
                pb.bar()
                pb.v(lambda e: e.tensor_scalar(out=ssr[:, 0:1], in0=ssr[:, 0:1],
                                               scalar1=1.0 / 64.0, scalar2=None, op0=ALU.mult))
                pb.bar()
                pb.v(lambda e: e.tensor_scalar(out=coefT[:, :8], in0=exr[:, :8],
                                               scalar1=1.0, scalar2=ssr[:, 0:1],
                                               op0=ALU.is_ge, op1=ALU.mult))
                pb.bar()
                pb.t(lambda e: e.transpose(pt8[:8, :], coefT[:, :8], IDN[:, :]))
                pb.bar()
                pb.v(lambda e, sub=sub: e.tensor_copy(out=coef[:8, sub * 128:(sub + 1) * 128],
                                                      in_=pt8[:8, :]))
                pb.bar()
            for ee in range(8):
                pb.dl(lambda e, ee=ee: e.dma_start(out=crow[:1, :], in_=coef[ee:ee + 1, :]))
                for dt in range(16):
                    pb.dl(lambda e, ee=ee, dt=dt: e.dma_start(
                        out=ffas[:, dt * 64:(dt + 1) * 64],
                        in_=ffa[ee, dt * 128:(dt + 1) * 128, :]))
                pb.bar()
                for dt in range(16):
                    pb.t(lambda e, dt=dt: e.matmul(
                        psA[:64, :], ffas[:, dt * 64:(dt + 1) * 64],
                        xln[dt][:, :], start=(dt == 0), stop=(dt == 15)))
                pb.t(lambda e: e.matmul(psB[:64, :], ones_row[:1, :64], crow[:1, :],
                                        start=True, stop=True))
                pb.bar()
                pb.v(lambda e: e.tensor_copy(out=bcsb[:64, :], in_=psB[:64, :]))
                pb.bar()
                pb.v(lambda e, ee=ee: e.tensor_mul(hprime[ee][:64, :], psA[:64, :], bcsb[:64, :]))
                pb.bar()
            for dm in range(16):
                for ee in range(8):
                    pb.dl(lambda e, ee=ee, dm=dm: e.dma_start(
                        out=ffbt[:64, ee * 128:(ee + 1) * 128],
                        in_=ffb[ee, :, dm * 128:(dm + 1) * 128]))
                pb.bar()
                for ee in range(8):
                    pb.t(lambda e, ee=ee: e.matmul(
                        psC[:, :], ffbt[:64, ee * 128:(ee + 1) * 128],
                        hprime[ee][:64, :], start=(ee == 0), stop=(ee == 7)))
                pb.bar()
                pb.v(lambda e, dm=dm: e.tensor_add(xg[dm][:, :], xg[dm][:, :], psC[:, :]))
                pb.bar()
                pb.dg(lambda e, dm=dm, g0=g0: e.dma_start(
                    out=x2[dm * 128:(dm + 1) * 128, g0:g0 + 512], in_=xg[dm][:, :]))
            pb.bar()
        pb.emit(nc)
    return nc


def build_C2():
    nc = bass.Bass("TRN2", target_bir_lowering=False)
    x2 = nc.dram_tensor("x2", [D, TPC], F32, kind="ExternalInput").ap()
    wq = nc.dram_tensor("wq", [NH, D, 128], F32, kind="ExternalInput").ap()
    wk = nc.dram_tensor("wk", [NH, D, 128], F32, kind="ExternalInput").ap()
    wv = nc.dram_tensor("wv", [NH, D, 128], F32, kind="ExternalInput").ap()
    q1 = nc.dram_tensor("q1", [NH, 128, TPC], BF16, kind="ExternalOutput").ap()
    k1 = nc.dram_tensor("k1", [NH, 128, TPC], BF16, kind="ExternalOutput").ap()
    v1 = nc.dram_tensor("v1", [NH, 128, TPC], BF16, kind="ExternalOutput").ap()

    pb = PB()
    with ExitStack() as st:
        e_ = st.enter_context
        xg = [e_(nc.sbuf_tensor("xsb%d" % d, [128, 512], F32)) for d in range(16)]
        xlb = [e_(nc.sbuf_tensor("xl%d" % d, [128, TPC], BF16)) for d in range(16)]
        sq = e_(nc.sbuf_tensor("sq", [128, 4 * 512], F32))
        scr = e_(nc.sbuf_tensor("scr", [1, 512], F32))
        mrow = e_(nc.sbuf_tensor("mrow", [1, 512], F32))
        rrow = e_(nc.sbuf_tensor("rrow", [1, 512], F32))
        wsb = e_(nc.sbuf_tensor("wsb", [128, 2048], BF16))
        outb = e_(nc.sbuf_tensor("outb", [128, 512], BF16))
        ones_col = e_(nc.sbuf_tensor("onesc", [128, 1], F32))
        ones_row = e_(nc.sbuf_tensor("onesr", [1, 128], F32))
        cb = e_(nc.sbuf_tensor("cb", [128, 4], F32))
        psA = e_(nc.psum_tensor("psA", [128, 512], F32))
        psB = e_(nc.psum_tensor("psB", [128, 512], F32))
        pr1 = e_(nc.psum_tensor("pr1", [1, 512], F32))
        pr2 = e_(nc.psum_tensor("pr2", [1, 512], F32))

        pb.v(lambda e: e.memset(ones_col[:, :], 1.0))
        pb.v(lambda e: e.memset(ones_row[:1, :], 1.0))
        pb.v(lambda e: e.memset(cb[:, 0:1], 1e-5))
        pb.v(lambda e: e.memset(cb[:, 1:2], 1e-6))
        pb.v(lambda e: e.memset(cb[:, 2:3], 1.0))
        pb.v(lambda e: e.memset(cb[:, 3:4], -16.0))
        pb.bar()
        for gi in range(4):
            g0 = gi * 512
            for d in range(16):
                pb.dl(lambda e, d=d, g0=g0: e.dma_start(
                    out=xg[d][:, :], in_=x2[d * 128:(d + 1) * 128, g0:g0 + 512]))
            pb.bar()
            emit_ln(pb, xg, [(0, 512)], psA, psB, pr1, pr2, sq, mrow, rrow, scr,
                    [xx[:, g0:g0 + 512] for xx in xlb], ones_col, ones_row, cb[:1, 0:1])
        for h in range(NH):
            for (wap, dst, scale) in ((wq, q1, QS), (wk, k1, None), (wv, v1, None)):
                for dt in range(16):
                    pb.dg(lambda e, h=h, dt=dt, wap=wap: e.dma_start(
                        out=wsb[:, dt * 128:(dt + 1) * 128],
                        in_=wap[h, dt * 128:(dt + 1) * 128, :]))
                pb.bar()
                for gi in range(4):
                    g0 = gi * 512
                    for dt in range(16):
                        pb.t(lambda e, dt=dt, g0=g0: e.matmul(
                            psA[:, :], wsb[:, dt * 128:(dt + 1) * 128],
                            xlb[dt][:, g0:g0 + 512], start=(dt == 0), stop=(dt == 15)))
                    pb.bar()
                    if scale is None:
                        pb.v(lambda e: e.tensor_copy(out=outb[:, :], in_=psA[:, :]))
                    else:
                        pb.v(lambda e, scale=scale: e.tensor_scalar(
                            out=outb[:, :], in0=psA[:, :], scalar1=scale,
                            scalar2=None, op0=ALU.mult))
                    pb.bar()
                    pb.dg(lambda e, h=h, g0=g0, dst=dst: e.dma_start(
                        out=dst[h, :, g0:g0 + 512], in_=outb[:, :]))
                pb.bar()
        pb.emit(nc)
    return nc


def build_E():
    nc = bass.Bass("TRN2", target_bir_lowering=False)
    q1 = nc.dram_tensor("q1", [4, 128, TB], BF16, kind="ExternalInput").ap()
    k1 = nc.dram_tensor("k1", [4, 128, TB], BF16, kind="ExternalInput").ap()
    v1 = nc.dram_tensor("v1", [4, 128, TB], BF16, kind="ExternalInput").ap()
    mneg = nc.dram_tensor("mneg", [128, 128], F32, kind="ExternalInput").ap()
    idn = nc.dram_tensor("idn", [128, 128], F32, kind="ExternalInput").ap()
    zf = nc.dram_tensor("zf", [4, 128, TB], BF16, kind="ExternalOutput").ap()

    pb = PB()
    with ExitStack() as st:
        e_ = st.enter_context
        qs = [e_(nc.sbuf_tensor("qs%d" % u, [128, TB], BF16)) for u in range(2)]
        ks = [e_(nc.sbuf_tensor("ks%d" % u, [128, TB], BF16)) for u in range(2)]
        vs = [e_(nc.sbuf_tensor("vs%d" % u, [128, TB], BF16)) for u in range(2)]
        vt = [e_(nc.sbuf_tensor("vt%d" % u, [128, TB], BF16)) for u in range(2)]
        zfm = [e_(nc.sbuf_tensor("zfm%d" % u, [128, TB], BF16)) for u in range(2)]
        PT = [[e_(nc.sbuf_tensor("PT%d_%d" % (u, i), [128, 512], BF16)) for i in range(2)]
              for u in range(2)]
        MNb = e_(nc.sbuf_tensor("MNb", [128, 128], BF16))
        IDN = e_(nc.sbuf_tensor("IDN", [128, 128], F32))
        IDNb = e_(nc.sbuf_tensor("IDNb", [128, 128], BF16))
        onecb = e_(nc.sbuf_tensor("onecb", [128, 1], BF16))
        cb = e_(nc.sbuf_tensor("cb", [128, 4], F32))
        lacc = e_(nc.sbuf_tensor("lacc", [1, 256], F32))
        zacc = e_(nc.sbuf_tensor("zacc", [128, 256], F32))
        lcol = e_(nc.sbuf_tensor("lcol", [128, 2], F32))
        zsc = e_(nc.sbuf_tensor("zsc", [128, 256], F32))
        pS = [[e_(nc.psum_tensor("pS%d_%d" % (u, i), [128, 512], F32)) for i in range(2)]
              for u in range(2)]
        pZ = e_(nc.psum_tensor("pZ", [128, 256], F32))
        pL = e_(nc.psum_tensor("pL", [1, 256], F32))
        pT = e_(nc.psum_tensor("pT", [128, 256], F32))
        pTb = e_(nc.psum_tensor("pTb", [128, 512], BF16))

        pb.dl(lambda e: e.dma_start(out=IDN[:, :], in_=idn[:, :]))
        pb.dg(lambda e: e.dma_start(out=MNb[:, :], in_=mneg[:, :]))
        pb.dg(lambda e: e.dma_start(out=IDNb[:, :], in_=idn[:, :]))
        pb.v(lambda e: e.memset(onecb[:, :], 1.0))
        pb.v(lambda e: e.memset(cb[:, 3:4], -16.0))
        pb.bar()
        for pair in range(2):
            for u in range(2):
                gi = 2 * pair + u
                pb.dl(lambda e, gi=gi, u=u: e.dma_start(out=qs[u][:, :], in_=q1[gi, :, :]))
                pb.dl(lambda e, gi=gi, u=u: e.dma_start(out=ks[u][:, :], in_=k1[gi, :, :]))
                pb.dl(lambda e, gi=gi, u=u: e.dma_start(out=vs[u][:, :], in_=v1[gi, :, :]))
            pb.bar()
            for j0 in range(0, 64, 2):
                for u in range(2):
                    for dj in range(2):
                        pb.t(lambda e, u=u, j=j0 + dj, dj=dj: e.transpose(
                            pTb[:, (u * 2 + dj) * 128:(u * 2 + dj + 1) * 128],
                            vs[u][:, j * 128:(j + 1) * 128], IDNb[:, :]))
                pb.bar()
                for u in range(2):
                    for dj in range(2):
                        pb.s(lambda e, u=u, j=j0 + dj, dj=dj: e.copy(
                            out=vt[u][:, j * 128:(j + 1) * 128],
                            in_=pTb[:, (u * 2 + dj) * 128:(u * 2 + dj + 1) * 128]))
                pb.bar()
            for qb in range(64):
                nch = qb // 4 + 1
                nbtot = qb + 1

                def lpv(u, c):
                    jlo = 4 * c
                    jhi = min(4 * c + 3, qb)
                    nb = jhi - jlo + 1
                    par = c % 2
                    for bi in range(nb):
                        pb.t(lambda e, u=u, par=par, bi=bi, nb=nb: e.matmul(
                            pL[:1, u * 128:(u + 1) * 128], onecb[:, 0:1],
                            PT[u][par][:, bi * 128:(bi + 1) * 128],
                            start=(bi == 0), stop=(bi == nb - 1),
                            skip_group_check=True))
                    for bi in range(nb):
                        pb.t(lambda e, u=u, par=par, bi=bi, jlo=jlo, nb=nb: e.matmul(
                            pZ[:, u * 128:(u + 1) * 128],
                            PT[u][par][:, bi * 128:(bi + 1) * 128],
                            vt[u][:, (jlo + bi) * 128:(jlo + bi + 1) * 128],
                            start=(bi == 0), stop=(bi == nb - 1),
                            skip_group_check=True))

                def scores(u, c):
                    jlo = 4 * c
                    jhi = min(4 * c + 3, qb)
                    nb = jhi - jlo + 1
                    par = c % 2
                    for j in range(jlo, jhi + 1):
                        pb.t(lambda e, u=u, j=j, jlo=jlo, par=par, qb=qb: e.matmul(
                            pS[u][par][:, (j - jlo) * 128:(j - jlo + 1) * 128],
                            ks[u][:, j * 128:(j + 1) * 128], qs[u][:, qb * 128:(qb + 1) * 128],
                            start=True, stop=(j != qb), skip_group_check=True))
                    if jhi == qb:
                        pb.t(lambda e, u=u, par=par, nb=nb: e.matmul(
                            pS[u][par][:, (nb - 1) * 128:nb * 128], IDNb[:, :], MNb[:, :],
                            start=False, stop=True, skip_group_check=True))
                    return nb

                for c in range(nch):
                    for u in range(2):
                        scores(u, c)
                        if c > 0:
                            lpv(u, c - 1)
                    if c == 0:
                        pb.v(lambda e: e.memset(lacc[:1, :], 0.0))
                        pb.v(lambda e: e.memset(zacc[:, :], 0.0))
                    pb.bar()
                    for u in range(2):
                        nb = min(4 * c + 3, qb) - 4 * c + 1
                        pb.s(lambda e, u=u, par=c % 2, nb=nb: e.activation(
                            out=PT[u][par][:, :nb * 128], in_=pS[u][par][:, :nb * 128],
                            func=AF.Exp, bias=cb[:, 3:4]))
                    if c > 0:
                        pb.v(lambda e: e.tensor_add(lacc[:1, :], lacc[:1, :], pL[:1, :]))
                        pb.v(lambda e: e.tensor_add(zacc[:, :], zacc[:, :], pZ[:, :]))
                    pb.bar()
                for u in range(2):
                    lpv(u, nch - 1)
                pb.bar()
                pb.v(lambda e: e.tensor_add(lacc[:1, :], lacc[:1, :], pL[:1, :]))
                pb.v(lambda e: e.tensor_add(zacc[:, :], zacc[:, :], pZ[:, :]))
                pb.bar()
                for u in range(2):
                    pb.t(lambda e, u=u: e.transpose(pT[:, u:u + 1],
                                                    lacc[:1, u * 128:(u + 1) * 128], IDN[:1, :1]))
                pb.bar()
                pb.v(lambda e: e.tensor_copy(out=lcol[:, 0:2], in_=pT[:, 0:2]))
                pb.bar()
                pb.v(lambda e: e.reciprocal(out=lcol[:, 0:2], in_=lcol[:, 0:2]))
                pb.bar()
                for u in range(2):
                    pb.v(lambda e, u=u: e.tensor_scalar(
                        out=zsc[:, u * 128:(u + 1) * 128], in0=zacc[:, u * 128:(u + 1) * 128],
                        scalar1=lcol[:, u:u + 1], scalar2=None, op0=ALU.mult))
                pb.bar()
                for u in range(2):
                    pb.t(lambda e, u=u: e.transpose(pT[:, u * 128:(u + 1) * 128],
                                                    zsc[:, u * 128:(u + 1) * 128], IDN[:, :]))
                pb.bar()
                for u in range(2):
                    pb.v(lambda e, u=u, qb=qb: e.tensor_copy(
                        out=zfm[u][:, qb * 128:(qb + 1) * 128],
                        in_=pT[:, u * 128:(u + 1) * 128]))
                pb.bar()
            for u in range(2):
                gi = 2 * pair + u
                pb.dg(lambda e, gi=gi, u=u: e.dma_start(out=zf[gi, :, :], in_=zfm[u][:, :]))
            pb.bar()
        pb.emit(nc)
    return nc


def build_F():
    nc = bass.Bass("TRN2", target_bir_lowering=False)
    x2 = nc.dram_tensor("x2", [D, TPC], F32, kind="ExternalInput").ap()
    zsh = nc.dram_tensor("zsh", [NH, 128, TPC], BF16, kind="ExternalInput").ap()
    wot = nc.dram_tensor("wot", [NH, 128, D], F32, kind="ExternalInput").ap()
    x3 = nc.dram_tensor("x3", [D, TPC], F32, kind="ExternalOutput").ap()

    pb = PB()
    with ExitStack() as st:
        e_ = st.enter_context
        zs = [e_(nc.sbuf_tensor("z%d" % h, [128, TPC], BF16)) for h in range(NH)]
        ws = e_(nc.sbuf_tensor("ws", [128, NH * 128], BF16))
        xtile = e_(nc.sbuf_tensor("xtile", [128, 512], F32))
        psA = e_(nc.psum_tensor("psA", [128, 512], F32))

        for h in range(NH):
            pb.dl(lambda e, h=h: e.dma_start(out=zs[h][:, :], in_=zsh[h, :, :]))
        pb.bar()
        for dm in range(16):
            for h in range(NH):
                pb.dg(lambda e, h=h, dm=dm: e.dma_start(
                    out=ws[:, h * 128:(h + 1) * 128],
                    in_=wot[h, :, dm * 128:(dm + 1) * 128]))
            pb.bar()
            for gi in range(4):
                g0 = gi * 512
                pb.dl(lambda e, dm=dm, g0=g0: e.dma_start(
                    out=xtile[:, :], in_=x2[dm * 128:(dm + 1) * 128, g0:g0 + 512]))
                for h in range(NH):
                    pb.t(lambda e, h=h, g0=g0: e.matmul(
                        psA[:, :], ws[:, h * 128:(h + 1) * 128], zs[h][:, g0:g0 + 512],
                        start=(h == 0), stop=(h == NH - 1)))
                pb.bar()
                pb.v(lambda e: e.tensor_add(xtile[:, :], xtile[:, :], psA[:, :]))
                pb.bar()
                pb.dg(lambda e, dm=dm, g0=g0: e.dma_start(
                    out=x3[dm * 128:(dm + 1) * 128, g0:g0 + 512], in_=xtile[:, :]))
                pb.bar()
        pb.emit(nc)
    return nc


def build_G():
    nc = bass.Bass("TRN2", target_bir_lowering=False)
    x3a = nc.dram_tensor("x3a", [D, 2048], F32, kind="ExternalInput").ap()
    tidx = nc.dram_tensor("tidx", [4, 2048], I32, kind="ExternalInput").ap()
    tw = nc.dram_tensor("tw", [4, 2048], F32, kind="ExternalInput").ap()
    eids = nc.dram_tensor("eids", [4, 2], F32, kind="ExternalInput").ap()
    gwt = nc.dram_tensor("gwt", [2, D, 1408], F32, kind="ExternalInput").ap()
    uwt = nc.dram_tensor("uwt", [2, D, 1408], F32, kind="ExternalInput").ap()
    dwt = nc.dram_tensor("dwt", [2, 1408, D], F32, kind="ExternalInput").ap()
    part = nc.dram_tensor("part", [D, 2048], F32, kind="ExternalOutput").ap()

    NI = 11
    pb = PB()
    with ExitStack() as st:
        e_ = st.enter_context
        xg = [e_(nc.sbuf_tensor("xsb%d" % d, [128, 512], F32)) for d in range(16)]
        xlb = [e_(nc.sbuf_tensor("xl%d" % d, [128, 1024], BF16)) for d in range(16)]
        hp = [[e_(nc.sbuf_tensor("hp%d_%d" % (ee, o), [128, 1024], BF16))
               for o in range(NI)] for ee in range(2)]
        sq = e_(nc.sbuf_tensor("sq", [128, 4 * 512], F32))
        scr = e_(nc.sbuf_tensor("scr", [1, 512], F32))
        mrow = e_(nc.sbuf_tensor("mrow", [1, 512], F32))
        rrow = e_(nc.sbuf_tensor("rrow", [1, 512], F32))
        gws = e_(nc.sbuf_tensor("gws", [128, 2048], BF16))
        uws = e_(nc.sbuf_tensor("uws", [128, 2048], BF16))
        dws = e_(nc.sbuf_tensor("dws", [128, 2 * NI * 128], BF16))
        idxf = e_(nc.sbuf_tensor("idxf", [4, 2048], F32))
        idxi = e_(nc.sbuf_tensor("idxi", [4, 2048], I32))
        tws = e_(nc.sbuf_tensor("tws", [4, 2048], F32))
        eid = e_(nc.sbuf_tensor("eid", [4, 2], F32))
        wmk = e_(nc.sbuf_tensor("wmk", [4, 2048], F32))
        mrowe = [e_(nc.sbuf_tensor("mr%d" % ee, [1, 2048], F32)) for ee in range(2)]
        gact = e_(nc.sbuf_tensor("gact", [128, 512], F32))
        uact = e_(nc.sbuf_tensor("uact", [128, 512], F32))
        bcm = e_(nc.sbuf_tensor("bcm", [128, 512], F32))
        ones_col = e_(nc.sbuf_tensor("onesc", [128, 1], F32))
        ones_row = e_(nc.sbuf_tensor("onesr", [1, 128], F32))
        cb = e_(nc.sbuf_tensor("cb", [128, 4], F32))
        psA = e_(nc.psum_tensor("psA", [128, 512], F32))
        psB = e_(nc.psum_tensor("psB", [128, 512], F32))
        psC = e_(nc.psum_tensor("psC", [128, 512], F32))
        pr1 = e_(nc.psum_tensor("pr1", [1, 512], F32))
        pr2 = e_(nc.psum_tensor("pr2", [1, 512], F32))

        pb.dl(lambda e: e.dma_start(out=idxi[:, :], in_=tidx[:, :]))
        pb.dl(lambda e: e.dma_start(out=tws[:, :], in_=tw[:, :]))
        pb.dl(lambda e: e.dma_start(out=eid[:, :], in_=eids[:, :]))
        pb.v(lambda e: e.memset(ones_col[:, :], 1.0))
        pb.v(lambda e: e.memset(ones_row[:1, :], 1.0))
        pb.v(lambda e: e.memset(cb[:, 0:1], 1e-5))
        pb.v(lambda e: e.memset(cb[:, 1:2], 1e-6))
        pb.v(lambda e: e.memset(cb[:, 2:3], 1.0))
        pb.v(lambda e: e.memset(cb[:, 3:4], -16.0))
        pb.bar()
        pb.v(lambda e: e.tensor_copy(out=idxf[:, :], in_=idxi[:, :]))
        pb.bar()
        for ee in range(2):
            pb.v(lambda e, ee=ee: e.tensor_scalar(out=wmk[:, :], in0=idxf[:, :],
                                                  scalar1=eid[:4, ee:ee + 1], scalar2=None,
                                                  op0=ALU.is_equal))
            pb.bar()
            pb.v(lambda e: e.tensor_mul(wmk[:, :], wmk[:, :], tws[:, :]))
            pb.bar()
            for gi in range(4):
                pb.t(lambda e, gi=gi: e.matmul(pr1[:1, :], ones_col[:4, 0:1],
                                               wmk[:4, gi * 512:(gi + 1) * 512],
                                               start=True, stop=True))
                pb.bar()
                pb.v(lambda e, ee=ee, gi=gi: e.tensor_copy(
                    out=mrowe[ee][:1, gi * 512:(gi + 1) * 512], in_=pr1[:1, :]))
                pb.bar()
        for hf in range(2):
            h0 = hf * 1024
            for gi2 in range(2):
                g0 = h0 + gi2 * 512
                for d in range(16):
                    pb.dl(lambda e, d=d, g0=g0: e.dma_start(
                        out=xg[d][:, :], in_=x3a[d * 128:(d + 1) * 128, g0:g0 + 512]))
                pb.bar()
                emit_ln(pb, xg, [(0, 512)], psA, psB, pr1, pr2, sq, mrow, rrow,
                        scr, [xx[:, gi2 * 512:gi2 * 512 + 512] for xx in xlb],
                        ones_col, ones_row, cb[:1, 0:1])
            for ee in range(2):
                for o in range(NI):
                    for dt in range(16):
                        pb.dg(lambda e, ee=ee, o=o, dt=dt: e.dma_start(
                            out=gws[:, dt * 128:(dt + 1) * 128],
                            in_=gwt[ee, dt * 128:(dt + 1) * 128, o * 128:(o + 1) * 128]))
                        pb.dg(lambda e, ee=ee, o=o, dt=dt: e.dma_start(
                            out=uws[:, dt * 128:(dt + 1) * 128],
                            in_=uwt[ee, dt * 128:(dt + 1) * 128, o * 128:(o + 1) * 128]))
                    pb.bar()
                    for gi2 in range(2):
                        gl = gi2 * 512
                        for dt in range(16):
                            pb.t(lambda e, dt=dt, gl=gl: e.matmul(
                                psA[:, :], gws[:, dt * 128:(dt + 1) * 128],
                                xlb[dt][:, gl:gl + 512], start=(dt == 0), stop=(dt == 15)))
                        for dt in range(16):
                            pb.t(lambda e, dt=dt, gl=gl: e.matmul(
                                psB[:, :], uws[:, dt * 128:(dt + 1) * 128],
                                xlb[dt][:, gl:gl + 512], start=(dt == 0), stop=(dt == 15)))
                        pb.t(lambda e, ee=ee, h0=h0, gl=gl: e.matmul(
                            psC[:, :], ones_row[:1, :], mrowe[ee][:1, h0 + gl:h0 + gl + 512],
                            start=True, stop=True))
                        pb.bar()
                        pb.s(lambda e: e.activation(out=gact[:, :], in_=psA[:, :],
                                                    func=AF.Exp, scale=-1.0))
                        pb.bar()
                        pb.v(lambda e: e.tensor_scalar(out=gact[:, :], in0=gact[:, :],
                                                       scalar1=1.0, scalar2=None, op0=ALU.add))
                        pb.bar()
                        pb.v(lambda e: e.reciprocal(out=gact[:, :], in_=gact[:, :]))
                        pb.bar()
                        pb.v(lambda e: e.tensor_mul(gact[:, :], gact[:, :], psA[:, :]))
                        pb.bar()
                        pb.v(lambda e: e.tensor_mul(uact[:, :], gact[:, :], psB[:, :]))
                        pb.bar()
                        pb.v(lambda e: e.tensor_copy(out=bcm[:, :], in_=psC[:, :]))
                        pb.bar()
                        pb.v(lambda e, ee=ee, o=o, gl=gl: e.tensor_mul(
                            hp[ee][o][:, gl:gl + 512], uact[:, :], bcm[:, :]))
                        pb.bar()
            for dm in range(16):
                for ee in range(2):
                    for o in range(NI):
                        pb.dg(lambda e, ee=ee, o=o, dm=dm: e.dma_start(
                            out=dws[:, (ee * NI + o) * 128:(ee * NI + o + 1) * 128],
                            in_=dwt[ee, o * 128:(o + 1) * 128, dm * 128:(dm + 1) * 128]))
                pb.bar()
                for gi2 in range(2):
                    gl = gi2 * 512
                    for i2 in range(2 * NI):
                        ee, o = i2 // NI, i2 % NI
                        pb.t(lambda e, ee=ee, o=o, gl=gl, i2=i2: e.matmul(
                            psA[:, :], dws[:, (ee * NI + o) * 128:(ee * NI + o + 1) * 128],
                            hp[ee][o][:, gl:gl + 512], start=(i2 == 0), stop=(i2 == 2 * NI - 1)))
                    pb.bar()
                    pb.v(lambda e: e.tensor_copy(out=gact[:, :], in_=psA[:, :]))
                    pb.bar()
                    pb.dg(lambda e, dm=dm, h0=h0, gl=gl: e.dma_start(
                        out=part[dm * 128:(dm + 1) * 128, h0 + gl:h0 + gl + 512],
                        in_=gact[:, :]))
                pb.bar()
        pb.emit(nc)
    return nc


def build_H():
    nc = bass.Bass("TRN2", target_bir_lowering=False)
    parts = nc.dram_tensor("parts", [8, D, 256], F32, kind="ExternalInput").ap()
    x3s = nc.dram_tensor("x3s", [D, 256], F32, kind="ExternalInput").ap()
    outh = nc.dram_tensor("outh", [D, 256], F32, kind="ExternalOutput").ap()
    pb = PB()
    with ExitStack() as st:
        e_ = st.enter_context
        acc = e_(nc.sbuf_tensor("acc", [128, 256], F32))
        pt = e_(nc.sbuf_tensor("pt", [128, 256], F32))
        for d in range(16):
            pb.dl(lambda e, d=d: e.dma_start(out=acc[:, :], in_=x3s[d * 128:(d + 1) * 128, :]))
            pb.bar()
            for i in range(8):
                pb.dl(lambda e, i=i, d=d: e.dma_start(out=pt[:, :],
                                                      in_=parts[i, d * 128:(d + 1) * 128, :]))
                pb.bar()
                pb.v(lambda e: e.tensor_add(acc[:, :], acc[:, :], pt[:, :]))
                pb.bar()
            pb.dg(lambda e, d=d: e.dma_start(out=outh[d * 128:(d + 1) * 128, :], in_=acc[:, :]))
            pb.bar()
        pb.emit(nc)
    return nc


def kernel(L0_attn_hidden_states, L0_attn_W_qkv, L0_attn_W_ab, L0_attn_conv_w,
           L0_attn_A_log, L0_attn_dt_bias, L0_attn_o_norm_w, L0_attn_W_o,
           L0_ffn_A, L0_ffn_B, L0_ffn_Wr, L0_ffn_br,
           L1_attn_W_Q, L1_attn_W_K, L1_attn_W_V, L1_attn_W_O,
           L1_ffn_topk_idx, L1_ffn_topk_weight, L1_ffn_gate_weight,
           L1_ffn_up_weight, L1_ffn_down_weight):
    cores = list(range(NC))
    f32 = np.float32
    asc = np.ascontiguousarray
    mUi, mLs, mNeg = _np_masks()
    idn = np.eye(128, dtype=f32)

    x = np.asarray(L0_attn_hidden_states, f32)
    x_fm = asc(x.reshape(NTOK, D).T)

    ncA = build_A()
    imA = []
    wab = np.asarray(L0_attn_W_ab, f32)
    for c in cores:
        xh = np.zeros((D, TPC + 3), f32)
        xh[:, 3:] = x_fm[:, c * TPC:(c + 1) * TPC]
        if c % 4 != 0:
            xh[:, :3] = x_fm[:, c * TPC - 3:c * TPC]
        imA.append({"xh": asc(xh), "wqkv": np.asarray(L0_attn_W_qkv, f32),
                    "waa": asc(wab[:, 0:16]), "wbb": asc(wab[:, 16:32]),
                    "wgate": asc(wab[:, 32:]),
                    "convw": np.asarray(L0_attn_conv_w, f32),
                    "alog": np.asarray(L0_attn_A_log, f32),
                    "dtb": np.asarray(L0_attn_dt_bias, f32)})
    rA = run_bass_kernel_spmd(ncA, imA, core_ids=cores).results
    qf_g = np.concatenate([r["qf"] for r in rA], axis=2)
    kf_g = np.concatenate([r["kf"] for r in rA], axis=2)
    vf_g = np.concatenate([r["vf"] for r in rA], axis=2)
    gg_g = np.concatenate([r["gg"] for r in rA], axis=1)
    bb_g = np.concatenate([r["bb"] for r in rA], axis=1)
    gate_g = np.concatenate([r["gatef"] for r in rA], axis=1)

    insts = [(i // NH, i % NH) for i in range(32)]

    ncB = build_B()
    imB = []
    for c in cores:
        my = insts[4 * c:4 * c + 4]
        imB.append({
            "qb": asc(np.stack([qf_g[h, :, b * TB:(b + 1) * TB] for (b, h) in my])),
            "kb": asc(np.stack([kf_g[h, :, b * TB:(b + 1) * TB] for (b, h) in my])),
            "vb": asc(np.stack([vf_g[h, :, b * TB:(b + 1) * TB] for (b, h) in my])),
            "gb": asc(np.stack([gg_g[h, b * TB:(b + 1) * TB].reshape(64, 128).T for (b, h) in my])),
            "bbeta": asc(np.stack([bb_g[h, b * TB:(b + 1) * TB].reshape(64, 128).T for (b, h) in my])),
            "mui": mUi, "mls": mLs, "idn": idn})
    rB = run_bass_kernel_spmd(ncB, imB, core_ids=cores).results
    o_g = np.zeros((2, NH, 128, TB), f32)
    for c in cores:
        for k2, (b, h) in enumerate(insts[4 * c:4 * c + 4]):
            o_g[b, h] = rB[c]["of"][k2]

    ncC1 = build_C1()
    imC1 = []
    for c in cores:
        b = c // 4
        tl = (c * TPC) % TB
        o_b = o_g[b].reshape(D, TB)
        imC1.append({"xsh": asc(x_fm[:, c * TPC:(c + 1) * TPC]),
                     "osh": asc(o_b[:, tl:tl + TPC]),
                     "gsh": asc(gate_g[:, c * TPC:(c + 1) * TPC]),
                     "onw": np.asarray(L0_attn_o_norm_w, f32),
                     "wo": np.asarray(L0_attn_W_o, f32),
                     "wra": asc(np.asarray(L0_ffn_Wr, f32).T),
                     "bra": np.asarray(L0_ffn_br, f32),
                     "ffa": asc(np.asarray(L0_ffn_A, f32).transpose(0, 2, 1)),
                     "ffb": asc(np.asarray(L0_ffn_B, f32).transpose(0, 2, 1)),
                     "idn": idn})
    rC1 = run_bass_kernel_spmd(ncC1, imC1, core_ids=cores).results
    x2_sh = [r["x2"] for r in rC1]

    ncC2 = build_C2()
    wqh = asc(np.asarray(L1_attn_W_Q, f32).transpose(0, 2, 1))
    wkh = asc(np.asarray(L1_attn_W_K, f32).transpose(0, 2, 1))
    wvh = asc(np.asarray(L1_attn_W_V, f32).transpose(0, 2, 1))
    imC2 = [{"x2": asc(x2_sh[c]), "wq": wqh, "wk": wkh, "wv": wvh} for c in cores]
    rC2 = run_bass_kernel_spmd(ncC2, imC2, core_ids=cores).results
    q1_g = np.concatenate([r["q1"] for r in rC2], axis=2)
    k1_g = np.concatenate([r["k1"] for r in rC2], axis=2)
    v1_g = np.concatenate([r["v1"] for r in rC2], axis=2)

    ncE = build_E()
    imE = []
    for c in cores:
        my = insts[4 * c:4 * c + 4]
        imE.append({"q1": asc(np.stack([q1_g[h, :, b * TB:(b + 1) * TB] for (b, h) in my])),
                    "k1": asc(np.stack([k1_g[h, :, b * TB:(b + 1) * TB] for (b, h) in my])),
                    "v1": asc(np.stack([v1_g[h, :, b * TB:(b + 1) * TB] for (b, h) in my])),
                    "mneg": mNeg, "idn": idn})
    rE = run_bass_kernel_spmd(ncE, imE, core_ids=cores).results
    z_g = np.zeros((2, NH, 128, TB), rE[0]["zf"].dtype)
    for c in cores:
        for k2, (b, h) in enumerate(insts[4 * c:4 * c + 4]):
            z_g[b, h] = rE[c]["zf"][k2]

    ncF = build_F()
    woth = asc(np.asarray(L1_attn_W_O, f32).transpose(0, 2, 1))
    imF = []
    for c in cores:
        b = c // 4
        tl = (c * TPC) % TB
        imF.append({"x2": asc(x2_sh[c]), "zsh": asc(z_g[b][:, :, tl:tl + TPC]),
                    "wot": woth})
    rF = run_bass_kernel_spmd(ncF, imF, core_ids=cores).results
    x3_g = np.concatenate([r["x3"] for r in rF], axis=1)

    ncG = build_G()
    x3a = asc(x3_g[:, :2048])
    tidx = asc(np.asarray(L1_ffn_topk_idx).reshape(2048, 4).T.astype(np.int32))
    twr = asc(np.asarray(L1_ffn_topk_weight, f32).reshape(2048, 4).T)
    gwt = np.asarray(L1_ffn_gate_weight, f32).transpose(0, 2, 1)
    uwt = np.asarray(L1_ffn_up_weight, f32).transpose(0, 2, 1)
    dwt = np.asarray(L1_ffn_down_weight, f32).transpose(0, 2, 1)
    imG = []
    for c in cores:
        eids = np.zeros((4, 2), f32)
        eids[:, 0] = 2 * c
        eids[:, 1] = 2 * c + 1
        imG.append({"x3a": x3a, "tidx": tidx, "tw": twr, "eids": eids,
                    "gwt": asc(gwt[2 * c:2 * c + 2]),
                    "uwt": asc(uwt[2 * c:2 * c + 2]),
                    "dwt": asc(dwt[2 * c:2 * c + 2])})
    rG = run_bass_kernel_spmd(ncG, imG, core_ids=cores).results
    parts_g = np.stack([r["part"] for r in rG])

    ncH = build_H()
    imH = []
    for c in cores:
        imH.append({"parts": asc(parts_g[:, :, c * 256:(c + 1) * 256]),
                    "x3s": asc(x3a[:, c * 256:(c + 1) * 256])})
    rH = run_bass_kernel_spmd(ncH, imH, core_ids=cores).results
    outh = np.concatenate([r["outh"] for r in rH], axis=1)

    final = np.array(x3_g)
    final[:, :2048] = outh
    return asc(final.T).reshape(2, TB, D).astype(np.float32)


# version 03
# speedup vs baseline: 1.4533x; 1.1955x over previous
import numpy as np
from contextlib import ExitStack
import concourse.bass as bass
import concourse.mybir as mybir
from concourse.bass_utils import run_bass_kernel_spmd

F32 = mybir.dt.float32
BF16 = mybir.dt.bfloat16
I32 = mybir.dt.int32
AF = mybir.ActivationFunctionType
ALU = mybir.AluOpType
AX = mybir.AxisListType

NC = 8
B_CHUNKS = 64
D = 2048
TPC = 2048
NTOK = 16384
NH = 16
TB = 8192
QS = 0.08838834764831845

ENGS = ("tensor", "vector", "scalar", "sync", "gpsimd")


class PB:

    def __init__(self):
        self.items = {e: [] for e in ENGS}
        self.total = 0
        self.waited = {e: 0 for e in ENGS}
        self.cur = {e: None for e in ENGS}
        self.dmas = {e: 0 for e in ENGS}
        self.pdma = {e: False for e in ENGS}

    def op(self, eng, fn, dma=False):
        it = {"fn": fn, "waits": [], "incs": []}
        if self.waited[eng] < self.total:
            it["waits"].append(("P", self.total))
            self.waited[eng] = self.total
        self.items[eng].append(it)
        if dma:
            self.dmas[eng] += 1
            it["incs"].append(("D" + eng, 16))
            self.pdma[eng] = True
        else:
            self.cur[eng] = it
        return it

    def t(self, fn): self.op("tensor", fn)
    def v(self, fn): self.op("vector", fn)
    def s(self, fn): self.op("scalar", fn)
    def g(self, fn): self.op("gpsimd", fn)
    def dl(self, fn): self.op("sync", fn, dma=True)
    def dg(self, fn): self.op("gpsimd", fn, dma=True)

    def bar(self):
        n = 0
        for e in ENGS:
            if self.cur[e] is not None:
                self.cur[e]["incs"].append(("P", 1))
                n += 1
                self.cur[e] = None
            if self.pdma[e]:
                it = {"fn": None, "waits": [("D" + e, 16 * self.dmas[e])],
                      "incs": [("P", 1)]}
                self.items[e].append(it)
                n += 1
                self.pdma[e] = False
        self.total += n

    def emit(self, nc):
        self.bar()
        with ExitStack() as st:
            sems = {"P": st.enter_context(nc.semaphore("Pph"))}
            for e in ENGS:
                sems["D" + e] = st.enter_context(nc.semaphore("Dq_" + e))
            block = st.enter_context(nc.Block())
            for e in ENGS:
                items = self.items[e]
                if not items:
                    continue

                def body(eng, items=items):
                    for it in items:
                        for (s_, v_) in it["waits"]:
                            eng.wait_ge(sems[s_], v_)
                        if it["fn"] is None:
                            for (s_, n_) in it["incs"]:
                                eng.sem_inc(sems[s_], n_)
                        else:
                            inst = it["fn"](eng)
                            for (s_, n_) in it["incs"]:
                                inst = inst.then_inc(sems[s_], n_)
                getattr(block, e)(body)


def _np_masks():
    i = np.arange(128)
    mUi = (i[:, None] <= i[None, :]).astype(np.float32)
    mLs = (i[:, None] > i[None, :]).astype(np.float32)
    mNeg = np.where(i[:, None] > i[None, :], -200.0, 0.0).astype(np.float32)
    return mUi, mLs, mNeg


def emit_ln(pb, xin, groups, psA, psB, pr1, pr2, sq, mrow, rrow, scr, xout,
            ones_col, ones_row, ebias, d_total=2048.0):
    nd = len(xin)
    for (off, w) in groups:
        for r0 in range(0, nd, 4):
            for d in range(r0, min(r0 + 4, nd)):
                pb.s(lambda e, d=d, r0=r0, off=off, w=w: e.activation(
                    out=sq[:, (d - r0) * 512:(d - r0) * 512 + w],
                    in_=xin[d][:, off:off + w], func=AF.Square))
            pb.bar()
            for d in range(r0, min(r0 + 4, nd)):
                pb.t(lambda e, d=d, off=off, w=w: e.matmul(
                    pr1[:1, :w], ones_col[:, 0:1], xin[d][:, off:off + w],
                    start=(d == 0), stop=(d == nd - 1)))
            for d in range(r0, min(r0 + 4, nd)):
                pb.t(lambda e, d=d, r0=r0, w=w: e.matmul(
                    pr2[:1, :w], ones_col[:, 0:1], sq[:, (d - r0) * 512:(d - r0) * 512 + w],
                    start=(d == 0), stop=(d == nd - 1)))
            pb.bar()
        pb.v(lambda e, w=w: e.tensor_scalar(
            out=mrow[:1, :w], in0=pr1[:1, :w], scalar1=1.0 / d_total,
            scalar2=None, op0=ALU.mult))
        pb.bar()
        pb.v(lambda e, w=w: e.tensor_mul(scr[:1, :w], mrow[:1, :w], mrow[:1, :w]))
        pb.bar()
        pb.v(lambda e, w=w: e.scalar_tensor_tensor(
            out=rrow[:1, :w], in0=pr2[:1, :w], scalar=1.0 / d_total,
            in1=scr[:1, :w], op0=ALU.mult, op1=ALU.subtract))
        pb.bar()
        pb.s(lambda e, w=w: e.activation(out=rrow[:1, :w], in_=rrow[:1, :w],
                                         func=AF.Ln, bias=ebias))
        pb.bar()
        pb.s(lambda e, w=w: e.activation(out=rrow[:1, :w], in_=rrow[:1, :w],
                                         func=AF.Exp, scale=-0.5))
        pb.bar()
        pb.t(lambda e, w=w: e.matmul(psA[:, :w], ones_row[:1, :], mrow[:1, :w],
                                     start=True, stop=True))
        pb.t(lambda e, w=w: e.matmul(psB[:, :w], ones_row[:1, :], rrow[:1, :w],
                                     start=True, stop=True))
        pb.bar()
        for d in range(nd):
            pb.v(lambda e, d=d, off=off, w=w: e.tensor_sub(
                sq[:, 0:w], xin[d][:, off:off + w], psA[:, :w]))
            pb.bar()
            pb.v(lambda e, d=d, off=off, w=w: e.tensor_mul(
                xout[d][:, off:off + w], sq[:, 0:w], psB[:, :w]))
            pb.bar()


def emit_silu(pb, out_ap, in_ap, scratch, w):
    pb.s(lambda e: e.activation(out=scratch[:, :w], in_=in_ap, func=AF.Exp, scale=-1.0))
    pb.bar()
    pb.v(lambda e: e.tensor_scalar(out=scratch[:, :w], in0=scratch[:, :w],
                                   scalar1=1.0, scalar2=None, op0=ALU.add))
    pb.bar()
    pb.v(lambda e: e.reciprocal(out=scratch[:, :w], in_=scratch[:, :w]))
    pb.bar()
    pb.v(lambda e: e.tensor_mul(out_ap, in_ap, scratch[:, :w]))
    pb.bar()


def build_A():
    nc = bass.Bass("TRN2", target_bir_lowering=False)
    xh = nc.dram_tensor("xh", [D, TPC + 3], F32, kind="ExternalInput").ap()
    wqkv = nc.dram_tensor("wqkv", [D, 6144], F32, kind="ExternalInput").ap()
    waa = nc.dram_tensor("waa", [D, 16], F32, kind="ExternalInput").ap()
    wbb = nc.dram_tensor("wbb", [D, 16], F32, kind="ExternalInput").ap()
    wgate = nc.dram_tensor("wgate", [D, 2048], F32, kind="ExternalInput").ap()
    convw = nc.dram_tensor("convw", [6144, 4], F32, kind="ExternalInput").ap()
    alog = nc.dram_tensor("alog", [16], F32, kind="ExternalInput").ap()
    dtb = nc.dram_tensor("dtb", [16], F32, kind="ExternalInput").ap()
    qf = nc.dram_tensor("qf", [NH, 128, TPC], F32, kind="ExternalOutput").ap()
    kf = nc.dram_tensor("kf", [NH, 128, TPC], F32, kind="ExternalOutput").ap()
    vf = nc.dram_tensor("vf", [NH, 128, TPC], F32, kind="ExternalOutput").ap()
    gg = nc.dram_tensor("gg", [16, TPC], F32, kind="ExternalOutput").ap()
    bb = nc.dram_tensor("bb", [16, TPC], F32, kind="ExternalOutput").ap()
    gatef = nc.dram_tensor("gatef", [D, TPC], F32, kind="ExternalOutput").ap()

    W = TPC + 3
    groups = [(0, 512), (512, 512), (1024, 512), (1536, 512), (2048, 3)]
    pb = PB()
    with ExitStack() as st:
        e_ = st.enter_context
        xg = [e_(nc.sbuf_tensor("xg%d" % d, [128, 512], F32)) for d in range(16)]
        xlb = [e_(nc.sbuf_tensor("xlb%d" % d, [128, W], BF16)) for d in range(16)]
        sq = e_(nc.sbuf_tensor("sq", [128, 4 * 512], F32))
        scr = e_(nc.sbuf_tensor("scr", [1, 512], F32))
        mrow = e_(nc.sbuf_tensor("mrow", [1, 512], F32))
        rrow = e_(nc.sbuf_tensor("rrow", [1, 512], F32))
        srow = e_(nc.sbuf_tensor("srow", [1, 2048], F32))
        wsb = [e_(nc.sbuf_tensor("wsb%d" % i, [128, 2048], BF16)) for i in range(2)]
        qkvfull = e_(nc.sbuf_tensor("qkvfull", [128, W], BF16))
        convf = e_(nc.sbuf_tensor("convf", [128, W], BF16))
        esb = e_(nc.sbuf_tensor("esb", [128, W], F32))
        sln = e_(nc.sbuf_tensor("sln", [128, W], BF16))
        outt = e_(nc.sbuf_tensor("outt", [128, 2048], F32))
        cw = e_(nc.sbuf_tensor("cw", [128, 48 * 4], F32))
        av = e_(nc.sbuf_tensor("av", [16, 4], F32))
        ones_col = e_(nc.sbuf_tensor("onesc", [128, 1], F32))
        ones_colb = e_(nc.sbuf_tensor("onescb", [128, 1], BF16))
        ones_row = e_(nc.sbuf_tensor("onesr", [1, 128], F32))
        cb = e_(nc.sbuf_tensor("cb", [128, 4], F32))
        abg = e_(nc.sbuf_tensor("abg", [16, 2048], F32))
        ab2 = e_(nc.sbuf_tensor("ab2", [16, 2048], F32))
        pq = [e_(nc.psum_tensor("pq%d" % g, [128, 512], F32)) for g in range(5)]
        pr1 = e_(nc.psum_tensor("pr1", [1, 512], F32))
        pr2 = e_(nc.psum_tensor("pr2", [1, 512], F32))
        psA = pq[0]
        psB = pq[1]

        for o in range(48):
            pb.dl(lambda e, o=o: e.dma_start(out=cw[:, o * 4:o * 4 + 4],
                                             in_=convw[o * 128:(o + 1) * 128, :]))
        pb.dl(lambda e: e.dma_start(out=av[:, 0], in_=alog[:]))
        pb.dl(lambda e: e.dma_start(out=av[:, 1], in_=dtb[:]))
        pb.v(lambda e: e.memset(ones_col[:, :], 1.0))
        pb.v(lambda e: e.memset(ones_colb[:, :], 1.0))
        pb.v(lambda e: e.memset(ones_row[:1, :], 1.0))
        pb.v(lambda e: e.memset(cb[:, 0:1], 1e-5))
        pb.v(lambda e: e.memset(cb[:, 1:2], 1e-6))
        pb.v(lambda e: e.memset(cb[:, 2:3], 1.0))
        pb.bar()
        pb.s(lambda e: e.activation(out=av[:16, 2:3], in_=av[:16, 0:1], func=AF.Exp))
        pb.bar()
        pb.v(lambda e: e.tensor_scalar(out=av[:16, 2:3], in0=av[:16, 2:3],
                                       scalar1=-1.0, scalar2=None, op0=ALU.mult))
        pb.bar()
        for (off, w) in groups:
            for d in range(16):
                pb.dl(lambda e, d=d, off=off, w=w: e.dma_start(
                    out=xg[d][:, :w], in_=xh[d * 128:(d + 1) * 128, off:off + w]))
            pb.bar()
            emit_ln(pb, [xx[:, 0:w] for xx in xg], [(0, w)], pq[2], pq[3], pr1, pr2,
                    sq, mrow, rrow, scr, [xx[:, off:off + w] for xx in xlb],
                    ones_col, ones_row, cb[:1, 0:1])

        def stream_w(o, wap, ncols, buf):
            for dt in range(16):
                pb.dg(lambda e, dt=dt, o=o, wap=wap, ncols=ncols, buf=buf: e.dma_start(
                    out=wsb[buf][:, dt * ncols:(dt + 1) * ncols],
                    in_=wap[dt * 128:(dt + 1) * 128, o * ncols:(o + 1) * ncols]))

        stream_w(0, wqkv, 128, 0)
        pb.bar()
        for o in range(48):
            buf = o % 2
            if o < 47:
                stream_w(o + 1, wqkv, 128, 1 - buf)
            for gi, (off, w) in enumerate(groups):
                for dt in range(16):
                    pb.t(lambda e, dt=dt, off=off, w=w, gi=gi, buf=buf: e.matmul(
                        pq[gi][:, :w], wsb[buf][:, dt * 128:(dt + 1) * 128],
                        xlb[dt][:, off:off + w], start=(dt == 0), stop=(dt == 15)))
            pb.bar()
            for gi, (off, w) in enumerate(groups):
                pb.v(lambda e, gi=gi, off=off, w=w: e.tensor_copy(
                    out=qkvfull[:, off:off + w], in_=pq[gi][:, :w]))
            pb.bar()
            pb.v(lambda e, o=o: e.tensor_scalar(
                out=convf[:, 3:W], in0=qkvfull[:, 3:W],
                scalar1=cw[:, o * 4 + 3:o * 4 + 4], scalar2=None, op0=ALU.mult))
            pb.bar()
            for j in range(3):
                pb.v(lambda e, j=j, o=o: e.scalar_tensor_tensor(
                    out=convf[:, 3:W], in0=qkvfull[:, j:W - 3 + j],
                    scalar=cw[:, o * 4 + j:o * 4 + j + 1],
                    in1=convf[:, 3:W], op0=ALU.mult, op1=ALU.add))
                pb.bar()
            pb.s(lambda e: e.activation(out=esb[:, 3:W], in_=convf[:, 3:W],
                                        func=AF.Exp, scale=-1.0))
            pb.bar()
            pb.v(lambda e: e.tensor_scalar(out=esb[:, 3:W], in0=esb[:, 3:W],
                                           scalar1=1.0, scalar2=None, op0=ALU.add))
            pb.bar()
            pb.v(lambda e: e.reciprocal(out=esb[:, 3:W], in_=esb[:, 3:W]))
            pb.bar()
            pb.v(lambda e: e.tensor_mul(sln[:, 3:W], convf[:, 3:W], esb[:, 3:W]))
            pb.bar()
            if o < 16:
                h, dst, scale = o, qf, QS
            elif o < 32:
                h, dst, scale = o - 16, kf, 1.0
            else:
                h, dst, scale = o - 32, vf, None
            if scale is None:
                pb.v(lambda e: e.tensor_copy(out=outt[:, :], in_=sln[:, 3:W]))
                pb.bar()
            else:
                pb.s(lambda e: e.activation(out=esb[:, 3:W], in_=sln[:, 3:W],
                                            func=AF.Square))
                pb.bar()
                for g4 in range(4):
                    pb.t(lambda e, g4=g4: e.matmul(
                        pq[g4][:1, :], ones_col[:, 0:1], esb[:, 3 + g4 * 512:3 + (g4 + 1) * 512],
                        start=True, stop=True))
                pb.bar()
                for g4 in range(4):
                    pb.v(lambda e, g4=g4: e.tensor_copy(out=srow[:1, g4 * 512:(g4 + 1) * 512],
                                                        in_=pq[g4][:1, :]))
                pb.bar()
                pb.s(lambda e: e.activation(out=srow[:1, :], in_=srow[:1, :],
                                            func=AF.Ln, bias=cb[:1, 1:2]))
                pb.bar()
                pb.s(lambda e: e.activation(out=srow[:1, :], in_=srow[:1, :],
                                            func=AF.Exp, scale=-0.5))
                pb.bar()
                for g4 in range(4):
                    pb.t(lambda e, g4=g4: e.matmul(
                        pq[g4][:, :], ones_row[:1, :], srow[:1, g4 * 512:(g4 + 1) * 512],
                        start=True, stop=True))
                pb.bar()
                for g4 in range(4):
                    pb.v(lambda e, g4=g4, scale=scale: e.scalar_tensor_tensor(
                        out=outt[:, g4 * 512:(g4 + 1) * 512],
                        in0=sln[:, 3 + g4 * 512:3 + (g4 + 1) * 512], scalar=scale,
                        in1=pq[g4][:, :], op0=ALU.mult, op1=ALU.mult))
                pb.bar()
            pb.dg(lambda e, h=h, dst=dst: e.dma_start(out=dst[h, :, :], in_=outt[:, :]))
        pb.bar()
        stream_w(0, wgate, 128, 0)
        pb.bar()
        for o in range(16):
            buf = o % 2
            if o < 15:
                stream_w(o + 1, wgate, 128, 1 - buf)
            for gi, (off, w) in enumerate(groups[:4]):
                for dt in range(16):
                    pb.t(lambda e, dt=dt, off=off, w=w, gi=gi, buf=buf: e.matmul(
                        pq[gi][:, :w], wsb[buf][:, dt * 128:(dt + 1) * 128],
                        xlb[dt][:, 3 + off:3 + off + w], start=(dt == 0), stop=(dt == 15)))
            pb.bar()
            for gi in range(4):
                pb.v(lambda e, gi=gi: e.tensor_copy(
                    out=outt[:, gi * 512:(gi + 1) * 512], in_=pq[gi][:, :]))
            pb.bar()
            pb.dg(lambda e, o=o: e.dma_start(
                out=gatef[o * 128:(o + 1) * 128, :], in_=outt[:, :]))
        pb.bar()
        for which in range(2):
            wap = waa if which == 0 else wbb
            for dt in range(16):
                pb.dg(lambda e, dt=dt, wap=wap: e.dma_start(
                    out=wsb[0][:, dt * 16:(dt + 1) * 16],
                    in_=wap[dt * 128:(dt + 1) * 128, :]))
            pb.bar()
            for gi, (off, w) in enumerate(groups[:4]):
                for dt in range(16):
                    pb.t(lambda e, dt=dt, off=off, w=w, gi=gi: e.matmul(
                        pq[gi][:16, :w], wsb[0][:, dt * 16:(dt + 1) * 16],
                        xlb[dt][:, 3 + off:3 + off + w], start=(dt == 0), stop=(dt == 15)))
            pb.bar()
            if which == 0:
                for gi in range(4):
                    pb.s(lambda e, gi=gi: e.activation(
                        out=abg[:16, gi * 512:(gi + 1) * 512], in_=pq[gi][:16, :],
                        func=AF.Exp, bias=av[:16, 1:2]))
                pb.bar()
                pb.s(lambda e: e.activation(out=abg[:16, :], in_=abg[:16, :],
                                            func=AF.Ln, bias=cb[:16, 2:3]))
                pb.bar()
                pb.v(lambda e: e.tensor_scalar(
                    out=ab2[:16, :], in0=abg[:16, :], scalar1=av[:16, 2:3],
                    scalar2=None, op0=ALU.mult))
                pb.bar()
                pb.dg(lambda e: e.dma_start(out=gg[:, :], in_=ab2[:16, :]))
            else:
                for gi in range(4):
                    pb.s(lambda e, gi=gi: e.activation(
                        out=abg[:16, gi * 512:(gi + 1) * 512], in_=pq[gi][:16, :],
                        func=AF.Exp, scale=-1.0))
                pb.bar()
                pb.v(lambda e: e.tensor_scalar(out=abg[:16, :], in0=abg[:16, :],
                                               scalar1=1.0, scalar2=None, op0=ALU.add))
                pb.bar()
                pb.v(lambda e: e.reciprocal(out=abg[:16, :], in_=abg[:16, :]))
                pb.bar()
                pb.v(lambda e: e.tensor_scalar(out=ab2[:16, :], in0=abg[:16, :],
                                               scalar1=2.0, scalar2=None, op0=ALU.mult))
                pb.bar()
                pb.dg(lambda e: e.dma_start(out=bb[:, :], in_=ab2[:16, :]))
            pb.bar()
        pb.emit(nc)
    return nc


def build_B():
    nc = bass.Bass("TRN2", target_bir_lowering=False)
    qb = nc.dram_tensor("qb", [4, 128, TB], F32, kind="ExternalInput").ap()
    kb = nc.dram_tensor("kb", [4, 128, TB], F32, kind="ExternalInput").ap()
    vb = nc.dram_tensor("vb", [4, 128, TB], F32, kind="ExternalInput").ap()
    gb = nc.dram_tensor("gb", [4, 128, 64], F32, kind="ExternalInput").ap()
    bbeta = nc.dram_tensor("bbeta", [4, 128, 64], F32, kind="ExternalInput").ap()
    mui = nc.dram_tensor("mui", [128, 128], F32, kind="ExternalInput").ap()
    mls = nc.dram_tensor("mls", [128, 128], F32, kind="ExternalInput").ap()
    idn = nc.dram_tensor("idn", [128, 128], F32, kind="ExternalInput").ap()
    of = nc.dram_tensor("of", [4, 128, TB], F32, kind="ExternalOutput").ap()

    NI4 = 4
    NCH = B_CHUNKS
    pb = PB()
    with ExitStack() as st:
        e_ = st.enter_context
        sb = lambda n, f: e_(nc.sbuf_tensor(n, [128, f], F32))
        sb4 = lambda n: [e_(nc.sbuf_tensor(n + str(i), [128, 128], F32)) for i in range(NI4)]
        MUi = sb("MUi", 128)
        MLs = sb("MLs", 128)
        IDN = sb("IDN", 128)
        ones_col = sb("onesc", 1)
        ones_row = e_(nc.sbuf_tensor("onesr", [1, 128], F32))
        ggs = sb("ggs", 4 * 64)
        bbs = sb("bbs", 4 * 64)
        cum = sb("cum", 4 * 64)
        totr = e_(nc.sbuf_tensor("totr", [1, 4 * 64], F32))
        tb_ = sb("tb_", 4 * 64)
        gtb = sb("gtb", 4 * 64)
        ksc = sb("ksc", 4 * 64)
        gam = sb("gam", 4 * 64)
        bga = sb("bga", 4 * 64)
        Ssb = sb4("Ssb")
        Qf = [[sb4("Qf%d" % p)[i] for i in range(1)][0] for p in range(0)]
        QfB = [sb("QfA", 4 * 128), sb("QfBp", 4 * 128)]
        KfB = [sb("KfA", 4 * 128), sb("KfBp", 4 * 128)]
        VfB = [sb("VfA", 4 * 128), sb("VfBp", 4 * 128)]
        Kt = sb("Kt", 4 * 128)
        Vt = sb("Vt", 4 * 128)
        crow = e_(nc.sbuf_tensor("crow", [1, 4 * 128], F32))
        edarg = sb("edarg", 4 * 128)
        edtarg = sb("edtarg", 4 * 128)
        Ed = sb("Ed", 4 * 128)
        EdT = sb("EdT", 4 * 128)
        ATm = sb("ATm", 4 * 128)
        Dm = sb("Dm", 4 * 128)
        Dt = sb("Dt", 4 * 128)
        Pc = sb("Pc", 4 * 128)
        Pts = [sb("Pt%d" % j, 4 * 128) for j in range(6)]
        usb = sb("usb", 4 * 128)
        t1 = sb("t1", 4 * 128)
        qs0g = sb("qs0g", 4 * 128)
        osb = sb("osb", 4 * 128)
        Kh = sb("Kh", 4 * 128)
        cb = e_(nc.sbuf_tensor("cbB", [128, 1], F32))
        pA = e_(nc.psum_tensor("pA", [128, 512], F32))
        pB = e_(nc.psum_tensor("pB", [128, 512], F32))
        pC = e_(nc.psum_tensor("pC", [128, 512], F32))
        pD = e_(nc.psum_tensor("pD", [128, 512], F32))
        pE = e_(nc.psum_tensor("pE", [128, 512], F32))
        pR = e_(nc.psum_tensor("pR", [1, 512], F32))

        BL = lambda ap, i: ap[:, i * 128:(i + 1) * 128]
        B64 = lambda ap, i: ap[:, i * 64:(i + 1) * 64]
        CC = lambda ap, i, ci: ap[:, i * 64 + ci:i * 64 + ci + 1]

        pb.dl(lambda e: e.dma_start(out=MUi[:, :], in_=mui[:, :]))
        pb.dl(lambda e: e.dma_start(out=MLs[:, :], in_=mls[:, :]))
        pb.dl(lambda e: e.dma_start(out=IDN[:, :], in_=idn[:, :]))
        for i in range(NI4):
            pb.dl(lambda e, i=i: e.dma_start(out=B64(ggs, i), in_=gb[i, :, :]))
            pb.dl(lambda e, i=i: e.dma_start(out=B64(bbs, i), in_=bbeta[i, :, :]))
        pb.v(lambda e: e.memset(ones_col[:, :], 1.0))
        pb.v(lambda e: e.memset(ones_row[:1, :], 1.0))
        for i in range(NI4):
            pb.v(lambda e, i=i: e.memset(Ssb[i][:, :], 0.0))
        pb.bar()
        pb.t(lambda e: e.matmul(pA[:, :256], MUi[:, :], ggs[:, :], start=True, stop=True))
        pb.t(lambda e: e.matmul(pR[:1, :256], ones_col[:, 0:1], ggs[:, :], start=True, stop=True))
        pb.bar()
        pb.v(lambda e: e.tensor_copy(out=cum[:, :], in_=pA[:, :256]))
        pb.v(lambda e: e.tensor_copy(out=totr[:1, :], in_=pR[:1, :256]))
        pb.bar()
        pb.t(lambda e: e.matmul(pB[:, :256], ones_row[:1, :], totr[:1, :], start=True, stop=True))
        pb.bar()
        pb.v(lambda e: e.tensor_copy(out=tb_[:, :], in_=pB[:, :256]))
        pb.bar()
        pb.s(lambda e: e.activation(out=gtb[:, :], in_=tb_[:, :], func=AF.Exp))
        pb.v(lambda e: e.tensor_sub(ksc[:, :], tb_[:, :], cum[:, :]))
        pb.bar()
        pb.s(lambda e: e.activation(out=gam[:, :], in_=cum[:, :], func=AF.Exp))
        pb.s(lambda e: e.activation(out=ksc[:, :], in_=ksc[:, :], func=AF.Exp))
        pb.bar()
        pb.v(lambda e: e.tensor_mul(bga[:, :], bbs[:, :], gam[:, :]))
        pb.bar()
        for i in range(NI4):
            pb.dl(lambda e, i=i: e.dma_start(out=BL(QfB[0], i), in_=qb[i, :, 0:128]))
            pb.dl(lambda e, i=i: e.dma_start(out=BL(KfB[0], i), in_=kb[i, :, 0:128]))
            pb.dl(lambda e, i=i: e.dma_start(out=BL(VfB[0], i), in_=vb[i, :, 0:128]))
        pb.bar()

        for ci in range(NCH):
            par = ci % 2
            Qc, Kc, Vc = QfB[par], KfB[par], VfB[par]
            if ci < NCH - 1:
                for i in range(NI4):
                    pb.dl(lambda e, i=i, ci=ci, par=par: e.dma_start(
                        out=BL(QfB[1 - par], i), in_=qb[i, :, (ci + 1) * 128:(ci + 2) * 128]))
                    pb.dl(lambda e, i=i, ci=ci, par=par: e.dma_start(
                        out=BL(KfB[1 - par], i), in_=kb[i, :, (ci + 1) * 128:(ci + 2) * 128]))
                    pb.dl(lambda e, i=i, ci=ci, par=par: e.dma_start(
                        out=BL(VfB[1 - par], i), in_=vb[i, :, (ci + 1) * 128:(ci + 2) * 128]))
            for i in range(NI4):
                pb.t(lambda e, i=i, Kc=Kc: e.transpose(BL(pA, i), BL(Kc, i), IDN[:, :]))
                pb.t(lambda e, i=i, Vc=Vc: e.transpose(BL(pB, i), BL(Vc, i), IDN[:, :]))
                pb.t(lambda e, i=i, ci=ci: e.transpose(pR[:1, i * 128:i * 128 + 128],
                                                       CC(cum, i, ci), IDN[:, :]))
                pb.t(lambda e, i=i, Kc=Kc, Qc=Qc: e.matmul(BL(pD, i), BL(Kc, i), BL(Qc, i), start=True, stop=True))
                pb.t(lambda e, i=i, Kc=Kc: e.matmul(BL(pE, i), BL(Kc, i), BL(Kc, i), start=True, stop=True))
            pb.bar()
            pb.v(lambda e: e.tensor_copy(out=Kt[:, :], in_=pA[:, :]))
            pb.v(lambda e: e.tensor_copy(out=Vt[:, :], in_=pB[:, :]))
            pb.v(lambda e: e.tensor_copy(out=crow[:1, :], in_=pR[:1, :]))
            pb.bar()
            pb.t(lambda e: e.matmul(pA[:, :], ones_row[:1, :], crow[:1, :], start=True, stop=True))
            for i in range(NI4):
                pb.t(lambda e, i=i, Kc=Kc: e.matmul(BL(pB, i), BL(Kc, i), Ssb[i][:, :], start=True, stop=True))
                pb.t(lambda e, i=i, Qc=Qc: e.matmul(BL(pC, i), BL(Qc, i), Ssb[i][:, :], start=True, stop=True))
            pb.bar()
            for i in range(NI4):
                pb.v(lambda e, i=i, ci=ci: e.tensor_scalar(
                    out=BL(edarg, i), in0=BL(pA, i), scalar1=-1.0,
                    scalar2=CC(cum, i, ci), op0=ALU.mult, op1=ALU.add))
                pb.v(lambda e, i=i, ci=ci: e.tensor_scalar(
                    out=BL(edtarg, i), in0=BL(pA, i), scalar1=CC(cum, i, ci),
                    scalar2=None, op0=ALU.subtract))
                pb.v(lambda e, i=i, ci=ci: e.tensor_scalar(
                    out=BL(t1, i), in0=BL(pB, i), scalar1=CC(bga, i, ci),
                    scalar2=None, op0=ALU.mult))
                pb.v(lambda e, i=i, ci=ci: e.tensor_scalar(
                    out=BL(qs0g, i), in0=BL(pC, i), scalar1=CC(gam, i, ci),
                    scalar2=None, op0=ALU.mult))
            pb.bar()
            pb.v(lambda e: e.tensor_scalar_min(out=edarg[:, :], in0=edarg[:, :], scalar1=0.0))
            pb.v(lambda e: e.tensor_scalar_min(out=edtarg[:, :], in0=edtarg[:, :], scalar1=0.0))
            pb.bar()
            pb.s(lambda e: e.activation(out=Ed[:, :], in_=edarg[:, :], func=AF.Exp))
            pb.s(lambda e: e.activation(out=EdT[:, :], in_=edtarg[:, :], func=AF.Exp))
            for i in range(NI4):
                pb.v(lambda e, i=i, ci=ci: e.scalar_tensor_tensor(
                    out=BL(usb, i), in0=BL(Vt, i), scalar=CC(bbs, i, ci),
                    in1=BL(t1, i), op0=ALU.mult, op1=ALU.subtract))
                pb.v(lambda e, i=i, ci=ci: e.tensor_scalar(
                    out=BL(Kh, i), in0=BL(Kt, i), scalar1=CC(ksc, i, ci),
                    scalar2=None, op0=ALU.mult))
            pb.bar()
            for i in range(NI4):
                pb.v(lambda e, i=i: e.tensor_mul(BL(Ed, i), BL(Ed, i), MLs[:, :]))
                pb.v(lambda e, i=i: e.tensor_mul(BL(EdT, i), BL(EdT, i), MUi[:, :]))
            pb.bar()
            for i in range(NI4):
                pb.v(lambda e, i=i: e.tensor_mul(BL(ATm, i), BL(EdT, i), BL(pD, i)))
                pb.v(lambda e, i=i, ci=ci: e.scalar_tensor_tensor(
                    out=BL(Dm, i), in0=BL(pE, i), scalar=CC(bbs, i, ci),
                    in1=BL(Ed, i), op0=ALU.mult, op1=ALU.mult))
            pb.bar()
            for i in range(NI4):
                pb.t(lambda e, i=i: e.transpose(BL(pA, i), BL(Dm, i), IDN[:, :]))
            pb.bar()
            pb.v(lambda e: e.tensor_copy(out=Dt[:, :], in_=pA[:, :]))
            pb.bar()
            srcT, src = Dt, Dm
            for j in range(6):
                for i in range(NI4):
                    pb.t(lambda e, i=i, sT=srcT, s_=src: e.matmul(BL(pA, i), BL(sT, i), BL(s_, i), start=True, stop=True))
                    pb.t(lambda e, i=i, sT=srcT, s_=src: e.matmul(BL(pB, i), BL(s_, i), BL(sT, i), start=True, stop=True))
                pb.bar()
                pb.v(lambda e: e.tensor_copy(out=Pc[:, :], in_=pA[:, :]))
                pb.v(lambda e, j=j: e.tensor_copy(out=Pts[j][:, :], in_=pB[:, :]))
                pb.bar()
                srcT, src = Pts[j], Pc
            for j in [5, 4, 3, 2, 1, 0]:
                for i in range(NI4):
                    pb.t(lambda e, i=i, j=j: e.matmul(BL(pA, i), BL(Pts[j], i), BL(usb, i), start=True, stop=True))
                pb.bar()
                pb.v(lambda e: e.tensor_add(usb[:, :], usb[:, :], pA[:, :]))
                pb.bar()
            for i in range(NI4):
                pb.t(lambda e, i=i: e.matmul(BL(pA, i), BL(Dt, i), BL(usb, i), start=True, stop=True))
            pb.bar()
            pb.v(lambda e: e.tensor_sub(usb[:, :], usb[:, :], pA[:, :]))
            pb.bar()
            for i in range(NI4):
                pb.t(lambda e, i=i: e.matmul(BL(pA, i), BL(ATm, i), BL(usb, i), start=True, stop=True))
                pb.t(lambda e, i=i: e.matmul(BL(pB, i), BL(Kh, i), BL(usb, i), start=True, stop=True))
            pb.bar()
            pb.v(lambda e: e.tensor_add(osb[:, :], qs0g[:, :], pA[:, :]))
            for i in range(NI4):
                pb.v(lambda e, i=i, ci=ci: e.scalar_tensor_tensor(
                    out=Ssb[i][:, :], in0=Ssb[i][:, :], scalar=CC(gtb, i, ci),
                    in1=BL(pB, i), op0=ALU.mult, op1=ALU.add))
            pb.bar()
            for i in range(NI4):
                pb.t(lambda e, i=i: e.transpose(BL(pC, i), BL(osb, i), IDN[:, :]))
            pb.bar()
            pb.v(lambda e: e.tensor_copy(out=t1[:, :], in_=pC[:, :]))
            pb.bar()
            for i in range(NI4):
                pb.dg(lambda e, i=i, ci=ci: e.dma_start(
                    out=of[i, :, ci * 128:(ci + 1) * 128], in_=BL(t1, i)))
        pb.bar()
        pb.emit(nc)
    return nc


def build_C1():
    nc = bass.Bass("TRN2", target_bir_lowering=False)
    xsh = nc.dram_tensor("xsh", [D, TPC], F32, kind="ExternalInput").ap()
    osh = nc.dram_tensor("osh", [D, TPC], F32, kind="ExternalInput").ap()
    gsh = nc.dram_tensor("gsh", [D, TPC], F32, kind="ExternalInput").ap()
    onw = nc.dram_tensor("onw", [128], F32, kind="ExternalInput").ap()
    wo = nc.dram_tensor("wo", [D, D], F32, kind="ExternalInput").ap()
    wra = nc.dram_tensor("wra", [D, 8], F32, kind="ExternalInput").ap()
    bra = nc.dram_tensor("bra", [8], F32, kind="ExternalInput").ap()
    ffa = nc.dram_tensor("ffa", [8, D, 64], F32, kind="ExternalInput").ap()
    ffb = nc.dram_tensor("ffb", [8, 64, D], F32, kind="ExternalInput").ap()
    idn = nc.dram_tensor("idn", [128, 128], F32, kind="ExternalInput").ap()
    x2 = nc.dram_tensor("x2", [D, TPC], F32, kind="ExternalOutput").ap()

    pb = PB()
    with ExitStack() as st:
        e_ = st.enter_context
        xg = [e_(nc.sbuf_tensor("xsb%d" % d, [128, 512], F32)) for d in range(16)]
        on = [e_(nc.sbuf_tensor("on%d" % h, [128, 512], F32)) for h in range(16)]
        xln = [e_(nc.sbuf_tensor("xl%d" % d, [128, 512], F32)) for d in range(16)]
        hprime = [e_(nc.sbuf_tensor("hp%d" % ee, [64, 512], F32)) for ee in range(8)]
        ot = e_(nc.sbuf_tensor("ot", [128, 512], F32))
        gt = e_(nc.sbuf_tensor("gt", [128, 512], F32))
        sq = e_(nc.sbuf_tensor("sq", [128, 4 * 512], F32))
        scr = e_(nc.sbuf_tensor("scr", [1, 512], F32))
        mrow = e_(nc.sbuf_tensor("mrow", [1, 512], F32))
        rrow = e_(nc.sbuf_tensor("rrow", [1, 512], F32))
        wsb = e_(nc.sbuf_tensor("wsb", [128, 2048], F32))
        IDN = e_(nc.sbuf_tensor("IDN", [128, 128], F32))
        onwc = e_(nc.sbuf_tensor("onwc", [128, 1], F32))
        brc = e_(nc.sbuf_tensor("brc", [8, 1], F32))
        wrs = e_(nc.sbuf_tensor("wrs", [128, 16 * 8], F32))
        ffas = e_(nc.sbuf_tensor("ffas", [128, 16 * 64], F32))
        ffbt = e_(nc.sbuf_tensor("ffbt", [64, 8 * 128], F32))
        lg = e_(nc.sbuf_tensor("lg", [8, 512], F32))
        lgt = e_(nc.sbuf_tensor("lgt", [128, 8], F32))
        rmx = e_(nc.sbuf_tensor("rmx", [128, 1], F32))
        exr = e_(nc.sbuf_tensor("exr", [128, 8], F32))
        ssr = e_(nc.sbuf_tensor("ssr", [128, 1], F32))
        coefT = e_(nc.sbuf_tensor("coefT", [128, 8], F32))
        coef = e_(nc.sbuf_tensor("coef", [8, 512], F32))
        crow = e_(nc.sbuf_tensor("crw", [1, 512], F32))
        bcsb = e_(nc.sbuf_tensor("bcsb", [64, 512], F32))
        ones_col = e_(nc.sbuf_tensor("onesc", [128, 1], F32))
        ones_row = e_(nc.sbuf_tensor("onesr", [1, 128], F32))
        cb = e_(nc.sbuf_tensor("cb", [128, 4], F32))
        psA = e_(nc.psum_tensor("psA", [128, 512], F32))
        psB = e_(nc.psum_tensor("psB", [128, 512], F32))
        psC = e_(nc.psum_tensor("psC", [128, 512], F32))
        pr1 = e_(nc.psum_tensor("pr1", [1, 512], F32))
        pr2 = e_(nc.psum_tensor("pr2", [1, 512], F32))
        pt8 = e_(nc.psum_tensor("pt8", [128, 128], F32))

        pb.dl(lambda e: e.dma_start(out=IDN[:, :], in_=idn[:, :]))
        pb.dl(lambda e: e.dma_start(out=onwc[:, 0], in_=onw[:]))
        pb.dl(lambda e: e.dma_start(out=brc[:8, 0], in_=bra[:]))
        for dt in range(16):
            pb.dl(lambda e, dt=dt: e.dma_start(out=wrs[:, dt * 8:(dt + 1) * 8],
                                               in_=wra[dt * 128:(dt + 1) * 128, :]))
        pb.v(lambda e: e.memset(ones_col[:, :], 1.0))
        pb.v(lambda e: e.memset(ones_row[:1, :], 1.0))
        pb.v(lambda e: e.memset(cb[:, 0:1], 1e-5))
        pb.v(lambda e: e.memset(cb[:, 1:2], 1e-6))
        pb.v(lambda e: e.memset(cb[:, 2:3], 1.0))
        pb.v(lambda e: e.memset(cb[:, 3:4], -16.0))
        pb.bar()

        for gi in range(4):
            g0 = gi * 512
            for d in range(16):
                pb.dl(lambda e, d=d, g0=g0: e.dma_start(
                    out=xg[d][:, :], in_=xsh[d * 128:(d + 1) * 128, g0:g0 + 512]))
            pb.bar()
            for h in range(16):
                pb.dl(lambda e, h=h, g0=g0: e.dma_start(
                    out=ot[:, :], in_=osh[h * 128:(h + 1) * 128, g0:g0 + 512]))
                pb.dl(lambda e, h=h, g0=g0: e.dma_start(
                    out=gt[:, :], in_=gsh[h * 128:(h + 1) * 128, g0:g0 + 512]))
                pb.bar()
                emit_silu(pb, gt[:, :], gt[:, :], sq, 512)
                pb.v(lambda e, h=h: e.tensor_mul(on[h][:, :], ot[:, :], gt[:, :]))
                pb.bar()
                pb.s(lambda e, h=h: e.activation(out=sq[:, :512], in_=on[h][:, :], func=AF.Square))
                pb.bar()
                pb.t(lambda e: e.matmul(pr1[:1, :], ones_col[:, 0:1], sq[:, :512],
                                        start=True, stop=True))
                pb.bar()
                pb.s(lambda e: e.activation(out=mrow[:1, :], in_=pr1[:1, :], func=AF.Ln,
                                            scale=1.0 / 128.0, bias=cb[:1, 0:1]))
                pb.bar()
                pb.s(lambda e: e.activation(out=mrow[:1, :], in_=mrow[:1, :], func=AF.Exp, scale=-0.5))
                pb.bar()
                pb.t(lambda e: e.matmul(psA[:, :], ones_row[:1, :], mrow[:1, :],
                                        start=True, stop=True))
                pb.bar()
                pb.v(lambda e, h=h: e.scalar_tensor_tensor(
                    out=on[h][:, :], in0=on[h][:, :], scalar=onwc[:, 0:1],
                    in1=psA[:, :], op0=ALU.mult, op1=ALU.mult))
                pb.bar()
            for dm in range(16):
                for dt in range(16):
                    pb.dl(lambda e, dm=dm, dt=dt: e.dma_start(
                        out=wsb[:, dt * 128:(dt + 1) * 128],
                        in_=wo[dt * 128:(dt + 1) * 128, dm * 128:(dm + 1) * 128]))
                pb.bar()
                for dt in range(16):
                    pb.t(lambda e, dm=dm, dt=dt: e.matmul(
                        psB[:, :], wsb[:, dt * 128:(dt + 1) * 128], on[dt][:, :],
                        start=(dt == 0), stop=(dt == 15)))
                pb.bar()
                pb.v(lambda e, dm=dm: e.tensor_add(xg[dm][:, :], xg[dm][:, :], psB[:, :]))
                pb.bar()
            emit_ln(pb, xg, [(0, 512)], psA, psB, pr1, pr2, sq, mrow, rrow, scr,
                    xln, ones_col, ones_row, cb[:1, 0:1])
            for dt in range(16):
                pb.t(lambda e, dt=dt: e.matmul(
                    psA[:8, :], wrs[:, dt * 8:(dt + 1) * 8], xln[dt][:, :],
                    start=(dt == 0), stop=(dt == 15)))
            pb.bar()
            pb.v(lambda e: e.tensor_scalar(out=lg[:8, :], in0=psA[:8, :],
                                           scalar1=brc[:8, 0:1], scalar2=None, op0=ALU.add))
            pb.bar()
            for sub in range(4):
                pb.t(lambda e, sub=sub: e.transpose(pt8[:, :8], lg[:8, sub * 128:(sub + 1) * 128],
                                                    IDN[:8, :8]))
                pb.bar()
                pb.v(lambda e: e.tensor_copy(out=lgt[:, :8], in_=pt8[:, :8]))
                pb.bar()
                pb.v(lambda e: e.reduce_max(out=rmx[:, 0:1], in_=lgt[:, :8], axis=AX.X))
                pb.bar()
                pb.v(lambda e: e.tensor_scalar(out=rmx[:, 0:1], in0=rmx[:, 0:1],
                                               scalar1=-1.0, scalar2=None, op0=ALU.mult))
                pb.bar()
                pb.s(lambda e: e.activation(out=exr[:, :8], in_=lgt[:, :8], func=AF.Exp,
                                            bias=rmx[:, 0:1]))
                pb.bar()
                pb.v(lambda e: e.reduce_sum(out=ssr[:, 0:1], in_=exr[:, :8], axis=AX.X))
                pb.bar()
                pb.v(lambda e: e.reciprocal(out=ssr[:, 0:1], in_=ssr[:, 0:1]))
                pb.bar()
                pb.v(lambda e: e.tensor_scalar(out=ssr[:, 0:1], in0=ssr[:, 0:1],
                                               scalar1=1.0 / 64.0, scalar2=None, op0=ALU.mult))
                pb.bar()
                pb.v(lambda e: e.tensor_scalar(out=coefT[:, :8], in0=exr[:, :8],
                                               scalar1=1.0, scalar2=ssr[:, 0:1],
                                               op0=ALU.is_ge, op1=ALU.mult))
                pb.bar()
                pb.t(lambda e: e.transpose(pt8[:8, :], coefT[:, :8], IDN[:, :]))
                pb.bar()
                pb.v(lambda e, sub=sub: e.tensor_copy(out=coef[:8, sub * 128:(sub + 1) * 128],
                                                      in_=pt8[:8, :]))
                pb.bar()
            for ee in range(8):
                pb.dl(lambda e, ee=ee: e.dma_start(out=crow[:1, :], in_=coef[ee:ee + 1, :]))
                for dt in range(16):
                    pb.dl(lambda e, ee=ee, dt=dt: e.dma_start(
                        out=ffas[:, dt * 64:(dt + 1) * 64],
                        in_=ffa[ee, dt * 128:(dt + 1) * 128, :]))
                pb.bar()
                for dt in range(16):
                    pb.t(lambda e, dt=dt: e.matmul(
                        psA[:64, :], ffas[:, dt * 64:(dt + 1) * 64],
                        xln[dt][:, :], start=(dt == 0), stop=(dt == 15)))
                pb.t(lambda e: e.matmul(psB[:64, :], ones_row[:1, :64], crow[:1, :],
                                        start=True, stop=True))
                pb.bar()
                pb.v(lambda e: e.tensor_copy(out=bcsb[:64, :], in_=psB[:64, :]))
                pb.bar()
                pb.v(lambda e, ee=ee: e.tensor_mul(hprime[ee][:64, :], psA[:64, :], bcsb[:64, :]))
                pb.bar()
            for dm in range(16):
                for ee in range(8):
                    pb.dl(lambda e, ee=ee, dm=dm: e.dma_start(
                        out=ffbt[:64, ee * 128:(ee + 1) * 128],
                        in_=ffb[ee, :, dm * 128:(dm + 1) * 128]))
                pb.bar()
                for ee in range(8):
                    pb.t(lambda e, ee=ee: e.matmul(
                        psC[:, :], ffbt[:64, ee * 128:(ee + 1) * 128],
                        hprime[ee][:64, :], start=(ee == 0), stop=(ee == 7)))
                pb.bar()
                pb.v(lambda e, dm=dm: e.tensor_add(xg[dm][:, :], xg[dm][:, :], psC[:, :]))
                pb.bar()
                pb.dg(lambda e, dm=dm, g0=g0: e.dma_start(
                    out=x2[dm * 128:(dm + 1) * 128, g0:g0 + 512], in_=xg[dm][:, :]))
            pb.bar()
        pb.emit(nc)
    return nc


def build_C2():
    nc = bass.Bass("TRN2", target_bir_lowering=False)
    x2 = nc.dram_tensor("x2", [D, TPC], F32, kind="ExternalInput").ap()
    wq = nc.dram_tensor("wq", [NH, D, 128], F32, kind="ExternalInput").ap()
    wk = nc.dram_tensor("wk", [NH, D, 128], F32, kind="ExternalInput").ap()
    wv = nc.dram_tensor("wv", [NH, D, 128], F32, kind="ExternalInput").ap()
    q1 = nc.dram_tensor("q1", [NH, 128, TPC], BF16, kind="ExternalOutput").ap()
    k1 = nc.dram_tensor("k1", [NH, 128, TPC], BF16, kind="ExternalOutput").ap()
    v1 = nc.dram_tensor("v1", [NH, 128, TPC], BF16, kind="ExternalOutput").ap()

    pb = PB()
    with ExitStack() as st:
        e_ = st.enter_context
        xg = [e_(nc.sbuf_tensor("xsb%d" % d, [128, 512], F32)) for d in range(16)]
        xlb = [e_(nc.sbuf_tensor("xl%d" % d, [128, TPC], BF16)) for d in range(16)]
        sq = e_(nc.sbuf_tensor("sq", [128, 4 * 512], F32))
        scr = e_(nc.sbuf_tensor("scr", [1, 512], F32))
        mrow = e_(nc.sbuf_tensor("mrow", [1, 512], F32))
        rrow = e_(nc.sbuf_tensor("rrow", [1, 512], F32))
        wsb = e_(nc.sbuf_tensor("wsb", [128, 2048], BF16))
        outb = e_(nc.sbuf_tensor("outb", [128, 512], BF16))
        ones_col = e_(nc.sbuf_tensor("onesc", [128, 1], F32))
        ones_row = e_(nc.sbuf_tensor("onesr", [1, 128], F32))
        cb = e_(nc.sbuf_tensor("cb", [128, 4], F32))
        psA = e_(nc.psum_tensor("psA", [128, 512], F32))
        psB = e_(nc.psum_tensor("psB", [128, 512], F32))
        pr1 = e_(nc.psum_tensor("pr1", [1, 512], F32))
        pr2 = e_(nc.psum_tensor("pr2", [1, 512], F32))

        pb.v(lambda e: e.memset(ones_col[:, :], 1.0))
        pb.v(lambda e: e.memset(ones_row[:1, :], 1.0))
        pb.v(lambda e: e.memset(cb[:, 0:1], 1e-5))
        pb.v(lambda e: e.memset(cb[:, 1:2], 1e-6))
        pb.v(lambda e: e.memset(cb[:, 2:3], 1.0))
        pb.v(lambda e: e.memset(cb[:, 3:4], -16.0))
        pb.bar()
        for gi in range(4):
            g0 = gi * 512
            for d in range(16):
                pb.dl(lambda e, d=d, g0=g0: e.dma_start(
                    out=xg[d][:, :], in_=x2[d * 128:(d + 1) * 128, g0:g0 + 512]))
            pb.bar()
            emit_ln(pb, xg, [(0, 512)], psA, psB, pr1, pr2, sq, mrow, rrow, scr,
                    [xx[:, g0:g0 + 512] for xx in xlb], ones_col, ones_row, cb[:1, 0:1])
        for h in range(NH):
            for (wap, dst, scale) in ((wq, q1, QS), (wk, k1, None), (wv, v1, None)):
                for dt in range(16):
                    pb.dg(lambda e, h=h, dt=dt, wap=wap: e.dma_start(
                        out=wsb[:, dt * 128:(dt + 1) * 128],
                        in_=wap[h, dt * 128:(dt + 1) * 128, :]))
                pb.bar()
                for gi in range(4):
                    g0 = gi * 512
                    for dt in range(16):
                        pb.t(lambda e, dt=dt, g0=g0: e.matmul(
                            psA[:, :], wsb[:, dt * 128:(dt + 1) * 128],
                            xlb[dt][:, g0:g0 + 512], start=(dt == 0), stop=(dt == 15)))
                    pb.bar()
                    if scale is None:
                        pb.v(lambda e: e.tensor_copy(out=outb[:, :], in_=psA[:, :]))
                    else:
                        pb.v(lambda e, scale=scale: e.tensor_scalar(
                            out=outb[:, :], in0=psA[:, :], scalar1=scale,
                            scalar2=None, op0=ALU.mult))
                    pb.bar()
                    pb.dg(lambda e, h=h, g0=g0, dst=dst: e.dma_start(
                        out=dst[h, :, g0:g0 + 512], in_=outb[:, :]))
                pb.bar()
        pb.emit(nc)
    return nc


def build_E():
    nc = bass.Bass("TRN2", target_bir_lowering=False)
    q1 = nc.dram_tensor("q1", [4, 128, TB], BF16, kind="ExternalInput").ap()
    k1 = nc.dram_tensor("k1", [4, 128, TB], BF16, kind="ExternalInput").ap()
    v1 = nc.dram_tensor("v1", [4, 128, TB], BF16, kind="ExternalInput").ap()
    mneg = nc.dram_tensor("mneg", [128, 128], F32, kind="ExternalInput").ap()
    idn = nc.dram_tensor("idn", [128, 128], F32, kind="ExternalInput").ap()
    zf = nc.dram_tensor("zf", [4, 128, TB], BF16, kind="ExternalOutput").ap()

    pb = PB()
    with ExitStack() as st:
        e_ = st.enter_context
        qs = [e_(nc.sbuf_tensor("qs%d" % u, [128, TB], BF16)) for u in range(2)]
        ks = [e_(nc.sbuf_tensor("ks%d" % u, [128, TB], BF16)) for u in range(2)]
        vs = [e_(nc.sbuf_tensor("vs%d" % u, [128, TB], BF16)) for u in range(2)]
        vt = [e_(nc.sbuf_tensor("vt%d" % u, [128, TB], BF16)) for u in range(2)]
        zfm = [e_(nc.sbuf_tensor("zfm%d" % u, [128, TB], BF16)) for u in range(2)]
        PT = [[e_(nc.sbuf_tensor("PT%d_%d" % (u, i), [128, 512], BF16)) for i in range(2)]
              for u in range(2)]
        MNb = e_(nc.sbuf_tensor("MNb", [128, 128], BF16))
        IDN = e_(nc.sbuf_tensor("IDN", [128, 128], F32))
        IDNb = e_(nc.sbuf_tensor("IDNb", [128, 128], BF16))
        onecb = e_(nc.sbuf_tensor("onecb", [128, 1], BF16))
        cb = e_(nc.sbuf_tensor("cb", [128, 4], F32))
        lacc = e_(nc.sbuf_tensor("lacc", [1, 256], F32))
        zacc = e_(nc.sbuf_tensor("zacc", [128, 256], F32))
        lcol = e_(nc.sbuf_tensor("lcol", [128, 2], F32))
        zsc = e_(nc.sbuf_tensor("zsc", [128, 256], F32))
        pS = [[e_(nc.psum_tensor("pS%d_%d" % (u, i), [128, 512], F32)) for i in range(2)]
              for u in range(2)]
        pZ = e_(nc.psum_tensor("pZ", [128, 256], F32))
        pL = e_(nc.psum_tensor("pL", [1, 256], F32))
        pT = e_(nc.psum_tensor("pT", [128, 256], F32))
        pTb = e_(nc.psum_tensor("pTb", [128, 512], BF16))

        pb.dl(lambda e: e.dma_start(out=IDN[:, :], in_=idn[:, :]))
        pb.dg(lambda e: e.dma_start(out=MNb[:, :], in_=mneg[:, :]))
        pb.dg(lambda e: e.dma_start(out=IDNb[:, :], in_=idn[:, :]))
        pb.v(lambda e: e.memset(onecb[:, :], 1.0))
        pb.v(lambda e: e.memset(cb[:, 3:4], -16.0))
        pb.bar()
        for pair in range(2):
            for u in range(2):
                gi = 2 * pair + u
                pb.dl(lambda e, gi=gi, u=u: e.dma_start(out=qs[u][:, :], in_=q1[gi, :, :]))
                pb.dl(lambda e, gi=gi, u=u: e.dma_start(out=ks[u][:, :], in_=k1[gi, :, :]))
                pb.dl(lambda e, gi=gi, u=u: e.dma_start(out=vs[u][:, :], in_=v1[gi, :, :]))
            pb.bar()
            for j0 in range(0, 64, 2):
                for u in range(2):
                    for dj in range(2):
                        pb.t(lambda e, u=u, j=j0 + dj, dj=dj: e.transpose(
                            pTb[:, (u * 2 + dj) * 128:(u * 2 + dj + 1) * 128],
                            vs[u][:, j * 128:(j + 1) * 128], IDNb[:, :]))
                pb.bar()
                for u in range(2):
                    for dj in range(2):
                        pb.s(lambda e, u=u, j=j0 + dj, dj=dj: e.copy(
                            out=vt[u][:, j * 128:(j + 1) * 128],
                            in_=pTb[:, (u * 2 + dj) * 128:(u * 2 + dj + 1) * 128]))
                pb.bar()
            for qb in range(64):
                nch = qb // 4 + 1
                nbtot = qb + 1

                def lpv(u, c):
                    jlo = 4 * c
                    jhi = min(4 * c + 3, qb)
                    nb = jhi - jlo + 1
                    par = c % 2
                    for bi in range(nb):
                        pb.t(lambda e, u=u, par=par, bi=bi, nb=nb: e.matmul(
                            pL[:1, u * 128:(u + 1) * 128], onecb[:, 0:1],
                            PT[u][par][:, bi * 128:(bi + 1) * 128],
                            start=(bi == 0), stop=(bi == nb - 1),
                            skip_group_check=True))
                    for bi in range(nb):
                        pb.t(lambda e, u=u, par=par, bi=bi, jlo=jlo, nb=nb: e.matmul(
                            pZ[:, u * 128:(u + 1) * 128],
                            PT[u][par][:, bi * 128:(bi + 1) * 128],
                            vt[u][:, (jlo + bi) * 128:(jlo + bi + 1) * 128],
                            start=(bi == 0), stop=(bi == nb - 1),
                            skip_group_check=True))

                def scores(u, c):
                    jlo = 4 * c
                    jhi = min(4 * c + 3, qb)
                    nb = jhi - jlo + 1
                    par = c % 2
                    for j in range(jlo, jhi + 1):
                        pb.t(lambda e, u=u, j=j, jlo=jlo, par=par, qb=qb: e.matmul(
                            pS[u][par][:, (j - jlo) * 128:(j - jlo + 1) * 128],
                            ks[u][:, j * 128:(j + 1) * 128], qs[u][:, qb * 128:(qb + 1) * 128],
                            start=True, stop=(j != qb), skip_group_check=True))
                    if jhi == qb:
                        pb.t(lambda e, u=u, par=par, nb=nb: e.matmul(
                            pS[u][par][:, (nb - 1) * 128:nb * 128], IDNb[:, :], MNb[:, :],
                            start=False, stop=True, skip_group_check=True))
                    return nb

                for c in range(nch):
                    for u in range(2):
                        scores(u, c)
                        if c > 0:
                            lpv(u, c - 1)
                    if c == 0:
                        pb.v(lambda e: e.memset(lacc[:1, :], 0.0))
                        pb.v(lambda e: e.memset(zacc[:, :], 0.0))
                    pb.bar()
                    for u in range(2):
                        nb = min(4 * c + 3, qb) - 4 * c + 1
                        pb.s(lambda e, u=u, par=c % 2, nb=nb: e.activation(
                            out=PT[u][par][:, :nb * 128], in_=pS[u][par][:, :nb * 128],
                            func=AF.Exp, bias=cb[:, 3:4]))
                    if c > 0:
                        pb.v(lambda e: e.tensor_add(lacc[:1, :], lacc[:1, :], pL[:1, :]))
                        pb.v(lambda e: e.tensor_add(zacc[:, :], zacc[:, :], pZ[:, :]))
                    pb.bar()
                for u in range(2):
                    lpv(u, nch - 1)
                pb.bar()
                pb.v(lambda e: e.tensor_add(lacc[:1, :], lacc[:1, :], pL[:1, :]))
                pb.v(lambda e: e.tensor_add(zacc[:, :], zacc[:, :], pZ[:, :]))
                pb.bar()
                for u in range(2):
                    pb.t(lambda e, u=u: e.transpose(pT[:, u:u + 1],
                                                    lacc[:1, u * 128:(u + 1) * 128], IDN[:1, :1]))
                pb.bar()
                pb.v(lambda e: e.tensor_copy(out=lcol[:, 0:2], in_=pT[:, 0:2]))
                pb.bar()
                pb.v(lambda e: e.reciprocal(out=lcol[:, 0:2], in_=lcol[:, 0:2]))
                pb.bar()
                for u in range(2):
                    pb.v(lambda e, u=u: e.tensor_scalar(
                        out=zsc[:, u * 128:(u + 1) * 128], in0=zacc[:, u * 128:(u + 1) * 128],
                        scalar1=lcol[:, u:u + 1], scalar2=None, op0=ALU.mult))
                pb.bar()
                for u in range(2):
                    pb.t(lambda e, u=u: e.transpose(pT[:, u * 128:(u + 1) * 128],
                                                    zsc[:, u * 128:(u + 1) * 128], IDN[:, :]))
                pb.bar()
                for u in range(2):
                    pb.v(lambda e, u=u, qb=qb: e.tensor_copy(
                        out=zfm[u][:, qb * 128:(qb + 1) * 128],
                        in_=pT[:, u * 128:(u + 1) * 128]))
                pb.bar()
            for u in range(2):
                gi = 2 * pair + u
                pb.dg(lambda e, gi=gi, u=u: e.dma_start(out=zf[gi, :, :], in_=zfm[u][:, :]))
            pb.bar()
        pb.emit(nc)
    return nc


def build_F():
    nc = bass.Bass("TRN2", target_bir_lowering=False)
    x2 = nc.dram_tensor("x2", [D, TPC], F32, kind="ExternalInput").ap()
    zsh = nc.dram_tensor("zsh", [NH, 128, TPC], BF16, kind="ExternalInput").ap()
    wot = nc.dram_tensor("wot", [NH, 128, D], F32, kind="ExternalInput").ap()
    x3 = nc.dram_tensor("x3", [D, TPC], F32, kind="ExternalOutput").ap()

    pb = PB()
    with ExitStack() as st:
        e_ = st.enter_context
        zs = [e_(nc.sbuf_tensor("z%d" % h, [128, TPC], BF16)) for h in range(NH)]
        ws = e_(nc.sbuf_tensor("ws", [128, NH * 128], BF16))
        xtile = e_(nc.sbuf_tensor("xtile", [128, 512], F32))
        psA = e_(nc.psum_tensor("psA", [128, 512], F32))

        for h in range(NH):
            pb.dl(lambda e, h=h: e.dma_start(out=zs[h][:, :], in_=zsh[h, :, :]))
        pb.bar()
        for dm in range(16):
            for h in range(NH):
                pb.dg(lambda e, h=h, dm=dm: e.dma_start(
                    out=ws[:, h * 128:(h + 1) * 128],
                    in_=wot[h, :, dm * 128:(dm + 1) * 128]))
            pb.bar()
            for gi in range(4):
                g0 = gi * 512
                pb.dl(lambda e, dm=dm, g0=g0: e.dma_start(
                    out=xtile[:, :], in_=x2[dm * 128:(dm + 1) * 128, g0:g0 + 512]))
                for h in range(NH):
                    pb.t(lambda e, h=h, g0=g0: e.matmul(
                        psA[:, :], ws[:, h * 128:(h + 1) * 128], zs[h][:, g0:g0 + 512],
                        start=(h == 0), stop=(h == NH - 1)))
                pb.bar()
                pb.v(lambda e: e.tensor_add(xtile[:, :], xtile[:, :], psA[:, :]))
                pb.bar()
                pb.dg(lambda e, dm=dm, g0=g0: e.dma_start(
                    out=x3[dm * 128:(dm + 1) * 128, g0:g0 + 512], in_=xtile[:, :]))
                pb.bar()
        pb.emit(nc)
    return nc


def build_G():
    nc = bass.Bass("TRN2", target_bir_lowering=False)
    x3a = nc.dram_tensor("x3a", [D, 2048], F32, kind="ExternalInput").ap()
    tidx = nc.dram_tensor("tidx", [4, 2048], I32, kind="ExternalInput").ap()
    tw = nc.dram_tensor("tw", [4, 2048], F32, kind="ExternalInput").ap()
    eids = nc.dram_tensor("eids", [4, 2], F32, kind="ExternalInput").ap()
    gwt = nc.dram_tensor("gwt", [2, D, 1408], F32, kind="ExternalInput").ap()
    uwt = nc.dram_tensor("uwt", [2, D, 1408], F32, kind="ExternalInput").ap()
    dwt = nc.dram_tensor("dwt", [2, 1408, D], F32, kind="ExternalInput").ap()
    part = nc.dram_tensor("part", [D, 2048], F32, kind="ExternalOutput").ap()

    NI = 11
    pb = PB()
    with ExitStack() as st:
        e_ = st.enter_context
        xg = [e_(nc.sbuf_tensor("xsb%d" % d, [128, 512], F32)) for d in range(16)]
        xlb = [e_(nc.sbuf_tensor("xl%d" % d, [128, 1024], BF16)) for d in range(16)]
        hp = [[e_(nc.sbuf_tensor("hp%d_%d" % (ee, o), [128, 1024], BF16))
               for o in range(NI)] for ee in range(2)]
        sq = e_(nc.sbuf_tensor("sq", [128, 4 * 512], F32))
        scr = e_(nc.sbuf_tensor("scr", [1, 512], F32))
        mrow = e_(nc.sbuf_tensor("mrow", [1, 512], F32))
        rrow = e_(nc.sbuf_tensor("rrow", [1, 512], F32))
        gws = e_(nc.sbuf_tensor("gws", [128, 2048], BF16))
        uws = e_(nc.sbuf_tensor("uws", [128, 2048], BF16))
        dws = e_(nc.sbuf_tensor("dws", [128, 2 * NI * 128], BF16))
        idxf = e_(nc.sbuf_tensor("idxf", [4, 2048], F32))
        idxi = e_(nc.sbuf_tensor("idxi", [4, 2048], I32))
        tws = e_(nc.sbuf_tensor("tws", [4, 2048], F32))
        eid = e_(nc.sbuf_tensor("eid", [4, 2], F32))
        wmk = e_(nc.sbuf_tensor("wmk", [4, 2048], F32))
        mrowe = [e_(nc.sbuf_tensor("mr%d" % ee, [1, 2048], F32)) for ee in range(2)]
        gact = e_(nc.sbuf_tensor("gact", [128, 512], F32))
        uact = e_(nc.sbuf_tensor("uact", [128, 512], F32))
        bcm = e_(nc.sbuf_tensor("bcm", [128, 512], F32))
        ones_col = e_(nc.sbuf_tensor("onesc", [128, 1], F32))
        ones_row = e_(nc.sbuf_tensor("onesr", [1, 128], F32))
        cb = e_(nc.sbuf_tensor("cb", [128, 4], F32))
        psA = e_(nc.psum_tensor("psA", [128, 512], F32))
        psB = e_(nc.psum_tensor("psB", [128, 512], F32))
        psC = e_(nc.psum_tensor("psC", [128, 512], F32))
        pr1 = e_(nc.psum_tensor("pr1", [1, 512], F32))
        pr2 = e_(nc.psum_tensor("pr2", [1, 512], F32))

        pb.dl(lambda e: e.dma_start(out=idxi[:, :], in_=tidx[:, :]))
        pb.dl(lambda e: e.dma_start(out=tws[:, :], in_=tw[:, :]))
        pb.dl(lambda e: e.dma_start(out=eid[:, :], in_=eids[:, :]))
        pb.v(lambda e: e.memset(ones_col[:, :], 1.0))
        pb.v(lambda e: e.memset(ones_row[:1, :], 1.0))
        pb.v(lambda e: e.memset(cb[:, 0:1], 1e-5))
        pb.v(lambda e: e.memset(cb[:, 1:2], 1e-6))
        pb.v(lambda e: e.memset(cb[:, 2:3], 1.0))
        pb.v(lambda e: e.memset(cb[:, 3:4], -16.0))
        pb.bar()
        pb.v(lambda e: e.tensor_copy(out=idxf[:, :], in_=idxi[:, :]))
        pb.bar()
        for ee in range(2):
            pb.v(lambda e, ee=ee: e.tensor_scalar(out=wmk[:, :], in0=idxf[:, :],
                                                  scalar1=eid[:4, ee:ee + 1], scalar2=None,
                                                  op0=ALU.is_equal))
            pb.bar()
            pb.v(lambda e: e.tensor_mul(wmk[:, :], wmk[:, :], tws[:, :]))
            pb.bar()
            for gi in range(4):
                pb.t(lambda e, gi=gi: e.matmul(pr1[:1, :], ones_col[:4, 0:1],
                                               wmk[:4, gi * 512:(gi + 1) * 512],
                                               start=True, stop=True))
                pb.bar()
                pb.v(lambda e, ee=ee, gi=gi: e.tensor_copy(
                    out=mrowe[ee][:1, gi * 512:(gi + 1) * 512], in_=pr1[:1, :]))
                pb.bar()
        for hf in range(2):
            h0 = hf * 1024
            for gi2 in range(2):
                g0 = h0 + gi2 * 512
                for d in range(16):
                    pb.dl(lambda e, d=d, g0=g0: e.dma_start(
                        out=xg[d][:, :], in_=x3a[d * 128:(d + 1) * 128, g0:g0 + 512]))
                pb.bar()
                emit_ln(pb, xg, [(0, 512)], psA, psB, pr1, pr2, sq, mrow, rrow,
                        scr, [xx[:, gi2 * 512:gi2 * 512 + 512] for xx in xlb],
                        ones_col, ones_row, cb[:1, 0:1])
            for ee in range(2):
                for o in range(NI):
                    for dt in range(16):
                        pb.dg(lambda e, ee=ee, o=o, dt=dt: e.dma_start(
                            out=gws[:, dt * 128:(dt + 1) * 128],
                            in_=gwt[ee, dt * 128:(dt + 1) * 128, o * 128:(o + 1) * 128]))
                        pb.dg(lambda e, ee=ee, o=o, dt=dt: e.dma_start(
                            out=uws[:, dt * 128:(dt + 1) * 128],
                            in_=uwt[ee, dt * 128:(dt + 1) * 128, o * 128:(o + 1) * 128]))
                    pb.bar()
                    for gi2 in range(2):
                        gl = gi2 * 512
                        for dt in range(16):
                            pb.t(lambda e, dt=dt, gl=gl: e.matmul(
                                psA[:, :], gws[:, dt * 128:(dt + 1) * 128],
                                xlb[dt][:, gl:gl + 512], start=(dt == 0), stop=(dt == 15)))
                        for dt in range(16):
                            pb.t(lambda e, dt=dt, gl=gl: e.matmul(
                                psB[:, :], uws[:, dt * 128:(dt + 1) * 128],
                                xlb[dt][:, gl:gl + 512], start=(dt == 0), stop=(dt == 15)))
                        pb.t(lambda e, ee=ee, h0=h0, gl=gl: e.matmul(
                            psC[:, :], ones_row[:1, :], mrowe[ee][:1, h0 + gl:h0 + gl + 512],
                            start=True, stop=True))
                        pb.bar()
                        pb.s(lambda e: e.activation(out=gact[:, :], in_=psA[:, :],
                                                    func=AF.Exp, scale=-1.0))
                        pb.bar()
                        pb.v(lambda e: e.tensor_scalar(out=gact[:, :], in0=gact[:, :],
                                                       scalar1=1.0, scalar2=None, op0=ALU.add))
                        pb.bar()
                        pb.v(lambda e: e.reciprocal(out=gact[:, :], in_=gact[:, :]))
                        pb.bar()
                        pb.v(lambda e: e.tensor_mul(gact[:, :], gact[:, :], psA[:, :]))
                        pb.bar()
                        pb.v(lambda e: e.tensor_mul(uact[:, :], gact[:, :], psB[:, :]))
                        pb.bar()
                        pb.v(lambda e: e.tensor_copy(out=bcm[:, :], in_=psC[:, :]))
                        pb.bar()
                        pb.v(lambda e, ee=ee, o=o, gl=gl: e.tensor_mul(
                            hp[ee][o][:, gl:gl + 512], uact[:, :], bcm[:, :]))
                        pb.bar()
            for dm in range(16):
                for ee in range(2):
                    for o in range(NI):
                        pb.dg(lambda e, ee=ee, o=o, dm=dm: e.dma_start(
                            out=dws[:, (ee * NI + o) * 128:(ee * NI + o + 1) * 128],
                            in_=dwt[ee, o * 128:(o + 1) * 128, dm * 128:(dm + 1) * 128]))
                pb.bar()
                for gi2 in range(2):
                    gl = gi2 * 512
                    for i2 in range(2 * NI):
                        ee, o = i2 // NI, i2 % NI
                        pb.t(lambda e, ee=ee, o=o, gl=gl, i2=i2: e.matmul(
                            psA[:, :], dws[:, (ee * NI + o) * 128:(ee * NI + o + 1) * 128],
                            hp[ee][o][:, gl:gl + 512], start=(i2 == 0), stop=(i2 == 2 * NI - 1)))
                    pb.bar()
                    pb.v(lambda e: e.tensor_copy(out=gact[:, :], in_=psA[:, :]))
                    pb.bar()
                    pb.dg(lambda e, dm=dm, h0=h0, gl=gl: e.dma_start(
                        out=part[dm * 128:(dm + 1) * 128, h0 + gl:h0 + gl + 512],
                        in_=gact[:, :]))
                pb.bar()
        pb.emit(nc)
    return nc


def build_H():
    nc = bass.Bass("TRN2", target_bir_lowering=False)
    parts = nc.dram_tensor("parts", [8, D, 256], F32, kind="ExternalInput").ap()
    x3s = nc.dram_tensor("x3s", [D, 256], F32, kind="ExternalInput").ap()
    outh = nc.dram_tensor("outh", [D, 256], F32, kind="ExternalOutput").ap()
    pb = PB()
    with ExitStack() as st:
        e_ = st.enter_context
        acc = e_(nc.sbuf_tensor("acc", [128, 256], F32))
        pt = e_(nc.sbuf_tensor("pt", [128, 256], F32))
        for d in range(16):
            pb.dl(lambda e, d=d: e.dma_start(out=acc[:, :], in_=x3s[d * 128:(d + 1) * 128, :]))
            pb.bar()
            for i in range(8):
                pb.dl(lambda e, i=i, d=d: e.dma_start(out=pt[:, :],
                                                      in_=parts[i, d * 128:(d + 1) * 128, :]))
                pb.bar()
                pb.v(lambda e: e.tensor_add(acc[:, :], acc[:, :], pt[:, :]))
                pb.bar()
            pb.dg(lambda e, d=d: e.dma_start(out=outh[d * 128:(d + 1) * 128, :], in_=acc[:, :]))
            pb.bar()
        pb.emit(nc)
    return nc


def kernel(L0_attn_hidden_states, L0_attn_W_qkv, L0_attn_W_ab, L0_attn_conv_w,
           L0_attn_A_log, L0_attn_dt_bias, L0_attn_o_norm_w, L0_attn_W_o,
           L0_ffn_A, L0_ffn_B, L0_ffn_Wr, L0_ffn_br,
           L1_attn_W_Q, L1_attn_W_K, L1_attn_W_V, L1_attn_W_O,
           L1_ffn_topk_idx, L1_ffn_topk_weight, L1_ffn_gate_weight,
           L1_ffn_up_weight, L1_ffn_down_weight):
    cores = list(range(NC))
    f32 = np.float32
    asc = np.ascontiguousarray
    mUi, mLs, mNeg = _np_masks()
    idn = np.eye(128, dtype=f32)

    x = np.asarray(L0_attn_hidden_states, f32)
    x_fm = asc(x.reshape(NTOK, D).T)

    ncA = build_A()
    imA = []
    wab = np.asarray(L0_attn_W_ab, f32)
    for c in cores:
        xh = np.zeros((D, TPC + 3), f32)
        xh[:, 3:] = x_fm[:, c * TPC:(c + 1) * TPC]
        if c % 4 != 0:
            xh[:, :3] = x_fm[:, c * TPC - 3:c * TPC]
        imA.append({"xh": asc(xh), "wqkv": np.asarray(L0_attn_W_qkv, f32),
                    "waa": asc(wab[:, 0:16]), "wbb": asc(wab[:, 16:32]),
                    "wgate": asc(wab[:, 32:]),
                    "convw": np.asarray(L0_attn_conv_w, f32),
                    "alog": np.asarray(L0_attn_A_log, f32),
                    "dtb": np.asarray(L0_attn_dt_bias, f32)})
    rA = run_bass_kernel_spmd(ncA, imA, core_ids=cores).results
    qf_g = np.concatenate([r["qf"] for r in rA], axis=2)
    kf_g = np.concatenate([r["kf"] for r in rA], axis=2)
    vf_g = np.concatenate([r["vf"] for r in rA], axis=2)
    gg_g = np.concatenate([r["gg"] for r in rA], axis=1)
    bb_g = np.concatenate([r["bb"] for r in rA], axis=1)
    gate_g = np.concatenate([r["gatef"] for r in rA], axis=1)

    insts = [(i // NH, i % NH) for i in range(32)]

    ncB = build_B()
    imB = []
    for c in cores:
        my = insts[4 * c:4 * c + 4]
        imB.append({
            "qb": asc(np.stack([qf_g[h, :, b * TB:(b + 1) * TB] for (b, h) in my])),
            "kb": asc(np.stack([kf_g[h, :, b * TB:(b + 1) * TB] for (b, h) in my])),
            "vb": asc(np.stack([vf_g[h, :, b * TB:(b + 1) * TB] for (b, h) in my])),
            "gb": asc(np.stack([gg_g[h, b * TB:(b + 1) * TB].reshape(64, 128).T for (b, h) in my])),
            "bbeta": asc(np.stack([bb_g[h, b * TB:(b + 1) * TB].reshape(64, 128).T for (b, h) in my])),
            "mui": mUi, "mls": mLs, "idn": idn})
    rB = run_bass_kernel_spmd(ncB, imB, core_ids=cores).results
    o_g = np.zeros((2, NH, 128, TB), f32)
    for c in cores:
        for k2, (b, h) in enumerate(insts[4 * c:4 * c + 4]):
            o_g[b, h] = rB[c]["of"][k2]

    ncC1 = build_C1()
    imC1 = []
    for c in cores:
        b = c // 4
        tl = (c * TPC) % TB
        o_b = o_g[b].reshape(D, TB)
        imC1.append({"xsh": asc(x_fm[:, c * TPC:(c + 1) * TPC]),
                     "osh": asc(o_b[:, tl:tl + TPC]),
                     "gsh": asc(gate_g[:, c * TPC:(c + 1) * TPC]),
                     "onw": np.asarray(L0_attn_o_norm_w, f32),
                     "wo": np.asarray(L0_attn_W_o, f32),
                     "wra": asc(np.asarray(L0_ffn_Wr, f32).T),
                     "bra": np.asarray(L0_ffn_br, f32),
                     "ffa": asc(np.asarray(L0_ffn_A, f32).transpose(0, 2, 1)),
                     "ffb": asc(np.asarray(L0_ffn_B, f32).transpose(0, 2, 1)),
                     "idn": idn})
    rC1 = run_bass_kernel_spmd(ncC1, imC1, core_ids=cores).results
    x2_sh = [r["x2"] for r in rC1]

    ncC2 = build_C2()
    wqh = asc(np.asarray(L1_attn_W_Q, f32).transpose(0, 2, 1))
    wkh = asc(np.asarray(L1_attn_W_K, f32).transpose(0, 2, 1))
    wvh = asc(np.asarray(L1_attn_W_V, f32).transpose(0, 2, 1))
    imC2 = [{"x2": asc(x2_sh[c]), "wq": wqh, "wk": wkh, "wv": wvh} for c in cores]
    rC2 = run_bass_kernel_spmd(ncC2, imC2, core_ids=cores).results
    q1_g = np.concatenate([r["q1"] for r in rC2], axis=2)
    k1_g = np.concatenate([r["k1"] for r in rC2], axis=2)
    v1_g = np.concatenate([r["v1"] for r in rC2], axis=2)

    ncE = build_E()
    imE = []
    for c in cores:
        my = insts[4 * c:4 * c + 4]
        imE.append({"q1": asc(np.stack([q1_g[h, :, b * TB:(b + 1) * TB] for (b, h) in my])),
                    "k1": asc(np.stack([k1_g[h, :, b * TB:(b + 1) * TB] for (b, h) in my])),
                    "v1": asc(np.stack([v1_g[h, :, b * TB:(b + 1) * TB] for (b, h) in my])),
                    "mneg": mNeg, "idn": idn})
    rE = run_bass_kernel_spmd(ncE, imE, core_ids=cores).results
    z_g = np.zeros((2, NH, 128, TB), rE[0]["zf"].dtype)
    for c in cores:
        for k2, (b, h) in enumerate(insts[4 * c:4 * c + 4]):
            z_g[b, h] = rE[c]["zf"][k2]

    ncF = build_F()
    woth = asc(np.asarray(L1_attn_W_O, f32).transpose(0, 2, 1))
    imF = []
    for c in cores:
        b = c // 4
        tl = (c * TPC) % TB
        imF.append({"x2": asc(x2_sh[c]), "zsh": asc(z_g[b][:, :, tl:tl + TPC]),
                    "wot": woth})
    rF = run_bass_kernel_spmd(ncF, imF, core_ids=cores).results
    x3_g = np.concatenate([r["x3"] for r in rF], axis=1)

    ncG = build_G()
    x3a = asc(x3_g[:, :2048])
    tidx = asc(np.asarray(L1_ffn_topk_idx).reshape(2048, 4).T.astype(np.int32))
    twr = asc(np.asarray(L1_ffn_topk_weight, f32).reshape(2048, 4).T)
    gwt = np.asarray(L1_ffn_gate_weight, f32).transpose(0, 2, 1)
    uwt = np.asarray(L1_ffn_up_weight, f32).transpose(0, 2, 1)
    dwt = np.asarray(L1_ffn_down_weight, f32).transpose(0, 2, 1)
    imG = []
    for c in cores:
        eids = np.zeros((4, 2), f32)
        eids[:, 0] = 2 * c
        eids[:, 1] = 2 * c + 1
        imG.append({"x3a": x3a, "tidx": tidx, "tw": twr, "eids": eids,
                    "gwt": asc(gwt[2 * c:2 * c + 2]),
                    "uwt": asc(uwt[2 * c:2 * c + 2]),
                    "dwt": asc(dwt[2 * c:2 * c + 2])})
    rG = run_bass_kernel_spmd(ncG, imG, core_ids=cores).results
    parts_g = np.stack([r["part"] for r in rG])

    ncH = build_H()
    imH = []
    for c in cores:
        imH.append({"parts": asc(parts_g[:, :, c * 256:(c + 1) * 256]),
                    "x3s": asc(x3a[:, c * 256:(c + 1) * 256])})
    rH = run_bass_kernel_spmd(ncH, imH, core_ids=cores).results
    outh = np.concatenate([r["outh"] for r in rH], axis=1)

    final = np.array(x3_g)
    final[:, :2048] = outh
    return asc(final.T).reshape(2, TB, D).astype(np.float32)


# version 04 (final)
# speedup vs baseline: 1.5814x; 1.0882x over previous
import numpy as np
from contextlib import ExitStack
import concourse.bass as bass
import concourse.mybir as mybir
from concourse.bass_utils import run_bass_kernel_spmd

F32 = mybir.dt.float32
BF16 = mybir.dt.bfloat16
I32 = mybir.dt.int32
AF = mybir.ActivationFunctionType
ALU = mybir.AluOpType
AX = mybir.AxisListType

NC = 8
B_CHUNKS = 64
D = 2048
TPC = 2048
NTOK = 16384
NH = 16
TB = 8192
QS = 0.08838834764831845

ENGS = ("tensor", "vector", "scalar", "sync", "gpsimd")


class PB:

    def __init__(self):
        self.items = {e: [] for e in ENGS}
        self.total = 0
        self.waited = {e: 0 for e in ENGS}
        self.cur = {e: None for e in ENGS}
        self.dmas = {e: 0 for e in ENGS}
        self.pdma = {e: False for e in ENGS}

    def op(self, eng, fn, dma=False):
        it = {"fn": fn, "waits": [], "incs": []}
        if self.waited[eng] < self.total:
            it["waits"].append(("P", self.total))
            self.waited[eng] = self.total
        self.items[eng].append(it)
        if dma:
            self.dmas[eng] += 1
            it["incs"].append(("D" + eng, 16))
            self.pdma[eng] = True
        else:
            self.cur[eng] = it
        return it

    def t(self, fn): self.op("tensor", fn)
    def v(self, fn): self.op("vector", fn)
    def s(self, fn): self.op("scalar", fn)
    def g(self, fn): self.op("gpsimd", fn)
    def dl(self, fn): self.op("sync", fn, dma=True)
    def dg(self, fn): self.op("gpsimd", fn, dma=True)

    def bar(self):
        n = 0
        for e in ENGS:
            if self.cur[e] is not None:
                self.cur[e]["incs"].append(("P", 1))
                n += 1
                self.cur[e] = None
            if self.pdma[e]:
                it = {"fn": None, "waits": [("D" + e, 16 * self.dmas[e])],
                      "incs": [("P", 1)]}
                self.items[e].append(it)
                n += 1
                self.pdma[e] = False
        self.total += n

    def emit(self, nc):
        self.bar()
        with ExitStack() as st:
            sems = {"P": st.enter_context(nc.semaphore("Pph"))}
            for e in ENGS:
                sems["D" + e] = st.enter_context(nc.semaphore("Dq_" + e))
            block = st.enter_context(nc.Block())
            for e in ENGS:
                items = self.items[e]
                if not items:
                    continue

                def body(eng, items=items):
                    for it in items:
                        for (s_, v_) in it["waits"]:
                            eng.wait_ge(sems[s_], v_)
                        if it["fn"] is None:
                            for (s_, n_) in it["incs"]:
                                eng.sem_inc(sems[s_], n_)
                        else:
                            inst = it["fn"](eng)
                            for (s_, n_) in it["incs"]:
                                inst = inst.then_inc(sems[s_], n_)
                getattr(block, e)(body)


def _np_masks():
    i = np.arange(128)
    mUi = (i[:, None] <= i[None, :]).astype(np.float32)
    mLs = (i[:, None] > i[None, :]).astype(np.float32)
    mNeg = np.where(i[:, None] > i[None, :], -200.0, 0.0).astype(np.float32)
    return mUi, mLs, mNeg


def emit_ln(pb, xin, groups, psA, psB, pr1, pr2, sq, mrow, rrow, scr, xout,
            ones_col, ones_row, ebias, d_total=2048.0):
    nd = len(xin)
    for (off, w) in groups:
        for r0 in range(0, nd, 4):
            for d in range(r0, min(r0 + 4, nd)):
                pb.s(lambda e, d=d, r0=r0, off=off, w=w: e.activation(
                    out=sq[:, (d - r0) * 512:(d - r0) * 512 + w],
                    in_=xin[d][:, off:off + w], func=AF.Square))
            pb.bar()
            for d in range(r0, min(r0 + 4, nd)):
                pb.t(lambda e, d=d, off=off, w=w: e.matmul(
                    pr1[:1, :w], ones_col[:, 0:1], xin[d][:, off:off + w],
                    start=(d == 0), stop=(d == nd - 1)))
            for d in range(r0, min(r0 + 4, nd)):
                pb.t(lambda e, d=d, r0=r0, w=w: e.matmul(
                    pr2[:1, :w], ones_col[:, 0:1], sq[:, (d - r0) * 512:(d - r0) * 512 + w],
                    start=(d == 0), stop=(d == nd - 1)))
            pb.bar()
        pb.v(lambda e, w=w: e.tensor_scalar(
            out=mrow[:1, :w], in0=pr1[:1, :w], scalar1=1.0 / d_total,
            scalar2=None, op0=ALU.mult))
        pb.bar()
        pb.v(lambda e, w=w: e.tensor_mul(scr[:1, :w], mrow[:1, :w], mrow[:1, :w]))
        pb.bar()
        pb.v(lambda e, w=w: e.scalar_tensor_tensor(
            out=rrow[:1, :w], in0=pr2[:1, :w], scalar=1.0 / d_total,
            in1=scr[:1, :w], op0=ALU.mult, op1=ALU.subtract))
        pb.bar()
        pb.s(lambda e, w=w: e.activation(out=rrow[:1, :w], in_=rrow[:1, :w],
                                         func=AF.Ln, bias=ebias))
        pb.bar()
        pb.s(lambda e, w=w: e.activation(out=rrow[:1, :w], in_=rrow[:1, :w],
                                         func=AF.Exp, scale=-0.5))
        pb.bar()
        pb.t(lambda e, w=w: e.matmul(psA[:, :w], ones_row[:1, :], mrow[:1, :w],
                                     start=True, stop=True))
        pb.t(lambda e, w=w: e.matmul(psB[:, :w], ones_row[:1, :], rrow[:1, :w],
                                     start=True, stop=True))
        pb.bar()
        for d in range(nd):
            pb.v(lambda e, d=d, off=off, w=w: e.tensor_sub(
                sq[:, 0:w], xin[d][:, off:off + w], psA[:, :w]))
            pb.bar()
            pb.v(lambda e, d=d, off=off, w=w: e.tensor_mul(
                xout[d][:, off:off + w], sq[:, 0:w], psB[:, :w]))
            pb.bar()


def emit_silu(pb, out_ap, in_ap, scratch, w):
    pb.s(lambda e: e.activation(out=scratch[:, :w], in_=in_ap, func=AF.Exp, scale=-1.0))
    pb.bar()
    pb.v(lambda e: e.tensor_scalar(out=scratch[:, :w], in0=scratch[:, :w],
                                   scalar1=1.0, scalar2=None, op0=ALU.add))
    pb.bar()
    pb.v(lambda e: e.reciprocal(out=scratch[:, :w], in_=scratch[:, :w]))
    pb.bar()
    pb.v(lambda e: e.tensor_mul(out_ap, in_ap, scratch[:, :w]))
    pb.bar()


def build_A():
    nc = bass.Bass("TRN2", target_bir_lowering=False)
    xh = nc.dram_tensor("xh", [D, TPC + 3], F32, kind="ExternalInput").ap()
    wqkv = nc.dram_tensor("wqkv", [D, 6144], F32, kind="ExternalInput").ap()
    waa = nc.dram_tensor("waa", [D, 16], F32, kind="ExternalInput").ap()
    wbb = nc.dram_tensor("wbb", [D, 16], F32, kind="ExternalInput").ap()
    wgate = nc.dram_tensor("wgate", [D, 2048], F32, kind="ExternalInput").ap()
    convw = nc.dram_tensor("convw", [6144, 4], F32, kind="ExternalInput").ap()
    alog = nc.dram_tensor("alog", [16], F32, kind="ExternalInput").ap()
    dtb = nc.dram_tensor("dtb", [16], F32, kind="ExternalInput").ap()
    qf = nc.dram_tensor("qf", [NH, 128, TPC], F32, kind="ExternalOutput").ap()
    kf = nc.dram_tensor("kf", [NH, 128, TPC], F32, kind="ExternalOutput").ap()
    vf = nc.dram_tensor("vf", [NH, 128, TPC], F32, kind="ExternalOutput").ap()
    gg = nc.dram_tensor("gg", [16, TPC], F32, kind="ExternalOutput").ap()
    bb = nc.dram_tensor("bb", [16, TPC], F32, kind="ExternalOutput").ap()
    gatef = nc.dram_tensor("gatef", [D, TPC], F32, kind="ExternalOutput").ap()

    W = TPC + 3
    groups = [(0, 512), (512, 512), (1024, 512), (1536, 512), (2048, 3)]
    pb = PB()
    with ExitStack() as st:
        e_ = st.enter_context
        xg = [e_(nc.sbuf_tensor("xg%d" % d, [128, 512], F32)) for d in range(16)]
        xlb = [e_(nc.sbuf_tensor("xlb%d" % d, [128, W], BF16)) for d in range(16)]
        sq = e_(nc.sbuf_tensor("sq", [128, 4 * 512], F32))
        scr = e_(nc.sbuf_tensor("scr", [1, 512], F32))
        mrow = e_(nc.sbuf_tensor("mrow", [1, 512], F32))
        rrow = e_(nc.sbuf_tensor("rrow", [1, 512], F32))
        srow = e_(nc.sbuf_tensor("srow", [1, 2048], F32))
        wsb = [e_(nc.sbuf_tensor("wsb%d" % i, [128, 2048], BF16)) for i in range(2)]
        qkvfull = e_(nc.sbuf_tensor("qkvfull", [128, W], BF16))
        convf = e_(nc.sbuf_tensor("convf", [128, W], BF16))
        esb = e_(nc.sbuf_tensor("esb", [128, W], F32))
        sln = e_(nc.sbuf_tensor("sln", [128, W], BF16))
        outt = e_(nc.sbuf_tensor("outt", [128, 2048], F32))
        cw = e_(nc.sbuf_tensor("cw", [128, 48 * 4], F32))
        av = e_(nc.sbuf_tensor("av", [16, 4], F32))
        ones_col = e_(nc.sbuf_tensor("onesc", [128, 1], F32))
        ones_colb = e_(nc.sbuf_tensor("onescb", [128, 1], BF16))
        ones_row = e_(nc.sbuf_tensor("onesr", [1, 128], F32))
        cb = e_(nc.sbuf_tensor("cb", [128, 4], F32))
        abg = e_(nc.sbuf_tensor("abg", [16, 2048], F32))
        ab2 = e_(nc.sbuf_tensor("ab2", [16, 2048], F32))
        pq = [e_(nc.psum_tensor("pq%d" % g, [128, 512], F32)) for g in range(5)]
        pr1 = e_(nc.psum_tensor("pr1", [1, 512], F32))
        pr2 = e_(nc.psum_tensor("pr2", [1, 512], F32))
        psA = pq[0]
        psB = pq[1]

        for o in range(48):
            pb.dl(lambda e, o=o: e.dma_start(out=cw[:, o * 4:o * 4 + 4],
                                             in_=convw[o * 128:(o + 1) * 128, :]))
        pb.dl(lambda e: e.dma_start(out=av[:, 0], in_=alog[:]))
        pb.dl(lambda e: e.dma_start(out=av[:, 1], in_=dtb[:]))
        pb.v(lambda e: e.memset(ones_col[:, :], 1.0))
        pb.v(lambda e: e.memset(ones_colb[:, :], 1.0))
        pb.v(lambda e: e.memset(ones_row[:1, :], 1.0))
        pb.v(lambda e: e.memset(cb[:, 0:1], 1e-5))
        pb.v(lambda e: e.memset(cb[:, 1:2], 1e-6))
        pb.v(lambda e: e.memset(cb[:, 2:3], 1.0))
        pb.bar()
        pb.s(lambda e: e.activation(out=av[:16, 2:3], in_=av[:16, 0:1], func=AF.Exp))
        pb.bar()
        pb.v(lambda e: e.tensor_scalar(out=av[:16, 2:3], in0=av[:16, 2:3],
                                       scalar1=-1.0, scalar2=None, op0=ALU.mult))
        pb.bar()
        for (off, w) in groups:
            for d in range(16):
                pb.dl(lambda e, d=d, off=off, w=w: e.dma_start(
                    out=xg[d][:, :w], in_=xh[d * 128:(d + 1) * 128, off:off + w]))
            pb.bar()
            emit_ln(pb, [xx[:, 0:w] for xx in xg], [(0, w)], pq[2], pq[3], pr1, pr2,
                    sq, mrow, rrow, scr, [xx[:, off:off + w] for xx in xlb],
                    ones_col, ones_row, cb[:1, 0:1])

        def stream_w(o, wap, ncols, buf):
            for dt in range(16):
                pb.dg(lambda e, dt=dt, o=o, wap=wap, ncols=ncols, buf=buf: e.dma_start(
                    out=wsb[buf][:, dt * ncols:(dt + 1) * ncols],
                    in_=wap[dt * 128:(dt + 1) * 128, o * ncols:(o + 1) * ncols]))

        stream_w(0, wqkv, 128, 0)
        pb.bar()
        for o in range(48):
            buf = o % 2
            if o < 47:
                stream_w(o + 1, wqkv, 128, 1 - buf)
            for gi, (off, w) in enumerate(groups):
                for dt in range(16):
                    pb.t(lambda e, dt=dt, off=off, w=w, gi=gi, buf=buf: e.matmul(
                        pq[gi][:, :w], wsb[buf][:, dt * 128:(dt + 1) * 128],
                        xlb[dt][:, off:off + w], start=(dt == 0), stop=(dt == 15)))
            pb.bar()
            for gi, (off, w) in enumerate(groups):
                pb.v(lambda e, gi=gi, off=off, w=w: e.tensor_copy(
                    out=qkvfull[:, off:off + w], in_=pq[gi][:, :w]))
            pb.bar()
            pb.v(lambda e, o=o: e.tensor_scalar(
                out=convf[:, 3:W], in0=qkvfull[:, 3:W],
                scalar1=cw[:, o * 4 + 3:o * 4 + 4], scalar2=None, op0=ALU.mult))
            pb.bar()
            for j in range(3):
                pb.v(lambda e, j=j, o=o: e.scalar_tensor_tensor(
                    out=convf[:, 3:W], in0=qkvfull[:, j:W - 3 + j],
                    scalar=cw[:, o * 4 + j:o * 4 + j + 1],
                    in1=convf[:, 3:W], op0=ALU.mult, op1=ALU.add))
                pb.bar()
            pb.s(lambda e: e.activation(out=esb[:, 3:W], in_=convf[:, 3:W],
                                        func=AF.Exp, scale=-1.0))
            pb.bar()
            pb.v(lambda e: e.tensor_scalar(out=esb[:, 3:W], in0=esb[:, 3:W],
                                           scalar1=1.0, scalar2=None, op0=ALU.add))
            pb.bar()
            pb.v(lambda e: e.reciprocal(out=esb[:, 3:W], in_=esb[:, 3:W]))
            pb.bar()
            pb.v(lambda e: e.tensor_mul(sln[:, 3:W], convf[:, 3:W], esb[:, 3:W]))
            pb.bar()
            if o < 16:
                h, dst, scale = o, qf, QS
            elif o < 32:
                h, dst, scale = o - 16, kf, 1.0
            else:
                h, dst, scale = o - 32, vf, None
            if scale is None:
                pb.v(lambda e: e.tensor_copy(out=outt[:, :], in_=sln[:, 3:W]))
                pb.bar()
            else:
                pb.s(lambda e: e.activation(out=esb[:, 3:W], in_=sln[:, 3:W],
                                            func=AF.Square))
                pb.bar()
                for g4 in range(4):
                    pb.t(lambda e, g4=g4: e.matmul(
                        pq[g4][:1, :], ones_col[:, 0:1], esb[:, 3 + g4 * 512:3 + (g4 + 1) * 512],
                        start=True, stop=True))
                pb.bar()
                for g4 in range(4):
                    pb.v(lambda e, g4=g4: e.tensor_copy(out=srow[:1, g4 * 512:(g4 + 1) * 512],
                                                        in_=pq[g4][:1, :]))
                pb.bar()
                pb.s(lambda e: e.activation(out=srow[:1, :], in_=srow[:1, :],
                                            func=AF.Ln, bias=cb[:1, 1:2]))
                pb.bar()
                pb.s(lambda e: e.activation(out=srow[:1, :], in_=srow[:1, :],
                                            func=AF.Exp, scale=-0.5))
                pb.bar()
                for g4 in range(4):
                    pb.t(lambda e, g4=g4: e.matmul(
                        pq[g4][:, :], ones_row[:1, :], srow[:1, g4 * 512:(g4 + 1) * 512],
                        start=True, stop=True))
                pb.bar()
                for g4 in range(4):
                    pb.v(lambda e, g4=g4, scale=scale: e.scalar_tensor_tensor(
                        out=outt[:, g4 * 512:(g4 + 1) * 512],
                        in0=sln[:, 3 + g4 * 512:3 + (g4 + 1) * 512], scalar=scale,
                        in1=pq[g4][:, :], op0=ALU.mult, op1=ALU.mult))
                pb.bar()
            pb.dg(lambda e, h=h, dst=dst: e.dma_start(out=dst[h, :, :], in_=outt[:, :]))
        pb.bar()
        stream_w(0, wgate, 128, 0)
        pb.bar()
        for o in range(16):
            buf = o % 2
            if o < 15:
                stream_w(o + 1, wgate, 128, 1 - buf)
            for gi, (off, w) in enumerate(groups[:4]):
                for dt in range(16):
                    pb.t(lambda e, dt=dt, off=off, w=w, gi=gi, buf=buf: e.matmul(
                        pq[gi][:, :w], wsb[buf][:, dt * 128:(dt + 1) * 128],
                        xlb[dt][:, 3 + off:3 + off + w], start=(dt == 0), stop=(dt == 15)))
            pb.bar()
            for gi in range(4):
                pb.v(lambda e, gi=gi: e.tensor_copy(
                    out=outt[:, gi * 512:(gi + 1) * 512], in_=pq[gi][:, :]))
            pb.bar()
            pb.dg(lambda e, o=o: e.dma_start(
                out=gatef[o * 128:(o + 1) * 128, :], in_=outt[:, :]))
        pb.bar()
        for which in range(2):
            wap = waa if which == 0 else wbb
            for dt in range(16):
                pb.dg(lambda e, dt=dt, wap=wap: e.dma_start(
                    out=wsb[0][:, dt * 16:(dt + 1) * 16],
                    in_=wap[dt * 128:(dt + 1) * 128, :]))
            pb.bar()
            for gi, (off, w) in enumerate(groups[:4]):
                for dt in range(16):
                    pb.t(lambda e, dt=dt, off=off, w=w, gi=gi: e.matmul(
                        pq[gi][:16, :w], wsb[0][:, dt * 16:(dt + 1) * 16],
                        xlb[dt][:, 3 + off:3 + off + w], start=(dt == 0), stop=(dt == 15)))
            pb.bar()
            if which == 0:
                for gi in range(4):
                    pb.s(lambda e, gi=gi: e.activation(
                        out=abg[:16, gi * 512:(gi + 1) * 512], in_=pq[gi][:16, :],
                        func=AF.Exp, bias=av[:16, 1:2]))
                pb.bar()
                pb.s(lambda e: e.activation(out=abg[:16, :], in_=abg[:16, :],
                                            func=AF.Ln, bias=cb[:16, 2:3]))
                pb.bar()
                pb.v(lambda e: e.tensor_scalar(
                    out=ab2[:16, :], in0=abg[:16, :], scalar1=av[:16, 2:3],
                    scalar2=None, op0=ALU.mult))
                pb.bar()
                pb.dg(lambda e: e.dma_start(out=gg[:, :], in_=ab2[:16, :]))
            else:
                for gi in range(4):
                    pb.s(lambda e, gi=gi: e.activation(
                        out=abg[:16, gi * 512:(gi + 1) * 512], in_=pq[gi][:16, :],
                        func=AF.Exp, scale=-1.0))
                pb.bar()
                pb.v(lambda e: e.tensor_scalar(out=abg[:16, :], in0=abg[:16, :],
                                               scalar1=1.0, scalar2=None, op0=ALU.add))
                pb.bar()
                pb.v(lambda e: e.reciprocal(out=abg[:16, :], in_=abg[:16, :]))
                pb.bar()
                pb.v(lambda e: e.tensor_scalar(out=ab2[:16, :], in0=abg[:16, :],
                                               scalar1=2.0, scalar2=None, op0=ALU.mult))
                pb.bar()
                pb.dg(lambda e: e.dma_start(out=bb[:, :], in_=ab2[:16, :]))
            pb.bar()
        pb.emit(nc)
    return nc


def build_B():
    nc = bass.Bass("TRN2", target_bir_lowering=False)
    qb = nc.dram_tensor("qb", [4, 128, TB], F32, kind="ExternalInput").ap()
    kb = nc.dram_tensor("kb", [4, 128, TB], F32, kind="ExternalInput").ap()
    vb = nc.dram_tensor("vb", [4, 128, TB], F32, kind="ExternalInput").ap()
    gb = nc.dram_tensor("gb", [4, 128, 64], F32, kind="ExternalInput").ap()
    bbeta = nc.dram_tensor("bbeta", [4, 128, 64], F32, kind="ExternalInput").ap()
    mui = nc.dram_tensor("mui", [128, 128], F32, kind="ExternalInput").ap()
    mls = nc.dram_tensor("mls", [128, 128], F32, kind="ExternalInput").ap()
    idn = nc.dram_tensor("idn", [128, 128], F32, kind="ExternalInput").ap()
    of = nc.dram_tensor("of", [4, 128, TB], F32, kind="ExternalOutput").ap()

    NI4 = 4
    NCH = B_CHUNKS
    pb = PB()
    with ExitStack() as st:
        e_ = st.enter_context
        sb = lambda n, f: e_(nc.sbuf_tensor(n, [128, f], F32))
        sb4 = lambda n: [e_(nc.sbuf_tensor(n + str(i), [128, 128], F32)) for i in range(NI4)]
        MUi = sb("MUi", 128)
        MLs = sb("MLs", 128)
        IDN = sb("IDN", 128)
        ones_col = sb("onesc", 1)
        ones_row = e_(nc.sbuf_tensor("onesr", [1, 128], F32))
        ggs = sb("ggs", 4 * 64)
        bbs = sb("bbs", 4 * 64)
        cum = sb("cum", 4 * 64)
        totr = e_(nc.sbuf_tensor("totr", [1, 4 * 64], F32))
        tb_ = sb("tb_", 4 * 64)
        gtb = sb("gtb", 4 * 64)
        ksc = sb("ksc", 4 * 64)
        gam = sb("gam", 4 * 64)
        bga = sb("bga", 4 * 64)
        Ssb = sb4("Ssb")
        Qf = [[sb4("Qf%d" % p)[i] for i in range(1)][0] for p in range(0)]
        QfB = [sb("QfA", 4 * 128), sb("QfBp", 4 * 128)]
        KfB = [sb("KfA", 4 * 128), sb("KfBp", 4 * 128)]
        VfB = [sb("VfA", 4 * 128), sb("VfBp", 4 * 128)]
        Kt = sb("Kt", 4 * 128)
        Vt = sb("Vt", 4 * 128)
        crow = e_(nc.sbuf_tensor("crow", [1, 4 * 128], F32))
        edarg = sb("edarg", 4 * 128)
        edtarg = sb("edtarg", 4 * 128)
        Ed = sb("Ed", 4 * 128)
        EdT = sb("EdT", 4 * 128)
        ATm = sb("ATm", 4 * 128)
        Dm = sb("Dm", 4 * 128)
        Dt = sb("Dt", 4 * 128)
        Pc = sb("Pc", 4 * 128)
        Pts = [sb("Pt%d" % j, 4 * 128) for j in range(6)]
        usb = sb("usb", 4 * 128)
        t1 = sb("t1", 4 * 128)
        qs0g = sb("qs0g", 4 * 128)
        osb = sb("osb", 4 * 128)
        Kh = sb("Kh", 4 * 128)
        cb = e_(nc.sbuf_tensor("cbB", [128, 1], F32))
        pA = e_(nc.psum_tensor("pA", [128, 512], F32))
        pB = e_(nc.psum_tensor("pB", [128, 512], F32))
        pC = e_(nc.psum_tensor("pC", [128, 512], F32))
        pD = e_(nc.psum_tensor("pD", [128, 512], F32))
        pE = e_(nc.psum_tensor("pE", [128, 512], F32))
        pR = e_(nc.psum_tensor("pR", [1, 512], F32))

        BL = lambda ap, i: ap[:, i * 128:(i + 1) * 128]
        B64 = lambda ap, i: ap[:, i * 64:(i + 1) * 64]
        CC = lambda ap, i, ci: ap[:, i * 64 + ci:i * 64 + ci + 1]

        pb.dl(lambda e: e.dma_start(out=MUi[:, :], in_=mui[:, :]))
        pb.dl(lambda e: e.dma_start(out=MLs[:, :], in_=mls[:, :]))
        pb.dl(lambda e: e.dma_start(out=IDN[:, :], in_=idn[:, :]))
        for i in range(NI4):
            pb.dl(lambda e, i=i: e.dma_start(out=B64(ggs, i), in_=gb[i, :, :]))
            pb.dl(lambda e, i=i: e.dma_start(out=B64(bbs, i), in_=bbeta[i, :, :]))
        pb.v(lambda e: e.memset(ones_col[:, :], 1.0))
        pb.v(lambda e: e.memset(ones_row[:1, :], 1.0))
        for i in range(NI4):
            pb.v(lambda e, i=i: e.memset(Ssb[i][:, :], 0.0))
        pb.bar()
        pb.t(lambda e: e.matmul(pA[:, :256], MUi[:, :], ggs[:, :], start=True, stop=True))
        pb.t(lambda e: e.matmul(pR[:1, :256], ones_col[:, 0:1], ggs[:, :], start=True, stop=True))
        pb.bar()
        pb.v(lambda e: e.tensor_copy(out=cum[:, :], in_=pA[:, :256]))
        pb.v(lambda e: e.tensor_copy(out=totr[:1, :], in_=pR[:1, :256]))
        pb.bar()
        pb.t(lambda e: e.matmul(pB[:, :256], ones_row[:1, :], totr[:1, :], start=True, stop=True))
        pb.bar()
        pb.v(lambda e: e.tensor_copy(out=tb_[:, :], in_=pB[:, :256]))
        pb.bar()
        pb.s(lambda e: e.activation(out=gtb[:, :], in_=tb_[:, :], func=AF.Exp))
        pb.v(lambda e: e.tensor_sub(ksc[:, :], tb_[:, :], cum[:, :]))
        pb.bar()
        pb.s(lambda e: e.activation(out=gam[:, :], in_=cum[:, :], func=AF.Exp))
        pb.s(lambda e: e.activation(out=ksc[:, :], in_=ksc[:, :], func=AF.Exp))
        pb.bar()
        pb.v(lambda e: e.tensor_mul(bga[:, :], bbs[:, :], gam[:, :]))
        pb.bar()
        for i in range(NI4):
            pb.dl(lambda e, i=i: e.dma_start(out=BL(QfB[0], i), in_=qb[i, :, 0:128]))
            pb.dl(lambda e, i=i: e.dma_start(out=BL(KfB[0], i), in_=kb[i, :, 0:128]))
            pb.dl(lambda e, i=i: e.dma_start(out=BL(VfB[0], i), in_=vb[i, :, 0:128]))
        pb.bar()

        for ci in range(NCH):
            par = ci % 2
            Qc, Kc, Vc = QfB[par], KfB[par], VfB[par]
            if ci < NCH - 1:
                for i in range(NI4):
                    pb.dl(lambda e, i=i, ci=ci, par=par: e.dma_start(
                        out=BL(QfB[1 - par], i), in_=qb[i, :, (ci + 1) * 128:(ci + 2) * 128]))
                    pb.dl(lambda e, i=i, ci=ci, par=par: e.dma_start(
                        out=BL(KfB[1 - par], i), in_=kb[i, :, (ci + 1) * 128:(ci + 2) * 128]))
                    pb.dl(lambda e, i=i, ci=ci, par=par: e.dma_start(
                        out=BL(VfB[1 - par], i), in_=vb[i, :, (ci + 1) * 128:(ci + 2) * 128]))
            for i in range(NI4):
                pb.t(lambda e, i=i, Kc=Kc: e.transpose(BL(pA, i), BL(Kc, i), IDN[:, :]))
                pb.t(lambda e, i=i, Vc=Vc: e.transpose(BL(pB, i), BL(Vc, i), IDN[:, :]))
                pb.t(lambda e, i=i, ci=ci: e.transpose(pR[:1, i * 128:i * 128 + 128],
                                                       CC(cum, i, ci), IDN[:, :]))
                pb.t(lambda e, i=i, Kc=Kc, Qc=Qc: e.matmul(BL(pD, i), BL(Kc, i), BL(Qc, i), start=True, stop=True))
                pb.t(lambda e, i=i, Kc=Kc: e.matmul(BL(pE, i), BL(Kc, i), BL(Kc, i), start=True, stop=True))
            pb.bar()
            pb.v(lambda e: e.tensor_copy(out=Kt[:, :], in_=pA[:, :]))
            pb.v(lambda e: e.tensor_copy(out=Vt[:, :], in_=pB[:, :]))
            pb.v(lambda e: e.tensor_copy(out=crow[:1, :], in_=pR[:1, :]))
            pb.bar()
            pb.t(lambda e: e.matmul(pA[:, :], ones_row[:1, :], crow[:1, :], start=True, stop=True))
            for i in range(NI4):
                pb.t(lambda e, i=i, Kc=Kc: e.matmul(BL(pB, i), BL(Kc, i), Ssb[i][:, :], start=True, stop=True))
                pb.t(lambda e, i=i, Qc=Qc: e.matmul(BL(pC, i), BL(Qc, i), Ssb[i][:, :], start=True, stop=True))
            pb.bar()
            for i in range(NI4):
                pb.v(lambda e, i=i, ci=ci: e.tensor_scalar(
                    out=BL(edarg, i), in0=BL(pA, i), scalar1=-1.0,
                    scalar2=CC(cum, i, ci), op0=ALU.mult, op1=ALU.add))
                pb.v(lambda e, i=i, ci=ci: e.tensor_scalar(
                    out=BL(edtarg, i), in0=BL(pA, i), scalar1=CC(cum, i, ci),
                    scalar2=None, op0=ALU.subtract))
                pb.v(lambda e, i=i, ci=ci: e.tensor_scalar(
                    out=BL(t1, i), in0=BL(pB, i), scalar1=CC(bga, i, ci),
                    scalar2=None, op0=ALU.mult))
                pb.v(lambda e, i=i, ci=ci: e.tensor_scalar(
                    out=BL(qs0g, i), in0=BL(pC, i), scalar1=CC(gam, i, ci),
                    scalar2=None, op0=ALU.mult))
            pb.bar()
            pb.v(lambda e: e.tensor_scalar_min(out=edarg[:, :], in0=edarg[:, :], scalar1=0.0))
            pb.v(lambda e: e.tensor_scalar_min(out=edtarg[:, :], in0=edtarg[:, :], scalar1=0.0))
            pb.bar()
            pb.s(lambda e: e.activation(out=Ed[:, :], in_=edarg[:, :], func=AF.Exp))
            pb.s(lambda e: e.activation(out=EdT[:, :], in_=edtarg[:, :], func=AF.Exp))
            for i in range(NI4):
                pb.v(lambda e, i=i, ci=ci: e.scalar_tensor_tensor(
                    out=BL(usb, i), in0=BL(Vt, i), scalar=CC(bbs, i, ci),
                    in1=BL(t1, i), op0=ALU.mult, op1=ALU.subtract))
                pb.v(lambda e, i=i, ci=ci: e.tensor_scalar(
                    out=BL(Kh, i), in0=BL(Kt, i), scalar1=CC(ksc, i, ci),
                    scalar2=None, op0=ALU.mult))
            pb.bar()
            for i in range(NI4):
                pb.v(lambda e, i=i: e.tensor_mul(BL(Ed, i), BL(Ed, i), MLs[:, :]))
                pb.v(lambda e, i=i: e.tensor_mul(BL(EdT, i), BL(EdT, i), MUi[:, :]))
            pb.bar()
            for i in range(NI4):
                pb.v(lambda e, i=i: e.tensor_mul(BL(ATm, i), BL(EdT, i), BL(pD, i)))
                pb.v(lambda e, i=i, ci=ci: e.scalar_tensor_tensor(
                    out=BL(Dm, i), in0=BL(pE, i), scalar=CC(bbs, i, ci),
                    in1=BL(Ed, i), op0=ALU.mult, op1=ALU.mult))
            pb.bar()
            for i in range(NI4):
                pb.t(lambda e, i=i: e.transpose(BL(pA, i), BL(Dm, i), IDN[:, :]))
            pb.bar()
            pb.v(lambda e: e.tensor_copy(out=Dt[:, :], in_=pA[:, :]))
            pb.bar()
            srcT, src = Dt, Dm
            for j in range(6):
                for i in range(NI4):
                    pb.t(lambda e, i=i, sT=srcT, s_=src: e.matmul(BL(pA, i), BL(sT, i), BL(s_, i), start=True, stop=True))
                    pb.t(lambda e, i=i, sT=srcT, s_=src: e.matmul(BL(pB, i), BL(s_, i), BL(sT, i), start=True, stop=True))
                pb.bar()
                pb.v(lambda e: e.tensor_copy(out=Pc[:, :], in_=pA[:, :]))
                pb.v(lambda e, j=j: e.tensor_copy(out=Pts[j][:, :], in_=pB[:, :]))
                pb.bar()
                srcT, src = Pts[j], Pc
            for j in [5, 4, 3, 2, 1, 0]:
                for i in range(NI4):
                    pb.t(lambda e, i=i, j=j: e.matmul(BL(pA, i), BL(Pts[j], i), BL(usb, i), start=True, stop=True))
                pb.bar()
                pb.v(lambda e: e.tensor_add(usb[:, :], usb[:, :], pA[:, :]))
                pb.bar()
            for i in range(NI4):
                pb.t(lambda e, i=i: e.matmul(BL(pA, i), BL(Dt, i), BL(usb, i), start=True, stop=True))
            pb.bar()
            pb.v(lambda e: e.tensor_sub(usb[:, :], usb[:, :], pA[:, :]))
            pb.bar()
            for i in range(NI4):
                pb.t(lambda e, i=i: e.matmul(BL(pA, i), BL(ATm, i), BL(usb, i), start=True, stop=True))
                pb.t(lambda e, i=i: e.matmul(BL(pB, i), BL(Kh, i), BL(usb, i), start=True, stop=True))
            pb.bar()
            pb.v(lambda e: e.tensor_add(osb[:, :], qs0g[:, :], pA[:, :]))
            for i in range(NI4):
                pb.v(lambda e, i=i, ci=ci: e.scalar_tensor_tensor(
                    out=Ssb[i][:, :], in0=Ssb[i][:, :], scalar=CC(gtb, i, ci),
                    in1=BL(pB, i), op0=ALU.mult, op1=ALU.add))
            pb.bar()
            for i in range(NI4):
                pb.t(lambda e, i=i: e.transpose(BL(pC, i), BL(osb, i), IDN[:, :]))
            pb.bar()
            pb.v(lambda e: e.tensor_copy(out=t1[:, :], in_=pC[:, :]))
            pb.bar()
            for i in range(NI4):
                pb.dg(lambda e, i=i, ci=ci: e.dma_start(
                    out=of[i, :, ci * 128:(ci + 1) * 128], in_=BL(t1, i)))
        pb.bar()
        pb.emit(nc)
    return nc


def build_C1():
    nc = bass.Bass("TRN2", target_bir_lowering=False)
    xsh = nc.dram_tensor("xsh", [D, TPC], F32, kind="ExternalInput").ap()
    osh = nc.dram_tensor("osh", [D, TPC], F32, kind="ExternalInput").ap()
    gsh = nc.dram_tensor("gsh", [D, TPC], F32, kind="ExternalInput").ap()
    onw = nc.dram_tensor("onw", [128], F32, kind="ExternalInput").ap()
    wo = nc.dram_tensor("wo", [D, D], F32, kind="ExternalInput").ap()
    wra = nc.dram_tensor("wra", [D, 8], F32, kind="ExternalInput").ap()
    bra = nc.dram_tensor("bra", [8], F32, kind="ExternalInput").ap()
    ffa = nc.dram_tensor("ffa", [8, D, 64], F32, kind="ExternalInput").ap()
    ffb = nc.dram_tensor("ffb", [8, 64, D], F32, kind="ExternalInput").ap()
    idn = nc.dram_tensor("idn", [128, 128], F32, kind="ExternalInput").ap()
    x2 = nc.dram_tensor("x2", [D, TPC], F32, kind="ExternalOutput").ap()

    pb = PB()
    with ExitStack() as st:
        e_ = st.enter_context
        xg = [e_(nc.sbuf_tensor("xsb%d" % d, [128, 512], F32)) for d in range(16)]
        on = [e_(nc.sbuf_tensor("on%d" % h, [128, 512], BF16)) for h in range(16)]
        xln = [e_(nc.sbuf_tensor("xl%d" % d, [128, 512], BF16)) for d in range(16)]
        hprime = [e_(nc.sbuf_tensor("hp%d" % ee, [64, 512], BF16)) for ee in range(8)]
        ot4 = e_(nc.sbuf_tensor("ot4", [128, 2048], F32))
        gt4 = e_(nc.sbuf_tensor("gt4", [128, 2048], F32))
        og4 = e_(nc.sbuf_tensor("og4", [128, 2048], F32))
        sq = e_(nc.sbuf_tensor("sq", [128, 4 * 512], F32))
        scr = e_(nc.sbuf_tensor("scr", [1, 512], F32))
        mrow = e_(nc.sbuf_tensor("mrow", [1, 512], F32))
        rrow = e_(nc.sbuf_tensor("rrow", [1, 512], F32))
        srow = e_(nc.sbuf_tensor("srow", [1, 2048], F32))
        wsb = [e_(nc.sbuf_tensor("wsb%d" % i, [128, 2048], BF16)) for i in range(2)]
        IDN = e_(nc.sbuf_tensor("IDN", [128, 128], F32))
        onwc = e_(nc.sbuf_tensor("onwc", [128, 1], F32))
        brc = e_(nc.sbuf_tensor("brc", [8, 1], F32))
        wrs = e_(nc.sbuf_tensor("wrs", [128, 16 * 8], BF16))
        ffas = e_(nc.sbuf_tensor("ffas", [128, 16 * 64], BF16))
        ffbt = e_(nc.sbuf_tensor("ffbt", [64, 8 * 128], BF16))
        lg = e_(nc.sbuf_tensor("lg", [8, 512], F32))
        lgt = e_(nc.sbuf_tensor("lgt", [128, 4 * 8], F32))
        rmx = e_(nc.sbuf_tensor("rmx", [128, 4], F32))
        exr = e_(nc.sbuf_tensor("exr", [128, 4 * 8], F32))
        ssr = e_(nc.sbuf_tensor("ssr", [128, 4], F32))
        coefT = e_(nc.sbuf_tensor("coefT", [128, 4 * 8], F32))
        coef = e_(nc.sbuf_tensor("coef", [8, 512], F32))
        crow = e_(nc.sbuf_tensor("crw", [1, 512], F32))
        bcsb = e_(nc.sbuf_tensor("bcsb", [64, 512], F32))
        ones_col = e_(nc.sbuf_tensor("onesc", [128, 1], F32))
        ones_row = e_(nc.sbuf_tensor("onesr", [1, 128], F32))
        cb = e_(nc.sbuf_tensor("cb", [128, 4], F32))
        psA = e_(nc.psum_tensor("psA", [128, 512], F32))
        psB = e_(nc.psum_tensor("psB", [128, 512], F32))
        psC = e_(nc.psum_tensor("psC", [128, 512], F32))
        pr1 = e_(nc.psum_tensor("pr1", [1, 512], F32))
        pr2 = e_(nc.psum_tensor("pr2", [1, 512], F32))
        pr3 = e_(nc.psum_tensor("pr3", [1, 512], F32))
        pr4 = e_(nc.psum_tensor("pr4", [1, 512], F32))
        pt8 = e_(nc.psum_tensor("pt8", [128, 512], F32))
        prs = [pr1, pr2, pr3, pr4]

        pb.dl(lambda e: e.dma_start(out=IDN[:, :], in_=idn[:, :]))
        pb.dl(lambda e: e.dma_start(out=onwc[:, 0], in_=onw[:]))
        pb.dl(lambda e: e.dma_start(out=brc[:8, 0], in_=bra[:]))
        for dt in range(16):
            pb.dg(lambda e, dt=dt: e.dma_start(out=wrs[:, dt * 8:(dt + 1) * 8],
                                               in_=wra[dt * 128:(dt + 1) * 128, :]))
        pb.v(lambda e: e.memset(ones_col[:, :], 1.0))
        pb.v(lambda e: e.memset(ones_row[:1, :], 1.0))
        pb.v(lambda e: e.memset(cb[:, 0:1], 1e-5))
        pb.bar()

        for gi in range(4):
            g0 = gi * 512
            for d in range(16):
                pb.dl(lambda e, d=d, g0=g0: e.dma_start(
                    out=xg[d][:, :], in_=xsh[d * 128:(d + 1) * 128, g0:g0 + 512]))
            pb.bar()
            for h0 in range(0, 16, 4):
                for dh in range(4):
                    pb.dl(lambda e, h=h0 + dh, dh=dh, g0=g0: e.dma_start(
                        out=ot4[:, dh * 512:(dh + 1) * 512],
                        in_=osh[h * 128:(h + 1) * 128, g0:g0 + 512]))
                    pb.dl(lambda e, h=h0 + dh, dh=dh, g0=g0: e.dma_start(
                        out=gt4[:, dh * 512:(dh + 1) * 512],
                        in_=gsh[h * 128:(h + 1) * 128, g0:g0 + 512]))
                pb.bar()
                pb.s(lambda e: e.activation(out=og4[:, :], in_=gt4[:, :], func=AF.Exp, scale=-1.0))
                pb.bar()
                pb.v(lambda e: e.tensor_scalar(out=og4[:, :], in0=og4[:, :],
                                               scalar1=1.0, scalar2=None, op0=ALU.add))
                pb.bar()
                pb.v(lambda e: e.reciprocal(out=og4[:, :], in_=og4[:, :]))
                pb.bar()
                pb.v(lambda e: e.tensor_mul(og4[:, :], og4[:, :], gt4[:, :]))
                pb.bar()
                pb.v(lambda e: e.tensor_mul(og4[:, :], og4[:, :], ot4[:, :]))
                pb.bar()
                pb.s(lambda e: e.activation(out=sq[:, :], in_=og4[:, :], func=AF.Square))
                pb.bar()
                for dh in range(4):
                    pb.t(lambda e, dh=dh: e.matmul(prs[dh][:1, :], ones_col[:, 0:1],
                                                   sq[:, dh * 512:(dh + 1) * 512],
                                                   start=True, stop=True))
                pb.bar()
                for dh in range(4):
                    pb.v(lambda e, dh=dh: e.tensor_copy(out=srow[:1, dh * 512:(dh + 1) * 512],
                                                        in_=prs[dh][:1, :]))
                pb.bar()
                pb.s(lambda e: e.activation(out=srow[:1, :], in_=srow[:1, :], func=AF.Ln,
                                            scale=1.0 / 128.0, bias=cb[:1, 0:1]))
                pb.bar()
                pb.s(lambda e: e.activation(out=srow[:1, :], in_=srow[:1, :], func=AF.Exp,
                                            scale=-0.5))
                pb.bar()
                for dh2 in range(0, 4, 2):
                    pb.t(lambda e, dh=dh2: e.matmul(psA[:, :], ones_row[:1, :],
                                                    srow[:1, dh * 512:(dh + 1) * 512],
                                                    start=True, stop=True))
                    pb.t(lambda e, dh=dh2 + 1: e.matmul(psB[:, :], ones_row[:1, :],
                                                        srow[:1, dh * 512:(dh + 1) * 512],
                                                        start=True, stop=True))
                    pb.bar()
                    pb.v(lambda e, dh=dh2, h0=h0: e.scalar_tensor_tensor(
                        out=on[h0 + dh][:, :], in0=og4[:, dh * 512:(dh + 1) * 512],
                        scalar=onwc[:, 0:1], in1=psA[:, :], op0=ALU.mult, op1=ALU.mult))
                    pb.v(lambda e, dh=dh2 + 1, h0=h0: e.scalar_tensor_tensor(
                        out=on[h0 + dh][:, :], in0=og4[:, dh * 512:(dh + 1) * 512],
                        scalar=onwc[:, 0:1], in1=psB[:, :], op0=ALU.mult, op1=ALU.mult))
                    pb.bar()
            for dt in range(16):
                pb.dg(lambda e, dt=dt: e.dma_start(
                    out=wsb[0][:, dt * 128:(dt + 1) * 128],
                    in_=wo[dt * 128:(dt + 1) * 128, 0:128]))
            pb.bar()
            for dm in range(16):
                buf = dm % 2
                if dm < 15:
                    for dt in range(16):
                        pb.dg(lambda e, dt=dt, dm=dm, buf=buf: e.dma_start(
                            out=wsb[1 - buf][:, dt * 128:(dt + 1) * 128],
                            in_=wo[dt * 128:(dt + 1) * 128, (dm + 1) * 128:(dm + 2) * 128]))
                for dt in range(16):
                    pb.t(lambda e, dm=dm, dt=dt, buf=buf: e.matmul(
                        psB[:, :], wsb[buf][:, dt * 128:(dt + 1) * 128], on[dt][:, :],
                        start=(dt == 0), stop=(dt == 15)))
                pb.bar()
                pb.v(lambda e, dm=dm: e.tensor_add(xg[dm][:, :], xg[dm][:, :], psB[:, :]))
                pb.bar()
            emit_ln(pb, xg, [(0, 512)], psA, psB, pr1, pr2, sq, mrow, rrow, scr,
                    xln, ones_col, ones_row, cb[:1, 0:1])
            for dt in range(16):
                pb.t(lambda e, dt=dt: e.matmul(
                    psA[:8, :], wrs[:, dt * 8:(dt + 1) * 8], xln[dt][:, :],
                    start=(dt == 0), stop=(dt == 15)))
            pb.bar()
            pb.v(lambda e: e.tensor_scalar(out=lg[:8, :], in0=psA[:8, :],
                                           scalar1=brc[:8, 0:1], scalar2=None, op0=ALU.add))
            pb.bar()
            for sub in range(4):
                pb.t(lambda e, sub=sub: e.transpose(pt8[:, sub * 8:sub * 8 + 8],
                                                    lg[:8, sub * 128:(sub + 1) * 128],
                                                    IDN[:8, :8]))
            pb.bar()
            pb.v(lambda e: e.tensor_copy(out=lgt[:, :], in_=pt8[:, :32]))
            pb.bar()
            for sub in range(4):
                pb.v(lambda e, sub=sub: e.reduce_max(out=rmx[:, sub:sub + 1],
                                                     in_=lgt[:, sub * 8:(sub + 1) * 8], axis=AX.X))
            pb.bar()
            pb.v(lambda e: e.tensor_scalar(out=rmx[:, :], in0=rmx[:, :],
                                           scalar1=-1.0, scalar2=None, op0=ALU.mult))
            pb.bar()
            for sub in range(4):
                pb.s(lambda e, sub=sub: e.activation(out=exr[:, sub * 8:(sub + 1) * 8],
                                                     in_=lgt[:, sub * 8:(sub + 1) * 8],
                                                     func=AF.Exp, bias=rmx[:, sub:sub + 1]))
            pb.bar()
            for sub in range(4):
                pb.v(lambda e, sub=sub: e.reduce_sum(out=ssr[:, sub:sub + 1],
                                                     in_=exr[:, sub * 8:(sub + 1) * 8], axis=AX.X))
            pb.bar()
            pb.v(lambda e: e.reciprocal(out=ssr[:, :], in_=ssr[:, :]))
            pb.bar()
            pb.v(lambda e: e.tensor_scalar(out=ssr[:, :], in0=ssr[:, :],
                                           scalar1=1.0 / 64.0, scalar2=None, op0=ALU.mult))
            pb.bar()
            for sub in range(4):
                pb.v(lambda e, sub=sub: e.tensor_scalar(
                    out=coefT[:, sub * 8:(sub + 1) * 8], in0=exr[:, sub * 8:(sub + 1) * 8],
                    scalar1=1.0, scalar2=ssr[:, sub:sub + 1], op0=ALU.is_ge, op1=ALU.mult))
            pb.bar()
            for sub in range(4):
                pb.t(lambda e, sub=sub: e.transpose(pt8[:8, sub * 128:(sub + 1) * 128],
                                                    coefT[:, sub * 8:(sub + 1) * 8], IDN[:, :]))
            pb.bar()
            pb.v(lambda e: e.tensor_copy(out=coef[:8, :], in_=pt8[:8, :]))
            pb.bar()
            for ee in range(8):
                pb.dl(lambda e, ee=ee: e.dma_start(out=crow[:1, :], in_=coef[ee:ee + 1, :]))
                for dt in range(16):
                    pb.dg(lambda e, ee=ee, dt=dt: e.dma_start(
                        out=ffas[:, dt * 64:(dt + 1) * 64],
                        in_=ffa[ee, dt * 128:(dt + 1) * 128, :]))
                pb.bar()
                for dt in range(16):
                    pb.t(lambda e, dt=dt: e.matmul(
                        psA[:64, :], ffas[:, dt * 64:(dt + 1) * 64],
                        xln[dt][:, :], start=(dt == 0), stop=(dt == 15)))
                pb.t(lambda e: e.matmul(psB[:64, :], ones_row[:1, :64], crow[:1, :],
                                        start=True, stop=True))
                pb.bar()
                pb.v(lambda e: e.tensor_copy(out=bcsb[:64, :], in_=psB[:64, :]))
                pb.bar()
                pb.v(lambda e, ee=ee: e.tensor_mul(hprime[ee][:64, :], psA[:64, :], bcsb[:64, :]))
                pb.bar()
            for dm in range(16):
                for ee in range(8):
                    pb.dg(lambda e, ee=ee, dm=dm: e.dma_start(
                        out=ffbt[:64, ee * 128:(ee + 1) * 128],
                        in_=ffb[ee, :, dm * 128:(dm + 1) * 128]))
                pb.bar()
                for ee in range(8):
                    pb.t(lambda e, ee=ee: e.matmul(
                        psC[:, :], ffbt[:64, ee * 128:(ee + 1) * 128],
                        hprime[ee][:64, :], start=(ee == 0), stop=(ee == 7)))
                pb.bar()
                pb.v(lambda e, dm=dm: e.tensor_add(xg[dm][:, :], xg[dm][:, :], psC[:, :]))
                pb.bar()
                pb.dg(lambda e, dm=dm, g0=g0: e.dma_start(
                    out=x2[dm * 128:(dm + 1) * 128, g0:g0 + 512], in_=xg[dm][:, :]))
            pb.bar()
        pb.emit(nc)
    return nc


def build_C2():
    nc = bass.Bass("TRN2", target_bir_lowering=False)
    x2 = nc.dram_tensor("x2", [D, TPC], F32, kind="ExternalInput").ap()
    wq = nc.dram_tensor("wq", [NH, D, 128], F32, kind="ExternalInput").ap()
    wk = nc.dram_tensor("wk", [NH, D, 128], F32, kind="ExternalInput").ap()
    wv = nc.dram_tensor("wv", [NH, D, 128], F32, kind="ExternalInput").ap()
    q1 = nc.dram_tensor("q1", [NH, 128, TPC], BF16, kind="ExternalOutput").ap()
    k1 = nc.dram_tensor("k1", [NH, 128, TPC], BF16, kind="ExternalOutput").ap()
    v1 = nc.dram_tensor("v1", [NH, 128, TPC], BF16, kind="ExternalOutput").ap()

    pb = PB()
    with ExitStack() as st:
        e_ = st.enter_context
        xg = [e_(nc.sbuf_tensor("xsb%d" % d, [128, 512], F32)) for d in range(16)]
        xlb = [e_(nc.sbuf_tensor("xl%d" % d, [128, TPC], BF16)) for d in range(16)]
        sq = e_(nc.sbuf_tensor("sq", [128, 4 * 512], F32))
        scr = e_(nc.sbuf_tensor("scr", [1, 512], F32))
        mrow = e_(nc.sbuf_tensor("mrow", [1, 512], F32))
        rrow = e_(nc.sbuf_tensor("rrow", [1, 512], F32))
        wsb = e_(nc.sbuf_tensor("wsb", [128, 2048], BF16))
        outb = e_(nc.sbuf_tensor("outb", [128, 512], BF16))
        ones_col = e_(nc.sbuf_tensor("onesc", [128, 1], F32))
        ones_row = e_(nc.sbuf_tensor("onesr", [1, 128], F32))
        cb = e_(nc.sbuf_tensor("cb", [128, 4], F32))
        psA = e_(nc.psum_tensor("psA", [128, 512], F32))
        psB = e_(nc.psum_tensor("psB", [128, 512], F32))
        pr1 = e_(nc.psum_tensor("pr1", [1, 512], F32))
        pr2 = e_(nc.psum_tensor("pr2", [1, 512], F32))

        pb.v(lambda e: e.memset(ones_col[:, :], 1.0))
        pb.v(lambda e: e.memset(ones_row[:1, :], 1.0))
        pb.v(lambda e: e.memset(cb[:, 0:1], 1e-5))
        pb.v(lambda e: e.memset(cb[:, 1:2], 1e-6))
        pb.v(lambda e: e.memset(cb[:, 2:3], 1.0))
        pb.v(lambda e: e.memset(cb[:, 3:4], -16.0))
        pb.bar()
        for gi in range(4):
            g0 = gi * 512
            for d in range(16):
                pb.dl(lambda e, d=d, g0=g0: e.dma_start(
                    out=xg[d][:, :], in_=x2[d * 128:(d + 1) * 128, g0:g0 + 512]))
            pb.bar()
            emit_ln(pb, xg, [(0, 512)], psA, psB, pr1, pr2, sq, mrow, rrow, scr,
                    [xx[:, g0:g0 + 512] for xx in xlb], ones_col, ones_row, cb[:1, 0:1])
        for h in range(NH):
            for (wap, dst, scale) in ((wq, q1, QS), (wk, k1, None), (wv, v1, None)):
                for dt in range(16):
                    pb.dg(lambda e, h=h, dt=dt, wap=wap: e.dma_start(
                        out=wsb[:, dt * 128:(dt + 1) * 128],
                        in_=wap[h, dt * 128:(dt + 1) * 128, :]))
                pb.bar()
                for gi in range(4):
                    g0 = gi * 512
                    for dt in range(16):
                        pb.t(lambda e, dt=dt, g0=g0: e.matmul(
                            psA[:, :], wsb[:, dt * 128:(dt + 1) * 128],
                            xlb[dt][:, g0:g0 + 512], start=(dt == 0), stop=(dt == 15)))
                    pb.bar()
                    if scale is None:
                        pb.v(lambda e: e.tensor_copy(out=outb[:, :], in_=psA[:, :]))
                    else:
                        pb.v(lambda e, scale=scale: e.tensor_scalar(
                            out=outb[:, :], in0=psA[:, :], scalar1=scale,
                            scalar2=None, op0=ALU.mult))
                    pb.bar()
                    pb.dg(lambda e, h=h, g0=g0, dst=dst: e.dma_start(
                        out=dst[h, :, g0:g0 + 512], in_=outb[:, :]))
                pb.bar()
        pb.emit(nc)
    return nc


def build_E():
    nc = bass.Bass("TRN2", target_bir_lowering=False)
    q1 = nc.dram_tensor("q1", [4, 128, TB], BF16, kind="ExternalInput").ap()
    k1 = nc.dram_tensor("k1", [4, 128, TB], BF16, kind="ExternalInput").ap()
    v1 = nc.dram_tensor("v1", [4, 128, TB], BF16, kind="ExternalInput").ap()
    mneg = nc.dram_tensor("mneg", [128, 128], F32, kind="ExternalInput").ap()
    idn = nc.dram_tensor("idn", [128, 128], F32, kind="ExternalInput").ap()
    zf = nc.dram_tensor("zf", [4, 128, TB], BF16, kind="ExternalOutput").ap()

    pb = PB()
    with ExitStack() as st:
        e_ = st.enter_context
        qs = [e_(nc.sbuf_tensor("qs%d" % u, [128, TB], BF16)) for u in range(2)]
        ks = [e_(nc.sbuf_tensor("ks%d" % u, [128, TB], BF16)) for u in range(2)]
        vs = [e_(nc.sbuf_tensor("vs%d" % u, [128, TB], BF16)) for u in range(2)]
        vt = [e_(nc.sbuf_tensor("vt%d" % u, [128, TB], BF16)) for u in range(2)]
        zfm = [e_(nc.sbuf_tensor("zfm%d" % u, [128, TB], BF16)) for u in range(2)]
        PT = [[e_(nc.sbuf_tensor("PT%d_%d" % (u, i), [128, 512], BF16)) for i in range(2)]
              for u in range(2)]
        MNb = e_(nc.sbuf_tensor("MNb", [128, 128], BF16))
        IDN = e_(nc.sbuf_tensor("IDN", [128, 128], F32))
        IDNb = e_(nc.sbuf_tensor("IDNb", [128, 128], BF16))
        onecb = e_(nc.sbuf_tensor("onecb", [128, 1], BF16))
        cb = e_(nc.sbuf_tensor("cb", [128, 4], F32))
        lacc = e_(nc.sbuf_tensor("lacc", [1, 256], F32))
        zacc = e_(nc.sbuf_tensor("zacc", [128, 256], F32))
        lcol = e_(nc.sbuf_tensor("lcol", [128, 2], F32))
        zsc = e_(nc.sbuf_tensor("zsc", [128, 256], F32))
        pS = [[e_(nc.psum_tensor("pS%d_%d" % (u, i), [128, 512], F32)) for i in range(2)]
              for u in range(2)]
        pZ = e_(nc.psum_tensor("pZ", [128, 256], F32))
        pL = e_(nc.psum_tensor("pL", [1, 256], F32))
        pT = e_(nc.psum_tensor("pT", [128, 256], F32))
        pTb = e_(nc.psum_tensor("pTb", [128, 512], BF16))

        pb.dl(lambda e: e.dma_start(out=IDN[:, :], in_=idn[:, :]))
        pb.dg(lambda e: e.dma_start(out=MNb[:, :], in_=mneg[:, :]))
        pb.dg(lambda e: e.dma_start(out=IDNb[:, :], in_=idn[:, :]))
        pb.v(lambda e: e.memset(onecb[:, :], 1.0))
        pb.v(lambda e: e.memset(cb[:, 3:4], -16.0))
        pb.bar()
        for pair in range(2):
            for u in range(2):
                gi = 2 * pair + u
                pb.dl(lambda e, gi=gi, u=u: e.dma_start(out=qs[u][:, :], in_=q1[gi, :, :]))
                pb.dl(lambda e, gi=gi, u=u: e.dma_start(out=ks[u][:, :], in_=k1[gi, :, :]))
                pb.dl(lambda e, gi=gi, u=u: e.dma_start(out=vs[u][:, :], in_=v1[gi, :, :]))
            pb.bar()
            for j0 in range(0, 64, 2):
                for u in range(2):
                    for dj in range(2):
                        pb.t(lambda e, u=u, j=j0 + dj, dj=dj: e.transpose(
                            pTb[:, (u * 2 + dj) * 128:(u * 2 + dj + 1) * 128],
                            vs[u][:, j * 128:(j + 1) * 128], IDNb[:, :]))
                pb.bar()
                for u in range(2):
                    for dj in range(2):
                        pb.s(lambda e, u=u, j=j0 + dj, dj=dj: e.copy(
                            out=vt[u][:, j * 128:(j + 1) * 128],
                            in_=pTb[:, (u * 2 + dj) * 128:(u * 2 + dj + 1) * 128]))
                pb.bar()
            for qb in range(64):
                nch = qb // 4 + 1
                nbtot = qb + 1

                def lpv(u, c):
                    jlo = 4 * c
                    jhi = min(4 * c + 3, qb)
                    nb = jhi - jlo + 1
                    par = c % 2
                    for bi in range(nb):
                        pb.t(lambda e, u=u, par=par, bi=bi, nb=nb: e.matmul(
                            pL[:1, u * 128:(u + 1) * 128], onecb[:, 0:1],
                            PT[u][par][:, bi * 128:(bi + 1) * 128],
                            start=(bi == 0), stop=(bi == nb - 1),
                            skip_group_check=True))
                    for bi in range(nb):
                        pb.t(lambda e, u=u, par=par, bi=bi, jlo=jlo, nb=nb: e.matmul(
                            pZ[:, u * 128:(u + 1) * 128],
                            PT[u][par][:, bi * 128:(bi + 1) * 128],
                            vt[u][:, (jlo + bi) * 128:(jlo + bi + 1) * 128],
                            start=(bi == 0), stop=(bi == nb - 1),
                            skip_group_check=True))

                def scores(u, c):
                    jlo = 4 * c
                    jhi = min(4 * c + 3, qb)
                    nb = jhi - jlo + 1
                    par = c % 2
                    for j in range(jlo, jhi + 1):
                        pb.t(lambda e, u=u, j=j, jlo=jlo, par=par, qb=qb: e.matmul(
                            pS[u][par][:, (j - jlo) * 128:(j - jlo + 1) * 128],
                            ks[u][:, j * 128:(j + 1) * 128], qs[u][:, qb * 128:(qb + 1) * 128],
                            start=True, stop=(j != qb), skip_group_check=True))
                    if jhi == qb:
                        pb.t(lambda e, u=u, par=par, nb=nb: e.matmul(
                            pS[u][par][:, (nb - 1) * 128:nb * 128], IDNb[:, :], MNb[:, :],
                            start=False, stop=True, skip_group_check=True))
                    return nb

                for c in range(nch):
                    for u in range(2):
                        scores(u, c)
                        if c > 0:
                            lpv(u, c - 1)
                    if c == 0:
                        pb.v(lambda e: e.memset(lacc[:1, :], 0.0))
                        pb.v(lambda e: e.memset(zacc[:, :], 0.0))
                    pb.bar()
                    for u in range(2):
                        nb = min(4 * c + 3, qb) - 4 * c + 1
                        pb.s(lambda e, u=u, par=c % 2, nb=nb: e.activation(
                            out=PT[u][par][:, :nb * 128], in_=pS[u][par][:, :nb * 128],
                            func=AF.Exp, bias=cb[:, 3:4]))
                    if c > 0:
                        pb.v(lambda e: e.tensor_add(lacc[:1, :], lacc[:1, :], pL[:1, :]))
                        pb.v(lambda e: e.tensor_add(zacc[:, :], zacc[:, :], pZ[:, :]))
                    pb.bar()
                for u in range(2):
                    lpv(u, nch - 1)
                pb.bar()
                pb.v(lambda e: e.tensor_add(lacc[:1, :], lacc[:1, :], pL[:1, :]))
                pb.v(lambda e: e.tensor_add(zacc[:, :], zacc[:, :], pZ[:, :]))
                pb.bar()
                for u in range(2):
                    pb.t(lambda e, u=u: e.transpose(pT[:, u:u + 1],
                                                    lacc[:1, u * 128:(u + 1) * 128], IDN[:1, :1]))
                pb.bar()
                pb.v(lambda e: e.tensor_copy(out=lcol[:, 0:2], in_=pT[:, 0:2]))
                pb.bar()
                pb.v(lambda e: e.reciprocal(out=lcol[:, 0:2], in_=lcol[:, 0:2]))
                pb.bar()
                for u in range(2):
                    pb.v(lambda e, u=u: e.tensor_scalar(
                        out=zsc[:, u * 128:(u + 1) * 128], in0=zacc[:, u * 128:(u + 1) * 128],
                        scalar1=lcol[:, u:u + 1], scalar2=None, op0=ALU.mult))
                pb.bar()
                for u in range(2):
                    pb.t(lambda e, u=u: e.transpose(pT[:, u * 128:(u + 1) * 128],
                                                    zsc[:, u * 128:(u + 1) * 128], IDN[:, :]))
                pb.bar()
                for u in range(2):
                    pb.v(lambda e, u=u, qb=qb: e.tensor_copy(
                        out=zfm[u][:, qb * 128:(qb + 1) * 128],
                        in_=pT[:, u * 128:(u + 1) * 128]))
                pb.bar()
            for u in range(2):
                gi = 2 * pair + u
                pb.dg(lambda e, gi=gi, u=u: e.dma_start(out=zf[gi, :, :], in_=zfm[u][:, :]))
            pb.bar()
        pb.emit(nc)
    return nc


def build_F():
    nc = bass.Bass("TRN2", target_bir_lowering=False)
    x2 = nc.dram_tensor("x2", [D, TPC], F32, kind="ExternalInput").ap()
    zsh = nc.dram_tensor("zsh", [NH, 128, TPC], BF16, kind="ExternalInput").ap()
    wot = nc.dram_tensor("wot", [NH, 128, D], F32, kind="ExternalInput").ap()
    x3 = nc.dram_tensor("x3", [D, TPC], F32, kind="ExternalOutput").ap()

    pb = PB()
    with ExitStack() as st:
        e_ = st.enter_context
        zs = [e_(nc.sbuf_tensor("z%d" % h, [128, TPC], BF16)) for h in range(NH)]
        ws = e_(nc.sbuf_tensor("ws", [128, NH * 128], BF16))
        xtile = e_(nc.sbuf_tensor("xtile", [128, 512], F32))
        psA = e_(nc.psum_tensor("psA", [128, 512], F32))

        for h in range(NH):
            pb.dl(lambda e, h=h: e.dma_start(out=zs[h][:, :], in_=zsh[h, :, :]))
        pb.bar()
        for dm in range(16):
            for h in range(NH):
                pb.dg(lambda e, h=h, dm=dm: e.dma_start(
                    out=ws[:, h * 128:(h + 1) * 128],
                    in_=wot[h, :, dm * 128:(dm + 1) * 128]))
            pb.bar()
            for gi in range(4):
                g0 = gi * 512
                pb.dl(lambda e, dm=dm, g0=g0: e.dma_start(
                    out=xtile[:, :], in_=x2[dm * 128:(dm + 1) * 128, g0:g0 + 512]))
                for h in range(NH):
                    pb.t(lambda e, h=h, g0=g0: e.matmul(
                        psA[:, :], ws[:, h * 128:(h + 1) * 128], zs[h][:, g0:g0 + 512],
                        start=(h == 0), stop=(h == NH - 1)))
                pb.bar()
                pb.v(lambda e: e.tensor_add(xtile[:, :], xtile[:, :], psA[:, :]))
                pb.bar()
                pb.dg(lambda e, dm=dm, g0=g0: e.dma_start(
                    out=x3[dm * 128:(dm + 1) * 128, g0:g0 + 512], in_=xtile[:, :]))
                pb.bar()
        pb.emit(nc)
    return nc


def build_G():
    nc = bass.Bass("TRN2", target_bir_lowering=False)
    x3a = nc.dram_tensor("x3a", [D, 2048], F32, kind="ExternalInput").ap()
    tidx = nc.dram_tensor("tidx", [4, 2048], I32, kind="ExternalInput").ap()
    tw = nc.dram_tensor("tw", [4, 2048], F32, kind="ExternalInput").ap()
    eids = nc.dram_tensor("eids", [4, 2], F32, kind="ExternalInput").ap()
    gwt = nc.dram_tensor("gwt", [2, D, 1408], F32, kind="ExternalInput").ap()
    uwt = nc.dram_tensor("uwt", [2, D, 1408], F32, kind="ExternalInput").ap()
    dwt = nc.dram_tensor("dwt", [2, 1408, D], F32, kind="ExternalInput").ap()
    part = nc.dram_tensor("part", [D, 2048], F32, kind="ExternalOutput").ap()

    NI = 11
    pb = PB()
    with ExitStack() as st:
        e_ = st.enter_context
        xg = [e_(nc.sbuf_tensor("xsb%d" % d, [128, 512], F32)) for d in range(16)]
        xlb = [e_(nc.sbuf_tensor("xl%d" % d, [128, 1024], BF16)) for d in range(16)]
        hp = [[e_(nc.sbuf_tensor("hp%d_%d" % (ee, o), [128, 1024], BF16))
               for o in range(NI)] for ee in range(2)]
        sq = e_(nc.sbuf_tensor("sq", [128, 4 * 512], F32))
        scr = e_(nc.sbuf_tensor("scr", [1, 512], F32))
        mrow = e_(nc.sbuf_tensor("mrow", [1, 512], F32))
        rrow = e_(nc.sbuf_tensor("rrow", [1, 512], F32))
        gws = e_(nc.sbuf_tensor("gws", [128, 2048], BF16))
        uws = e_(nc.sbuf_tensor("uws", [128, 2048], BF16))
        dws = e_(nc.sbuf_tensor("dws", [128, 2 * NI * 128], BF16))
        idxf = e_(nc.sbuf_tensor("idxf", [4, 2048], F32))
        idxi = e_(nc.sbuf_tensor("idxi", [4, 2048], I32))
        tws = e_(nc.sbuf_tensor("tws", [4, 2048], F32))
        eid = e_(nc.sbuf_tensor("eid", [4, 2], F32))
        wmk = e_(nc.sbuf_tensor("wmk", [4, 2048], F32))
        mrowe = [e_(nc.sbuf_tensor("mr%d" % ee, [1, 2048], F32)) for ee in range(2)]
        gact = e_(nc.sbuf_tensor("gact", [128, 512], F32))
        uact = e_(nc.sbuf_tensor("uact", [128, 512], F32))
        bcm = e_(nc.sbuf_tensor("bcm", [128, 512], F32))
        ones_col = e_(nc.sbuf_tensor("onesc", [128, 1], F32))
        ones_row = e_(nc.sbuf_tensor("onesr", [1, 128], F32))
        cb = e_(nc.sbuf_tensor("cb", [128, 4], F32))
        psA = e_(nc.psum_tensor("psA", [128, 512], F32))
        psB = e_(nc.psum_tensor("psB", [128, 512], F32))
        psC = e_(nc.psum_tensor("psC", [128, 512], F32))
        pr1 = e_(nc.psum_tensor("pr1", [1, 512], F32))
        pr2 = e_(nc.psum_tensor("pr2", [1, 512], F32))

        pb.dl(lambda e: e.dma_start(out=idxi[:, :], in_=tidx[:, :]))
        pb.dl(lambda e: e.dma_start(out=tws[:, :], in_=tw[:, :]))
        pb.dl(lambda e: e.dma_start(out=eid[:, :], in_=eids[:, :]))
        pb.v(lambda e: e.memset(ones_col[:, :], 1.0))
        pb.v(lambda e: e.memset(ones_row[:1, :], 1.0))
        pb.v(lambda e: e.memset(cb[:, 0:1], 1e-5))
        pb.v(lambda e: e.memset(cb[:, 1:2], 1e-6))
        pb.v(lambda e: e.memset(cb[:, 2:3], 1.0))
        pb.v(lambda e: e.memset(cb[:, 3:4], -16.0))
        pb.bar()
        pb.v(lambda e: e.tensor_copy(out=idxf[:, :], in_=idxi[:, :]))
        pb.bar()
        for ee in range(2):
            pb.v(lambda e, ee=ee: e.tensor_scalar(out=wmk[:, :], in0=idxf[:, :],
                                                  scalar1=eid[:4, ee:ee + 1], scalar2=None,
                                                  op0=ALU.is_equal))
            pb.bar()
            pb.v(lambda e: e.tensor_mul(wmk[:, :], wmk[:, :], tws[:, :]))
            pb.bar()
            for gi in range(4):
                pb.t(lambda e, gi=gi: e.matmul(pr1[:1, :], ones_col[:4, 0:1],
                                               wmk[:4, gi * 512:(gi + 1) * 512],
                                               start=True, stop=True))
                pb.bar()
                pb.v(lambda e, ee=ee, gi=gi: e.tensor_copy(
                    out=mrowe[ee][:1, gi * 512:(gi + 1) * 512], in_=pr1[:1, :]))
                pb.bar()
        for hf in range(2):
            h0 = hf * 1024
            for gi2 in range(2):
                g0 = h0 + gi2 * 512
                for d in range(16):
                    pb.dl(lambda e, d=d, g0=g0: e.dma_start(
                        out=xg[d][:, :], in_=x3a[d * 128:(d + 1) * 128, g0:g0 + 512]))
                pb.bar()
                emit_ln(pb, xg, [(0, 512)], psA, psB, pr1, pr2, sq, mrow, rrow,
                        scr, [xx[:, gi2 * 512:gi2 * 512 + 512] for xx in xlb],
                        ones_col, ones_row, cb[:1, 0:1])
            for ee in range(2):
                for o in range(NI):
                    for dt in range(16):
                        pb.dg(lambda e, ee=ee, o=o, dt=dt: e.dma_start(
                            out=gws[:, dt * 128:(dt + 1) * 128],
                            in_=gwt[ee, dt * 128:(dt + 1) * 128, o * 128:(o + 1) * 128]))
                        pb.dg(lambda e, ee=ee, o=o, dt=dt: e.dma_start(
                            out=uws[:, dt * 128:(dt + 1) * 128],
                            in_=uwt[ee, dt * 128:(dt + 1) * 128, o * 128:(o + 1) * 128]))
                    pb.bar()
                    for gi2 in range(2):
                        gl = gi2 * 512
                        for dt in range(16):
                            pb.t(lambda e, dt=dt, gl=gl: e.matmul(
                                psA[:, :], gws[:, dt * 128:(dt + 1) * 128],
                                xlb[dt][:, gl:gl + 512], start=(dt == 0), stop=(dt == 15)))
                        for dt in range(16):
                            pb.t(lambda e, dt=dt, gl=gl: e.matmul(
                                psB[:, :], uws[:, dt * 128:(dt + 1) * 128],
                                xlb[dt][:, gl:gl + 512], start=(dt == 0), stop=(dt == 15)))
                        pb.t(lambda e, ee=ee, h0=h0, gl=gl: e.matmul(
                            psC[:, :], ones_row[:1, :], mrowe[ee][:1, h0 + gl:h0 + gl + 512],
                            start=True, stop=True))
                        pb.bar()
                        pb.s(lambda e: e.activation(out=gact[:, :], in_=psA[:, :],
                                                    func=AF.Exp, scale=-1.0))
                        pb.bar()
                        pb.v(lambda e: e.tensor_scalar(out=gact[:, :], in0=gact[:, :],
                                                       scalar1=1.0, scalar2=None, op0=ALU.add))
                        pb.bar()
                        pb.v(lambda e: e.reciprocal(out=gact[:, :], in_=gact[:, :]))
                        pb.bar()
                        pb.v(lambda e: e.tensor_mul(gact[:, :], gact[:, :], psA[:, :]))
                        pb.bar()
                        pb.v(lambda e: e.tensor_mul(uact[:, :], gact[:, :], psB[:, :]))
                        pb.bar()
                        pb.v(lambda e: e.tensor_copy(out=bcm[:, :], in_=psC[:, :]))
                        pb.bar()
                        pb.v(lambda e, ee=ee, o=o, gl=gl: e.tensor_mul(
                            hp[ee][o][:, gl:gl + 512], uact[:, :], bcm[:, :]))
                        pb.bar()
            for dm in range(16):
                for ee in range(2):
                    for o in range(NI):
                        pb.dg(lambda e, ee=ee, o=o, dm=dm: e.dma_start(
                            out=dws[:, (ee * NI + o) * 128:(ee * NI + o + 1) * 128],
                            in_=dwt[ee, o * 128:(o + 1) * 128, dm * 128:(dm + 1) * 128]))
                pb.bar()
                for gi2 in range(2):
                    gl = gi2 * 512
                    for i2 in range(2 * NI):
                        ee, o = i2 // NI, i2 % NI
                        pb.t(lambda e, ee=ee, o=o, gl=gl, i2=i2: e.matmul(
                            psA[:, :], dws[:, (ee * NI + o) * 128:(ee * NI + o + 1) * 128],
                            hp[ee][o][:, gl:gl + 512], start=(i2 == 0), stop=(i2 == 2 * NI - 1)))
                    pb.bar()
                    pb.v(lambda e: e.tensor_copy(out=gact[:, :], in_=psA[:, :]))
                    pb.bar()
                    pb.dg(lambda e, dm=dm, h0=h0, gl=gl: e.dma_start(
                        out=part[dm * 128:(dm + 1) * 128, h0 + gl:h0 + gl + 512],
                        in_=gact[:, :]))
                pb.bar()
        pb.emit(nc)
    return nc


def build_H():
    nc = bass.Bass("TRN2", target_bir_lowering=False)
    parts = nc.dram_tensor("parts", [8, D, 256], F32, kind="ExternalInput").ap()
    x3s = nc.dram_tensor("x3s", [D, 256], F32, kind="ExternalInput").ap()
    outh = nc.dram_tensor("outh", [D, 256], F32, kind="ExternalOutput").ap()
    pb = PB()
    with ExitStack() as st:
        e_ = st.enter_context
        acc = e_(nc.sbuf_tensor("acc", [128, 256], F32))
        pt = e_(nc.sbuf_tensor("pt", [128, 256], F32))
        for d in range(16):
            pb.dl(lambda e, d=d: e.dma_start(out=acc[:, :], in_=x3s[d * 128:(d + 1) * 128, :]))
            pb.bar()
            for i in range(8):
                pb.dl(lambda e, i=i, d=d: e.dma_start(out=pt[:, :],
                                                      in_=parts[i, d * 128:(d + 1) * 128, :]))
                pb.bar()
                pb.v(lambda e: e.tensor_add(acc[:, :], acc[:, :], pt[:, :]))
                pb.bar()
            pb.dg(lambda e, d=d: e.dma_start(out=outh[d * 128:(d + 1) * 128, :], in_=acc[:, :]))
            pb.bar()
        pb.emit(nc)
    return nc


def kernel(L0_attn_hidden_states, L0_attn_W_qkv, L0_attn_W_ab, L0_attn_conv_w,
           L0_attn_A_log, L0_attn_dt_bias, L0_attn_o_norm_w, L0_attn_W_o,
           L0_ffn_A, L0_ffn_B, L0_ffn_Wr, L0_ffn_br,
           L1_attn_W_Q, L1_attn_W_K, L1_attn_W_V, L1_attn_W_O,
           L1_ffn_topk_idx, L1_ffn_topk_weight, L1_ffn_gate_weight,
           L1_ffn_up_weight, L1_ffn_down_weight):
    cores = list(range(NC))
    f32 = np.float32
    asc = np.ascontiguousarray
    mUi, mLs, mNeg = _np_masks()
    idn = np.eye(128, dtype=f32)

    x = np.asarray(L0_attn_hidden_states, f32)
    x_fm = asc(x.reshape(NTOK, D).T)

    ncA = build_A()
    imA = []
    wab = np.asarray(L0_attn_W_ab, f32)
    for c in cores:
        xh = np.zeros((D, TPC + 3), f32)
        xh[:, 3:] = x_fm[:, c * TPC:(c + 1) * TPC]
        if c % 4 != 0:
            xh[:, :3] = x_fm[:, c * TPC - 3:c * TPC]
        imA.append({"xh": asc(xh), "wqkv": np.asarray(L0_attn_W_qkv, f32),
                    "waa": asc(wab[:, 0:16]), "wbb": asc(wab[:, 16:32]),
                    "wgate": asc(wab[:, 32:]),
                    "convw": np.asarray(L0_attn_conv_w, f32),
                    "alog": np.asarray(L0_attn_A_log, f32),
                    "dtb": np.asarray(L0_attn_dt_bias, f32)})
    rA = run_bass_kernel_spmd(ncA, imA, core_ids=cores).results
    qf_g = np.concatenate([r["qf"] for r in rA], axis=2)
    kf_g = np.concatenate([r["kf"] for r in rA], axis=2)
    vf_g = np.concatenate([r["vf"] for r in rA], axis=2)
    gg_g = np.concatenate([r["gg"] for r in rA], axis=1)
    bb_g = np.concatenate([r["bb"] for r in rA], axis=1)
    gate_g = np.concatenate([r["gatef"] for r in rA], axis=1)

    insts = [(i // NH, i % NH) for i in range(32)]

    ncB = build_B()
    imB = []
    for c in cores:
        my = insts[4 * c:4 * c + 4]
        imB.append({
            "qb": asc(np.stack([qf_g[h, :, b * TB:(b + 1) * TB] for (b, h) in my])),
            "kb": asc(np.stack([kf_g[h, :, b * TB:(b + 1) * TB] for (b, h) in my])),
            "vb": asc(np.stack([vf_g[h, :, b * TB:(b + 1) * TB] for (b, h) in my])),
            "gb": asc(np.stack([gg_g[h, b * TB:(b + 1) * TB].reshape(64, 128).T for (b, h) in my])),
            "bbeta": asc(np.stack([bb_g[h, b * TB:(b + 1) * TB].reshape(64, 128).T for (b, h) in my])),
            "mui": mUi, "mls": mLs, "idn": idn})
    rB = run_bass_kernel_spmd(ncB, imB, core_ids=cores).results
    o_g = np.zeros((2, NH, 128, TB), f32)
    for c in cores:
        for k2, (b, h) in enumerate(insts[4 * c:4 * c + 4]):
            o_g[b, h] = rB[c]["of"][k2]

    ncC1 = build_C1()
    imC1 = []
    for c in cores:
        b = c // 4
        tl = (c * TPC) % TB
        o_b = o_g[b].reshape(D, TB)
        imC1.append({"xsh": asc(x_fm[:, c * TPC:(c + 1) * TPC]),
                     "osh": asc(o_b[:, tl:tl + TPC]),
                     "gsh": asc(gate_g[:, c * TPC:(c + 1) * TPC]),
                     "onw": np.asarray(L0_attn_o_norm_w, f32),
                     "wo": np.asarray(L0_attn_W_o, f32),
                     "wra": asc(np.asarray(L0_ffn_Wr, f32).T),
                     "bra": np.asarray(L0_ffn_br, f32),
                     "ffa": asc(np.asarray(L0_ffn_A, f32).transpose(0, 2, 1)),
                     "ffb": asc(np.asarray(L0_ffn_B, f32).transpose(0, 2, 1)),
                     "idn": idn})
    rC1 = run_bass_kernel_spmd(ncC1, imC1, core_ids=cores).results
    x2_sh = [r["x2"] for r in rC1]

    ncC2 = build_C2()
    wqh = asc(np.asarray(L1_attn_W_Q, f32).transpose(0, 2, 1))
    wkh = asc(np.asarray(L1_attn_W_K, f32).transpose(0, 2, 1))
    wvh = asc(np.asarray(L1_attn_W_V, f32).transpose(0, 2, 1))
    imC2 = [{"x2": asc(x2_sh[c]), "wq": wqh, "wk": wkh, "wv": wvh} for c in cores]
    rC2 = run_bass_kernel_spmd(ncC2, imC2, core_ids=cores).results
    q1_g = np.concatenate([r["q1"] for r in rC2], axis=2)
    k1_g = np.concatenate([r["k1"] for r in rC2], axis=2)
    v1_g = np.concatenate([r["v1"] for r in rC2], axis=2)

    ncE = build_E()
    imE = []
    for c in cores:
        my = insts[4 * c:4 * c + 4]
        imE.append({"q1": asc(np.stack([q1_g[h, :, b * TB:(b + 1) * TB] for (b, h) in my])),
                    "k1": asc(np.stack([k1_g[h, :, b * TB:(b + 1) * TB] for (b, h) in my])),
                    "v1": asc(np.stack([v1_g[h, :, b * TB:(b + 1) * TB] for (b, h) in my])),
                    "mneg": mNeg, "idn": idn})
    rE = run_bass_kernel_spmd(ncE, imE, core_ids=cores).results
    z_g = np.zeros((2, NH, 128, TB), rE[0]["zf"].dtype)
    for c in cores:
        for k2, (b, h) in enumerate(insts[4 * c:4 * c + 4]):
            z_g[b, h] = rE[c]["zf"][k2]

    ncF = build_F()
    woth = asc(np.asarray(L1_attn_W_O, f32).transpose(0, 2, 1))
    imF = []
    for c in cores:
        b = c // 4
        tl = (c * TPC) % TB
        imF.append({"x2": asc(x2_sh[c]), "zsh": asc(z_g[b][:, :, tl:tl + TPC]),
                    "wot": woth})
    rF = run_bass_kernel_spmd(ncF, imF, core_ids=cores).results
    x3_g = np.concatenate([r["x3"] for r in rF], axis=1)

    ncG = build_G()
    x3a = asc(x3_g[:, :2048])
    tidx = asc(np.asarray(L1_ffn_topk_idx).reshape(2048, 4).T.astype(np.int32))
    twr = asc(np.asarray(L1_ffn_topk_weight, f32).reshape(2048, 4).T)
    gwt = np.asarray(L1_ffn_gate_weight, f32).transpose(0, 2, 1)
    uwt = np.asarray(L1_ffn_up_weight, f32).transpose(0, 2, 1)
    dwt = np.asarray(L1_ffn_down_weight, f32).transpose(0, 2, 1)
    imG = []
    for c in cores:
        eids = np.zeros((4, 2), f32)
        eids[:, 0] = 2 * c
        eids[:, 1] = 2 * c + 1
        imG.append({"x3a": x3a, "tidx": tidx, "tw": twr, "eids": eids,
                    "gwt": asc(gwt[2 * c:2 * c + 2]),
                    "uwt": asc(uwt[2 * c:2 * c + 2]),
                    "dwt": asc(dwt[2 * c:2 * c + 2])})
    rG = run_bass_kernel_spmd(ncG, imG, core_ids=cores).results
    parts_g = np.stack([r["part"] for r in rG])

    ncH = build_H()
    imH = []
    for c in cores:
        imH.append({"parts": asc(parts_g[:, :, c * 256:(c + 1) * 256]),
                    "x3s": asc(x3a[:, c * 256:(c + 1) * 256])})
    rH = run_bass_kernel_spmd(ncH, imH, core_ids=cores).results
    outh = np.concatenate([r["outh"] for r in rH], axis=1)

    final = np.array(x3_g)
    final[:, :2048] = outh
    return asc(final.T).reshape(2, TB, D).astype(np.float32)
